# version 11
# baseline (speedup 1.0000x reference)
import contextlib
import os
import numpy as np
import ml_dtypes
import concourse.bass as bass
import concourse.mybir as mybir
from concourse.bass_utils import run_bass_kernel_spmd

F32 = mybir.dt.float32
BF16 = mybir.dt.bfloat16
I32 = mybir.dt.int32
AF = mybir.ActivationFunctionType
ALU = mybir.AluOpType
AXX = mybir.AxisListType.X

B, S, H = 8, 4096, 2048
NH, HD = 16, 128
MLP = 16384
NORM = 1.0 / np.sqrt(128.0)
EXP_SHIFT = -12.0
LN_EPS = 1e-5


ENGINES = ("pe", "act", "dve", "pool", "sp")
_BASS_ENG = {"pe": "tensor", "act": "scalar", "dve": "vector", "pool": "gpsimd", "sp": "sync"}


class Prog:

    NDMA = 8

    def __init__(self, nc):
        self.nc = nc
        self.items = {e: [] for e in ENGINES}
        self.ccount = {e: 0 for e in ENGINES}
        self.dcount = {e: 0 for e in ENGINES}
        self.dslot = {e: [0] * self.NDMA for e in ENGINES}
        self.regions = {}
        self.sem_names = ([f"cs_{e}" for e in ENGINES]
                          + [f"dq_{e}{j}" for e in ENGINES for j in range(self.NDMA)])

    def _deps(self, eng, reads, writes):
        waits = {}
        def add(tok):
            if tok is None:
                return
            s, v = tok
            if s == f"cs_{eng}" and eng == "pe":
                return
            if waits.get(s, 0) < v:
                waits[s] = v
        for k in reads:
            r = self.regions.get(k)
            if r:
                add(r["w"])
        for k in writes:
            r = self.regions.get(k)
            if r:
                add(r["w"])
                for s, v in r["r"].items():
                    add((s, v))
        return waits

    def _finish(self, eng, reads, writes, tok):
        s, v = tok
        for k in reads:
            r = self.regions.setdefault(k, {"w": None, "r": {}})["r"]
            if r.get(s, 0) < v:
                r[s] = v
        for k in writes:
            self.regions[k] = {"w": tok, "r": {}}

    def emit(self, eng, fn, reads=(), writes=(), dma=False):
        waits = self._deps(eng, reads, writes)
        if dma:
            slot = self.dcount[eng] % self.NDMA
            sem = f"dq_{eng}{slot}"
            prev = 16 * self.dslot[eng][slot]
            if prev and waits.get(sem, 0) < prev:
                waits[sem] = prev
            self.dcount[eng] += 1
            self.dslot[eng][slot] += 1
            inc = 16
            tok = (sem, 16 * self.dslot[eng][slot])
        else:
            self.ccount[eng] += 1
            sem, inc = f"cs_{eng}", 1
            tok = (sem, self.ccount[eng])
        self.items[eng].append((waits, fn, sem, inc))
        self._finish(eng, reads, writes, tok)
        return tok

    def barrier(self):
        arrive = {}
        for e in ENGINES:
            waits = {}
            for j in range(self.NDMA):
                if self.dslot[e][j]:
                    waits[f"dq_{e}{j}"] = 16 * self.dslot[e][j]
            self.ccount[e] += 1
            self.items[e].append((waits, lambda eng: eng.nop(nofuse=True), f"cs_{e}", 1))
            arrive[e] = self.ccount[e]
        for e in ENGINES:
            waits = {f"cs_{f}": arrive[f] for f in ENGINES if f != e}
            self.ccount[e] += 1
            self.items[e].append((waits, lambda eng: eng.nop(nofuse=True), f"cs_{e}", 1))
        bar = {e: (f"cs_{e}", self.ccount[e]) for e in ENGINES}
        for k in self.regions:
            w = self.regions[k]["w"]
            e = None
            if w is not None:
                s = w[0]
                e = s.split("_", 1)[1]
            self.regions[k] = {"w": bar.get(e) if e else None, "r": {}}

    def check(self):
        ptr = {e: 0 for e in ENGINES}
        sem = {n: 0 for n in self.sem_names}
        progress = True
        while progress:
            progress = False
            for e in ENGINES:
                while ptr[e] < len(self.items[e]):
                    waits, fn, s, inc = self.items[e][ptr[e]]
                    if any(sem[k] < v for k, v in waits.items()):
                        break
                    sem[s] += inc
                    ptr[e] += 1
                    progress = True
        stuck = {e: ptr[e] for e in ENGINES if ptr[e] < len(self.items[e])}
        if stuck:
            msgs = []
            for e, i in stuck.items():
                waits = self.items[e][i][0]
                bad = {k: (v, sem[k]) for k, v in waits.items() if sem[k] < v}
                msgs.append(f"{e}@{i}/{len(self.items[e])} waits {bad}")
            raise RuntimeError("DEADLOCK: " + "; ".join(msgs))

    def realize(self, st):
        nc = self.nc
        sems = {n: st.enter_context(nc.semaphore(n)) for n in self.sem_names}
        block = st.enter_context(nc.Block())
        clocks = {e: {} for e in ENGINES}
        waited = {e: set() for e in ENGINES}
        for f in ENGINES:
            for waits, _, _, _ in self.items[f]:
                for s, v in waits.items():
                    if s.startswith("cs_"):
                        waited[s[3:]].add(v)

        def run_engine(e):
            def body(engobj):
                clock = clocks[e]
                cnt = 0
                for waits, fn, sem, inc in self.items[e]:
                    n_emitted = 0
                    for s in sorted(waits):
                        v = waits[s]
                        if clock.get(s, 0) < v:
                            engobj.wait_ge(sems[s], v)
                            clock[s] = v
                            n_emitted += 1
                    if sem == f"cs_{e}":
                        cnt += 1
                        if (n_emitted == 0 and (cnt - 1) in waited[e]
                                and cnt - 1 > clock.get(sem, 0)):
                            engobj.wait_ge(sems[sem], cnt - 1)
                            clock[sem] = cnt - 1
                    inst = fn(engobj)
                    inst.then_inc(sems[sem], inc)
                for j in range(self.NDMA):
                    if self.dslot[e][j]:
                        engobj.wait_ge(sems[f"dq_{e}{j}"], 16 * self.dslot[e][j])
            return body

        for e in ENGINES:
            getattr(block, _BASS_ENG[e])(run_engine(e))



class Ctx:
    def __init__(self, nc, prog, st):
        self.nc, self.p, self.st = nc, prog, st
        ident_np = np.eye(128, dtype=ml_dtypes.bfloat16)
        self.ident = nc.inline_tensor(ident_np, "ident128")
        self.ident_sb = st.enter_context(nc.sbuf_tensor("ident_sb", [128, 128], BF16))
        self.consts = st.enter_context(nc.sbuf_tensor("consts", [128, 8], F32))
        p = prog
        p.emit("sp", lambda e: e.dma_start(out=self.ident_sb[:], in_=self.ident.ap()),
               writes=["ident_sb"], dma=True)
        p.emit("dve", lambda e: e.memset(self.consts[:, 0:1], LN_EPS), writes=["consts"])
        p.emit("dve", lambda e: e.memset(self.consts[:, 1:2], EXP_SHIFT), writes=["consts"])
        p.emit("dve", lambda e: e.memset(self.consts[:, 2:3], 1.0), writes=["consts"])
        p.emit("dve", lambda e: e.memset(self.consts[:, 3:4], -1e30), writes=["consts"])
        p.emit("dve", lambda e: e.memset(self.consts[:, 4:5], 0.5), writes=["consts"])
        p.emit("dve", lambda e: e.memset(self.consts[:, 5:6], 0.7978845608028654), writes=["consts"])
        nc.const_aps.register(F32, 0.5, self.consts[:, 4:5])
        nc.const_aps.register(F32, 0.7978845608028654, self.consts[:, 5:6])
        p.emit("dve", lambda e: e.memset(self.consts[:, 6:7], 0.7978845608028654 * 0.044715), writes=["consts"])
        nc.const_aps.register(F32, 0.7978845608028654 * 0.044715, self.consts[:, 6:7])

    @property
    def eps(self):
        return self.consts[:, 0:1]

    @property
    def eshift(self):
        return self.consts[:, 1:2]

    @property
    def half(self):
        return self.consts[:, 4:5]


def emit_ln_chunk(cx, key_in, xin_ap, xbf, xbf_key, stats, stats_key, trash, ntile=4, hdim=H):
    p = cx.p
    n = ntile
    c = 1.0 / hdim
    for j in range(n):
        tile = xin_ap[:, j * hdim:(j + 1) * hdim]
        p.emit("dve", lambda e, t=tile, j=j: e.reduce_sum(stats[:, j:j + 1], t, axis=AXX),
               reads=[key_in], writes=[stats_key])
        p.emit("act", lambda e, t=tile, j=j: e.activation(
            trash[:, 0:hdim], t, AF.Square, accum_out=stats[:, n + j:n + j + 1]),
            reads=[key_in], writes=[stats_key, "trash"])
    p.emit("dve", lambda e: e.tensor_scalar(stats[:, 2 * n:3 * n], stats[:, 0:n], c, None, ALU.mult),
           reads=[stats_key], writes=[stats_key])
    p.emit("dve", lambda e: e.tensor_scalar(stats[:, 3 * n:4 * n], stats[:, n:2 * n], c, None, ALU.mult),
           reads=[stats_key], writes=[stats_key])
    p.emit("dve", lambda e: e.tensor_mul(stats[:, 0:n], stats[:, 2 * n:3 * n], stats[:, 2 * n:3 * n]),
           reads=[stats_key], writes=[stats_key])
    p.emit("dve", lambda e: e.tensor_sub(stats[:, 3 * n:4 * n], stats[:, 3 * n:4 * n], stats[:, 0:n]),
           reads=[stats_key], writes=[stats_key])
    p.emit("act", lambda e: e.activation(stats[:, 3 * n:4 * n], stats[:, 3 * n:4 * n], AF.Ln, bias=cx.eps),
           reads=[stats_key, "consts"], writes=[stats_key])
    p.emit("act", lambda e: e.activation(stats[:, 3 * n:4 * n], stats[:, 3 * n:4 * n], AF.Exp, scale=-0.5),
           reads=[stats_key], writes=[stats_key])
    for j in range(n):
        tile = xin_ap[:, j * hdim:(j + 1) * hdim]
        p.emit("dve", lambda e, t=tile, j=j: e.tensor_scalar(
            xbf[:, j * hdim:(j + 1) * hdim], t, stats[:, 2 * n + j:2 * n + j + 1],
            stats[:, 3 * n + j:3 * n + j + 1], ALU.subtract, ALU.mult),
            reads=[key_in, stats_key], writes=[xbf_key])


def emit_transpose_chunk(cx, xbf, xbf_key, xlnT, xlnT_key, psbf, psbf_key, ntile=4, nht=16):
    p = cx.p
    tw = ntile * 128
    for ht in range(nht):
        for j in range(ntile):
            src = xbf[:, j * (nht * 128) + ht * 128: j * (nht * 128) + ht * 128 + 128]
            dst = psbf[:, j * 128:(j + 1) * 128]
            p.emit("pe", lambda e, s=src, d=dst: e.transpose(d, s, cx.ident_sb[:]),
                   reads=[xbf_key, "ident_sb"], writes=[psbf_key])
        p.emit("dve", lambda e, ht=ht: e.tensor_copy(xlnT[:, ht * tw:(ht + 1) * tw], psbf[:, 0:tw]),
               reads=[psbf_key], writes=[xlnT_key])



def phase1_qkv_attn(cx, d):
    nc, p, st = cx.nc, cx.p, cx.st
    with contextlib.ExitStack() as sst:
        xin = [sst.enter_context(nc.sbuf_tensor(f"p1_xin{i}", [128, 4 * H], F32)) for i in range(2)]
        xbf = sst.enter_context(nc.sbuf_tensor("p1_xbf", [128, 4 * H], BF16))
        xlnT = [sst.enter_context(nc.sbuf_tensor("p1_xlnT0", [128, 16 * 512], BF16))] * 2
        stats = sst.enter_context(nc.sbuf_tensor("p1_stats", [128, 16], F32))
        trash = sst.enter_context(nc.sbuf_tensor("p1_trash", [128, H], BF16))
        wsb = sst.enter_context(nc.sbuf_tensor("p1_wsb", [128, 16 * 768], BF16))
        qT = [sst.enter_context(nc.sbuf_tensor(f"p1_qT{i}", [128, 4096], BF16)) for i in range(2)]
        kT = [sst.enter_context(nc.sbuf_tensor(f"p1_kT{i}", [128, 4096], BF16)) for i in range(2)]
        vE = [sst.enter_context(nc.sbuf_tensor(f"p1_vE{i}", [128, 32 * 132], BF16)) for i in range(2)]
        vtmp = sst.enter_context(nc.sbuf_tensor("p1_vtmp", [128, 512], BF16))
        expT = [sst.enter_context(nc.sbuf_tensor(f"p1_expT{i}", [128, 32 * 256], BF16)) for i in range(2)]
        ctmp = sst.enter_context(nc.sbuf_tensor("p1_ctmp", [128, 128], BF16))
        cTsb = sst.enter_context(nc.sbuf_tensor("p1_cTsb", [128, 512], BF16))
        rs1 = sst.enter_context(nc.sbuf_tensor("p1_rs1", [128, 1], F32))
        psf = [sst.enter_context(nc.psum_tensor(f"p1_psf{i}", [128, 512], F32)) for i in range(6)]
        psb = [sst.enter_context(nc.psum_tensor(f"p1_psb{i}", [128, 1024], BF16)) for i in range(2)]

        p.emit("pool", lambda e: e.dma_start(
            out=wsb[:].rearrange("p (ht dd) -> p ht dd", dd=768),
            in_=d["wqkv0T"].rearrange("(ht p) dd -> p ht dd", p=128)),
            writes=["p1_wsb"], dma=True)
        for i in range(2):
            p.emit("dve", lambda e, i=i: e.memset(
                vE[i][:].rearrange("p (kt c) -> p kt c", c=132)[:, :, 128:129], 1.0),
                writes=[f"p1_vE{i}_ones"])

        for m in range(B):
            for tc in range(8):
                bi = tc % 2
                xink, xlnk = f"p1_xin{bi}", "p1_xlnT0"
                src = d["x_full"][(m * S + tc * 512):(m * S + (tc + 1) * 512), :]
                p.emit("sp", lambda e, s=src, bi=bi: e.dma_start(
                    out=xin[bi][:].rearrange("p (j h) -> p j h", h=H),
                    in_=s.rearrange("(j p) h -> p j h", p=128)),
                    writes=[xink], dma=True)
                emit_ln_chunk(cx, xink, xin[bi][:], xbf, "p1_xbf", stats, "p1_stats", trash)
                emit_transpose_chunk(cx, xbf, "p1_xbf", xlnT[bi], xlnk, psb[tc % 2], f"p1_psb{tc % 2}")
                for dt in range(6):
                    bank = psf[dt % 2]
                    bkey = f"p1_psf{dt % 2}"
                    for ht in range(16):
                        p.emit("pe", lambda e, dt=dt, ht=ht, bank=bank, bi=bi: e.matmul(
                            bank[:], wsb[:, ht * 768 + dt * 128: ht * 768 + (dt + 1) * 128],
                            xlnT[bi][:, ht * 512:(ht + 1) * 512],
                            start=(ht == 0), stop=(ht == 15)),
                            reads=["p1_wsb", xlnk], writes=[bkey])
                    hp = dt // 3
                    kind = dt % 3
                    if kind == 0:
                        p.emit("dve", lambda e, hp=hp, bank=bank, tc=tc: e.tensor_scalar(
                            qT[hp][:, tc * 512:(tc + 1) * 512], bank[:], NORM, None, ALU.mult),
                            reads=[bkey], writes=[f"p1_qT{hp}"])
                    elif kind == 1:
                        p.emit("dve", lambda e, hp=hp, bank=bank, tc=tc: e.tensor_copy(
                            kT[hp][:, tc * 512:(tc + 1) * 512], bank[:]),
                            reads=[bkey], writes=[f"p1_kT{hp}"])
                    else:
                        p.emit("dve", lambda e, bank=bank: e.tensor_copy(vtmp[:], bank[:]),
                               reads=[bkey], writes=["p1_vtmp"])
                        for q4 in range(4):
                            p.emit("pe", lambda e, q4=q4: e.transpose(
                                psb[1][:, q4 * 128:(q4 + 1) * 128], vtmp[:, q4 * 128:(q4 + 1) * 128],
                                cx.ident_sb[:]),
                                reads=["p1_vtmp", "ident_sb"], writes=["p1_psb1"])
                        p.emit("dve", lambda e, hp=hp, tc=tc: e.tensor_copy(
                            vE[hp][:].rearrange("p (kt c) -> p kt c", c=132)[:, tc * 4:(tc + 1) * 4, 0:128],
                            psb[1][:].rearrange("p (q4 c) -> p q4 c", c=128)[:, 0:4, :]),
                            reads=["p1_psb1"], writes=[f"p1_vE{hp}"])
            for hp in range(2):
                row0 = 256 * m + 128 * hp
                for qc in range(16):
                    par = qc % 2
                    ek = f"p1_expT{par}"
                    for kt in range(32):
                        bank = psf[2 + kt % 4]
                        bkey = f"p1_psf{2 + kt % 4}"
                        p.emit("pe", lambda e, hp=hp, kt=kt, qc=qc, bank=bank: e.matmul(
                            bank[:, 0:256], kT[hp][:, kt * 128:(kt + 1) * 128],
                            qT[hp][:, qc * 256:(qc + 1) * 256], start=True, stop=True),
                            reads=[f"p1_kT{hp}", f"p1_qT{hp}"], writes=[bkey])
                        p.emit("act", lambda e, kt=kt, bank=bank, par=par: e.activation(
                            expT[par][:, kt * 256:(kt + 1) * 256], bank[:, 0:256], AF.Exp, bias=cx.eshift),
                            reads=[bkey, "consts"], writes=[f"{ek}_{kt}"])
                    for qs in range(2):
                        bank = psf[qs]
                        bkey = f"p1_psf{qs}"
                        for kt in range(32):
                            p.emit("pe", lambda e, hp=hp, kt=kt, qs=qs, bank=bank, par=par: e.matmul(
                                bank[:, 0:129], expT[par][:, kt * 256 + qs * 128: kt * 256 + qs * 128 + 128],
                                vE[hp][:, kt * 132: kt * 132 + 129],
                                start=(kt == 0), stop=(kt == 31)),
                                reads=[f"{ek}_{kt}", f"p1_vE{hp}", f"p1_vE{hp}_ones"], writes=[bkey])
                        p.emit("act", lambda e, bank=bank: e.activation(rs1[:], bank[:, 128:129], AF.Ln),
                               reads=[bkey], writes=["p1_rs1"])
                        p.emit("act", lambda e: e.activation(rs1[:], rs1[:], AF.Exp, scale=-1.0),
                               reads=["p1_rs1"], writes=["p1_rs1"])
                        p.emit("dve", lambda e, bank=bank: e.tensor_scalar(
                            ctmp[:], bank[:, 0:128], rs1[:], None, ALU.mult),
                            reads=[bkey, "p1_rs1"], writes=["p1_ctmp"])
                        p.emit("pe", lambda e, qs=qs: e.transpose(
                            psb[0][:, qs * 128:(qs + 1) * 128], ctmp[:], cx.ident_sb[:]),
                            reads=["p1_ctmp", "ident_sb"], writes=["p1_psb0"])
                        p.emit("dve", lambda e, qs=qs: e.tensor_copy(
                            cTsb[:, qs * 128:(qs + 1) * 128], psb[0][:, qs * 128:(qs + 1) * 128]),
                            reads=["p1_psb0"], writes=["p1_cTsb"])
                    p.emit("act", lambda e, row0=row0, qc=qc: e.dma_start(
                        out=d["ctxT0"][row0:row0 + 128, qc * 256:(qc + 1) * 256], in_=cTsb[:, 0:256]),
                        reads=["p1_cTsb"], writes=[f"ctxT0_{row0}_{qc}"], dma=True)


def phase1_proj(cx, d):
    nc, p = cx.nc, cx.p
    with contextlib.ExitStack() as sst:
        wsb = sst.enter_context(nc.sbuf_tensor("p1b_w", [128, 16 * 2048], BF16))
        bbc = sst.enter_context(nc.sbuf_tensor("p1b_bbc", [128, 2048], F32))
        cT = [sst.enter_context(nc.sbuf_tensor(f"p1b_cT{i}", [128, 16 * 512], BF16)) for i in range(2)]
        xo = [sst.enter_context(nc.sbuf_tensor(f"p1b_xo{i}", [128, 4 * 2048], F32)) for i in range(2)]
        x1 = [sst.enter_context(nc.sbuf_tensor("p1b_x10", [128, 4 * 2048], F32))] * 2
        psf = [sst.enter_context(nc.psum_tensor(f"p1b_ps{i}", [128, 512], F32)) for i in range(4)]

        for i in range(4):
            p.emit("pool", lambda e, i=i: e.dma_start(
                out=wsb[:, i * 4 * 2048:(i + 1) * 4 * 2048].rearrange("p (ht o) -> p ht o", o=2048),
                in_=d["wproj0T"][i * 512:(i + 1) * 512, :].rearrange("(ht p) o -> p ht o", p=128)),
                writes=["p1b_w"], dma=True)
        p.emit("sp", lambda e: e.dma_start(
            out=bbc[:], in_=d["bproj0"].partition_broadcast(128)),
            writes=["p1b_bbc"], dma=True)

        for tc in range(8):
            bi = tc % 2
            ck, xok, x1k = f"p1b_cT{bi}", f"p1b_xo{bi}", "p1b_x10"
            p.emit("sp", lambda e, tc=tc, bi=bi: e.dma_start(
                out=cT[bi][:].rearrange("p (ht t) -> p ht t", t=512),
                in_=d["ctxT0"][:, tc * 512:(tc + 1) * 512].rearrange("(ht p) t -> p ht t", p=128)),
                reads=[f"ctxT0_all"], writes=[ck], dma=True)
            p.emit("sp", lambda e, tc=tc, bi=bi: e.dma_start(
                out=xo[bi][:].rearrange("p (j h) -> p j h", h=2048),
                in_=d["x_own"][tc * 512:(tc + 1) * 512, :].rearrange("(j p) h -> p j h", p=128)),
                writes=[xok], dma=True)
            for j in range(4):
                p.emit("dve", lambda e, j=j, bi=bi: e.tensor_add(
                    xo[bi][:, j * 2048:(j + 1) * 2048], xo[bi][:, j * 2048:(j + 1) * 2048], bbc[:]),
                    reads=[xok, "p1b_bbc"], writes=[xok])
            for ts in range(4):
                for oc in range(4):
                    bank = psf[oc]
                    bkey = f"p1b_ps{oc}"
                    for kt in range(16):
                        p.emit("pe", lambda e, ts=ts, oc=oc, kt=kt, bank=bank, bi=bi: e.matmul(
                            bank[:], cT[bi][:, kt * 512 + ts * 128: kt * 512 + ts * 128 + 128],
                            wsb[:, kt * 2048 + oc * 512: kt * 2048 + (oc + 1) * 512],
                            start=(kt == 0), stop=(kt == 15)),
                            reads=[ck, "p1b_w"], writes=[bkey])
                    p.emit("dve", lambda e, ts=ts, oc=oc, bank=bank, bi=bi: e.tensor_add(
                        x1[bi][:, ts * 2048 + oc * 512: ts * 2048 + (oc + 1) * 512],
                        bank[:], xo[bi][:, ts * 2048 + oc * 512: ts * 2048 + (oc + 1) * 512]),
                        reads=[f"p1b_ps{oc}", xok], writes=[x1k])
            p.emit("act", lambda e, tc=tc, bi=bi: e.dma_start(
                out=d["x1"][tc * 512:(tc + 1) * 512, :].rearrange("(j p) h -> p j h", p=128),
                in_=x1[bi][:].rearrange("p (j h) -> p j h", h=2048)),
                reads=[x1k], writes=[f"x1_{tc}"], dma=True)



def phase2_ffn0(cx, d):
    lvl = int(os.environ.get("P2LVL", "4"))
    nc, p = cx.nc, cx.p
    with contextlib.ExitStack() as sst:
        w0sb = sst.enter_context(nc.sbuf_tensor("p2_w0", [128, 16 * 2048], BF16))
        a2sb = sst.enter_context(nc.sbuf_tensor("p2_a2", [128, 16 * 128], BF16))
        b2sb = sst.enter_context(nc.sbuf_tensor("p2_b2", [128, 2048], BF16))
        gwsb = sst.enter_context(nc.sbuf_tensor("p2_gw", [128, 16 * 8], BF16))
        xin = [sst.enter_context(nc.sbuf_tensor(f"p2_xin{i}", [128, 4 * 2048], F32)) for i in range(2)]
        xbf = sst.enter_context(nc.sbuf_tensor("p2_xbf", [128, 4 * 2048], BF16))
        xlnT = sst.enter_context(nc.sbuf_tensor("p2_xlnT", [128, 16 * 512], BF16))
        x2b = sst.enter_context(nc.sbuf_tensor("p2_x2", [128, 4 * 2048], F32))
        stats = sst.enter_context(nc.sbuf_tensor("p2_stats", [128, 16], F32))
        trash = sst.enter_context(nc.sbuf_tensor("p2_trash", [128, H], BF16))
        gl = sst.enter_context(nc.sbuf_tensor("p2_gl", [128, 8], F32))
        el = sst.enter_context(nc.sbuf_tensor("p2_el", [128, 8], F32))
        sel = sst.enter_context(nc.sbuf_tensor("p2_sel", [128, 8], F32))
        msk = sst.enter_context(nc.sbuf_tensor("p2_msk", [128, 8], F32))
        cw = sst.enter_context(nc.sbuf_tensor("p2_cw", [128, 8], F32))
        mm = sst.enter_context(nc.sbuf_tensor("p2_mm", [128, 4], F32))
        tmw = sst.enter_context(nc.sbuf_tensor("p2_tmw", [128, 128], BF16))
        tcwT = sst.enter_context(nc.sbuf_tensor("p2_tcwT", [128, 4 * 128], BF16))
        psf = [sst.enter_context(nc.psum_tensor(f"p2_psf{i}", [128, 512], F32)) for i in range(6)]
        psb = [sst.enter_context(nc.psum_tensor(f"p2_psb{i}", [128, 1024], BF16)) for i in range(2)]

        for i in range(4):
            p.emit("pool", lambda e, i=i: e.dma_start(
                out=w0sb[:, i * 4 * 2048:(i + 1) * 4 * 2048].rearrange("p (ht o) -> p ht o", o=2048),
                in_=d["w0T"][i * 512:(i + 1) * 512, :].rearrange("(ht p) o -> p ht o", p=128)),
                writes=["p2_w0"], dma=True)
        p.emit("pool", lambda e: e.dma_start(
            out=a2sb[:].rearrange("p (ht c) -> p ht c", c=128),
            in_=d["A2"].rearrange("(ht p) c -> p ht c", p=128)),
            writes=["p2_a2"], dma=True)
        p.emit("pool", lambda e: e.dma_start(out=b2sb[:], in_=d["B2"]),
               writes=["p2_b2"], dma=True)
        p.emit("pool", lambda e: e.dma_start(
            out=gwsb[:].rearrange("p (ht c) -> p ht c", c=8),
            in_=d["gatewT"].rearrange("(ht p) c -> p ht c", p=128)),
            writes=["p2_gw"], dma=True)

        for tc in range(int(os.environ.get("P2TC", "8"))):
            bi = tc % 2
            xink = f"p2_xin{bi}"
            p.emit("sp", lambda e, tc=tc, bi=bi: e.dma_start(
                out=xin[bi][:].rearrange("p (j h) -> p j h", h=H),
                in_=d["x1"][tc * 512:(tc + 1) * 512, :].rearrange("(j p) h -> p j h", p=128)),
                writes=[xink], dma=True)
            emit_ln_chunk(cx, xink, xin[bi][:], xbf, "p2_xbf", stats, "p2_stats", trash)
            emit_transpose_chunk(cx, xbf, "p2_xbf", xlnT, "p2_xlnT", psb[0], "p2_psb0")
            for ts in range(4):
              if lvl >= 3:
                bank, bkey = psf[4], "p2_psf4"
                for kt in range(16):
                    p.emit("pe", lambda e, ts=ts, kt=kt, bank=bank: e.matmul(
                        bank[:, 0:8], xlnT[:, kt * 512 + ts * 128: kt * 512 + ts * 128 + 128],
                        gwsb[:, kt * 8:(kt + 1) * 8], start=(kt == 0), stop=(kt == 15)),
                        reads=["p2_xlnT", "p2_gw"], writes=[bkey])
                p.emit("dve", lambda e, bank=bank: e.tensor_copy(gl[:], bank[:, 0:8]),
                       reads=[bkey], writes=["p2_gl"])
                p.emit("dve", lambda e: e.reduce_max(mm[:, 0:1], gl[:], axis=AXX, negate=True),
                       reads=["p2_gl"], writes=["p2_mm"])
                p.emit("act", lambda e: e.activation(el[:], gl[:], AF.Exp, bias=mm[:, 0:1]),
                       reads=["p2_gl", "p2_mm"], writes=["p2_el"])
                p.emit("dve", lambda e: e.reduce_max(mm[:, 0:1], el[:], axis=AXX),
                       reads=["p2_el"], writes=["p2_mm"])
                p.emit("dve", lambda e: e.tensor_scalar(msk[:], el[:], mm[:, 0:1], None, ALU.is_equal),
                       reads=["p2_el", "p2_mm"], writes=["p2_msk"])
                p.emit("dve", lambda e: e.scalar_tensor_tensor(sel[:], msk[:], -2.0, el[:], ALU.mult, ALU.add),
                       reads=["p2_msk", "p2_el"], writes=["p2_sel"])
                p.emit("dve", lambda e: e.reduce_max(mm[:, 1:2], sel[:], axis=AXX),
                       reads=["p2_sel"], writes=["p2_mm"])
                p.emit("dve", lambda e: e.tensor_add(mm[:, 2:3], mm[:, 0:1], mm[:, 1:2]),
                       reads=["p2_mm"], writes=["p2_mm"])
                p.emit("act", lambda e: e.activation(mm[:, 3:4], mm[:, 2:3], AF.Ln),
                       reads=["p2_mm"], writes=["p2_mm"])
                p.emit("act", lambda e: e.activation(mm[:, 3:4], mm[:, 3:4], AF.Exp, scale=-1.0),
                       reads=["p2_mm"], writes=["p2_mm"])
                p.emit("dve", lambda e: e.tensor_scalar(msk[:], el[:], mm[:, 1:2], None, ALU.is_ge),
                       reads=["p2_el", "p2_mm"], writes=["p2_msk"])
                p.emit("dve", lambda e: e.tensor_mul(sel[:], el[:], msk[:]),
                       reads=["p2_el", "p2_msk"], writes=["p2_sel"])
                p.emit("dve", lambda e: e.tensor_scalar(cw[:], sel[:], mm[:, 3:4], 16.0, ALU.mult, ALU.mult),
                       reads=["p2_sel", "p2_mm"], writes=["p2_cw"])
              if lvl >= 4:
                bank2, bkey2 = psf[5], "p2_psf5"
                for kt in range(16):
                    p.emit("pe", lambda e, ts=ts, kt=kt, bank=bank2: e.matmul(
                        bank[:, 0:128],
                        a2sb[:, kt * 128:(kt + 1) * 128],
                        xlnT[:, kt * 512 + ts * 128: kt * 512 + ts * 128 + 128],
                        start=(kt == 0), stop=(kt == 15)),
                        reads=["p2_a2", "p2_xlnT"], writes=[bkey2])
                p.emit("act", lambda e, bank=bank2: e.copy(tmw[:], bank[:, 0:128]),
                       reads=[bkey2], writes=["p2_tmw"])
                p.emit("pe", lambda e: e.transpose(psb[1][:, 0:128], tmw[:], cx.ident_sb[:]),
                       reads=["p2_tmw", "ident_sb"], writes=["p2_psb1"])
                p.emit("dve", lambda e: e.tensor_copy(tmw[:], psb[1][:, 0:128]),
                       reads=["p2_psb1"], writes=["p2_tmw"])
                for ex in range(8):
                    p.emit("dve", lambda e, ex=ex: e.tensor_scalar(
                        tmw[:, ex * 16:(ex + 1) * 16], tmw[:, ex * 16:(ex + 1) * 16],
                        cw[:, ex:ex + 1], None, ALU.mult),
                        reads=["p2_tmw", "p2_cw"], writes=["p2_tmw"])
                p.emit("pe", lambda e: e.transpose(psb[1][:, 128:256], tmw[:], cx.ident_sb[:]),
                       reads=["p2_tmw", "ident_sb"], writes=["p2_psb1"])
                p.emit("dve", lambda e, ts=ts: e.tensor_copy(tcwT[:, ts * 128:(ts + 1) * 128], psb[1][:, 128:256]),
                       reads=["p2_psb1"], writes=["p2_tcwT"])
              if lvl >= 2:
                for oc in range(4):
                    bank3 = psf[oc]
                    bkey3 = f"p2_psf{oc}"
                    for kt in range(16 if lvl >= 4 else 15):
                        p.emit("pe", lambda e, ts=ts, oc=oc, kt=kt, bank=bank3: e.matmul(
                            bank[:],
                            xlnT[:, kt * 512 + ts * 128: kt * 512 + ts * 128 + 128],
                            w0sb[:, kt * 2048 + oc * 512: kt * 2048 + (oc + 1) * 512],
                            start=(kt == 0), stop=False),
                            reads=["p2_xlnT", "p2_w0"], writes=[bkey3])
                    if lvl >= 4:
                        p.emit("pe", lambda e, ts=ts, oc=oc, bank=bank3: e.matmul(
                            bank[:], tcwT[:, ts * 128:(ts + 1) * 128],
                            b2sb[:, oc * 512:(oc + 1) * 512], start=False, stop=True),
                            reads=["p2_tcwT", "p2_b2"], writes=[bkey3])
                    else:
                        p.emit("pe", lambda e, ts=ts, oc=oc, kt=15, bank=bank3: e.matmul(
                            bank[:],
                            xlnT[:, kt * 512 + ts * 128: kt * 512 + ts * 128 + 128],
                            w0sb[:, kt * 2048 + oc * 512: kt * 2048 + (oc + 1) * 512],
                            start=False, stop=True),
                            reads=["p2_xlnT", "p2_w0"], writes=[bkey3])
                    ev = int(os.environ.get("P2EVAC", "1"))
                    if (lvl >= 3 or ev == 1):
                        p.emit("dve", lambda e, ts=ts, oc=oc, bank=bank3, bi=bi: e.tensor_add(
                            x2b[:, ts * 2048 + oc * 512: ts * 2048 + (oc + 1) * 512],
                            bank[:], xin[bi][:, ts * 2048 + oc * 512: ts * 2048 + (oc + 1) * 512]),
                            reads=[bkey3, xink], writes=["p2_x2"])
                    elif ev == 2:
                        p.emit("dve", lambda e, ts=ts, oc=oc, bank=bank3: e.tensor_copy(
                            x2b[:, ts * 2048 + oc * 512: ts * 2048 + (oc + 1) * 512],
                            bank[:]),
                            reads=[bkey3], writes=["p2_x2"])
                    else:
                        p.emit("dve", lambda e, ts=ts, oc=oc, bank=bank3, bi=bi: e.tensor_copy(
                            x2b[:, ts * 2048 + oc * 512: ts * 2048 + (oc + 1) * 512],
                            xin[bi][:, ts * 2048 + oc * 512: ts * 2048 + (oc + 1) * 512]),
                            reads=[bkey3, xink], writes=["p2_x2"])
              if lvl < 2:
                for j in range(4):
                    p.emit("dve", lambda e, j=j, bi=bi: e.tensor_copy(
                        x2b[:, j * 2048:(j + 1) * 2048], xin[bi][:, j * 2048:(j + 1) * 2048]),
                        reads=[xink], writes=["p2_x2"])
            p.emit("act", lambda e, tc=tc: e.dma_start(
                out=d["x2"][tc * 512:(tc + 1) * 512, :].rearrange("(j p) h -> p j h", p=128),
                in_=x2b[:].rearrange("p (j h) -> p j h", h=H)),
                reads=["p2_x2"], writes=[f"x2_{tc}"], dma=True)



def phase3_qkv(cx, d):
    nc, p = cx.nc, cx.p
    with contextlib.ExitStack() as sst:
        xlnF = sst.enter_context(nc.sbuf_tensor("p3_xlnF", [128, 16 * 4096], BF16))
        xin = [sst.enter_context(nc.sbuf_tensor("p3_xin0", [128, 4 * 2048], F32))] * 2
        xbf = sst.enter_context(nc.sbuf_tensor("p3_xbf", [128, 4 * 2048], BF16))
        stats = sst.enter_context(nc.sbuf_tensor("p3_stats", [128, 16], F32))
        trash = sst.enter_context(nc.sbuf_tensor("p3_trash", [128, H], BF16))
        wbuf = [sst.enter_context(nc.sbuf_tensor(f"p3_w{i}", [128, 16 * 128], BF16)) for i in range(2)]
        qbuf = [sst.enter_context(nc.sbuf_tensor(f"p3_q{i}", [128, 512], BF16)) for i in range(2)]
        psf = [sst.enter_context(nc.psum_tensor(f"p3_psf{i}", [128, 512], F32)) for i in range(2)]
        psb = [sst.enter_context(nc.psum_tensor(f"p3_psb{i}", [128, 1024], BF16)) for i in range(2)]

        for tc in range(8):
            bi = 0
            xink = "p3_xin0"
            p.emit("sp", lambda e, tc=tc, bi=bi: e.dma_start(
                out=xin[bi][:].rearrange("p (j h) -> p j h", h=H),
                in_=d["x2"][tc * 512:(tc + 1) * 512, :].rearrange("(j p) h -> p j h", p=128)),
                writes=[xink], dma=True)
            emit_ln_chunk(cx, xink, xin[bi][:], xbf, "p3_xbf", stats, "p3_stats", trash)
            for ht in range(16):
                pk = f"p3_psb{ht % 2}"
                for j in range(4):
                    src = xbf[:, j * 2048 + ht * 128: j * 2048 + ht * 128 + 128]
                    p.emit("pe", lambda e, s=src, j=j, ht=ht: e.transpose(
                        psb[ht % 2][:, j * 128:(j + 1) * 128], s, cx.ident_sb[:]),
                        reads=["p3_xbf", "ident_sb"], writes=[pk])
                p.emit("dve", lambda e, ht=ht, tc=tc: e.tensor_copy(
                    xlnF[:, ht * 4096 + tc * 512: ht * 4096 + tc * 512 + 512], psb[ht % 2][:, 0:512]),
                    reads=[pk], writes=["p3_xlnF"])
        for dt in range(48):
            wb = dt % 2
            wk = f"p3_w{wb}"
            p.emit("pool", lambda e, dt=dt, wb=wb: e.dma_start(
                out=wbuf[wb][:].rearrange("p (ht c) -> p ht c", c=128),
                in_=d["wqkv1T"][:, dt * 128:(dt + 1) * 128].rearrange("(ht p) c -> p ht c", p=128)),
                writes=[wk], dma=True)
            for tg in range(8):
                bank, bkey = psf[tg % 2], f"p3_psf{tg % 2}"
                for kt in range(16):
                    p.emit("pe", lambda e, dt=dt, tg=tg, kt=kt, bank=bank, wb=wb: e.matmul(
                        bank[:], wbuf[wb][:, kt * 128:(kt + 1) * 128],
                        xlnF[:, kt * 4096 + tg * 512: kt * 4096 + (tg) * 512 + 512],
                        start=(kt == 0), stop=(kt == 15)),
                        reads=[wk, "p3_xlnF"], writes=[bkey])
                qb = tg % 2
                qk = f"p3_q{qb}"
                if dt < 16:
                    p.emit("dve", lambda e, bank=bank, qb=qb: e.tensor_scalar(
                        qbuf[qb][:], bank[:], NORM, None, ALU.mult), reads=[bkey], writes=[qk])
                else:
                    p.emit("dve", lambda e, bank=bank, qb=qb: e.tensor_copy(
                        qbuf[qb][:], bank[:]), reads=[bkey], writes=[qk])
                p.emit("act", lambda e, dt=dt, tg=tg, qb=qb: e.dma_start(
                    out=d["qkvT1"][dt * 128:(dt + 1) * 128, tg * 512:(tg + 1) * 512], in_=qbuf[qb][:]),
                    reads=[qk], writes=[f"qkvT1_{dt}_{tg}"], dma=True)


_CAUSAL_MASK = None


def _causal_mask_np():
    global _CAUSAL_MASK
    if _CAUSAL_MASK is None:
        m = np.zeros((128, 4 * 512), np.float32)
        for dlt in range(4):
            kj = np.arange(128)[:, None]
            qj = np.arange(512)[None, :]
            m[:, dlt * 512:(dlt + 1) * 512] = np.where(128 * dlt + kj <= qj, 0.0, -1e30)
        _CAUSAL_MASK = m
    return _CAUSAL_MASK


def phase3_attn(cx, d):
    nc, p = cx.nc, cx.p
    with contextlib.ExitStack() as sst:
        qT = sst.enter_context(nc.sbuf_tensor("p3c_qT", [128, 4096], BF16))
        kT = sst.enter_context(nc.sbuf_tensor("p3c_kT", [128, 4096], BF16))
        vT = sst.enter_context(nc.sbuf_tensor("p3c_vT", [128, 4096], BF16))
        vE = sst.enter_context(nc.sbuf_tensor("p3c_vE", [128, 32 * 132], BF16))
        expT = sst.enter_context(nc.sbuf_tensor("p3c_expT", [128, 32 * 512], BF16))
        msksb = sst.enter_context(nc.sbuf_tensor("p3c_msk", [128, 4 * 512], F32))
        ctmp = sst.enter_context(nc.sbuf_tensor("p3c_ctmp", [128, 128], BF16))
        cTsb = sst.enter_context(nc.sbuf_tensor("p3c_cTsb", [128, 512], BF16))
        rs1 = sst.enter_context(nc.sbuf_tensor("p3c_rs1", [128, 1], F32))
        psf = [sst.enter_context(nc.psum_tensor(f"p3c_psf{i}", [128, 512], F32)) for i in range(4)]
        psb = [sst.enter_context(nc.psum_tensor(f"p3c_psb{i}", [128, 1024], BF16)) for i in range(3)]

        mask_t = nc.inline_tensor(_causal_mask_np(), "causal_mask")
        p.emit("sp", lambda e: e.dma_start(out=msksb[:], in_=mask_t.ap()),
               writes=["p3c_msk"], dma=True)
        p.emit("dve", lambda e: e.memset(
            vE[:].rearrange("p (kt c) -> p kt c", c=132)[:, :, 128:129], 1.0),
            writes=["p3c_vE_ones"])

        for hd in range(16):
            p.emit("sp", lambda e, hd=hd: e.dma_start(
                out=qT[:], in_=d["qkvT1"][hd * 128:(hd + 1) * 128, :]),
                writes=["p3c_qT"], dma=True)
            p.emit("sp", lambda e, hd=hd: e.dma_start(
                out=kT[:], in_=d["qkvT1"][2048 + hd * 128: 2048 + (hd + 1) * 128, :]),
                writes=["p3c_kT"], dma=True)
            p.emit("sp", lambda e, hd=hd: e.dma_start(
                out=vT[:], in_=d["qkvT1"][4096 + hd * 128: 4096 + (hd + 1) * 128, :]),
                writes=["p3c_vT"], dma=True)
            for g4 in range(8):
                pk = f"p3c_psb{g4 % 2}"
                for q4 in range(4):
                    p.emit("pe", lambda e, g4=g4, q4=q4: e.transpose(
                        psb[g4 % 2][:, q4 * 128:(q4 + 1) * 128],
                        vT[:, g4 * 512 + q4 * 128: g4 * 512 + q4 * 128 + 128], cx.ident_sb[:]),
                        reads=["p3c_vT", "ident_sb"], writes=[pk])
                p.emit("dve", lambda e, g4=g4: e.tensor_copy(
                    vE[:].rearrange("p (kt c) -> p kt c", c=132)[:, g4 * 4:(g4 + 1) * 4, 0:128],
                    psb[g4 % 2][:].rearrange("p (q4 c) -> p q4 c", c=128)[:, 0:4, :]),
                    reads=[pk], writes=["p3c_vE"])
            row0 = hd * 128
            for qc in range(8):
                nkt = 4 * qc + 4
                for kt in range(nkt):
                    bank, bkey = psf[2 + kt % 2], f"p3c_psf{2 + kt % 2}"
                    p.emit("pe", lambda e, kt=kt, qc=qc, bank=bank: e.matmul(
                        bank[:], kT[:, kt * 128:(kt + 1) * 128],
                        qT[:, qc * 512:(qc + 1) * 512], start=True, stop=True),
                        reads=["p3c_kT", "p3c_qT"], writes=[bkey])
                    if kt >= 4 * qc:
                        dlt = kt - 4 * qc
                        p.emit("dve", lambda e, bank=bank, dlt=dlt: e.tensor_add(
                            bank[:], bank[:], msksb[:, dlt * 512:(dlt + 1) * 512]),
                            reads=[bkey, "p3c_msk"], writes=[bkey])
                    p.emit("act", lambda e, kt=kt, bank=bank: e.activation(
                        expT[:, kt * 512:(kt + 1) * 512], bank[:], AF.Exp, bias=cx.eshift),
                        reads=[bkey, "consts"], writes=[f"p3c_expT{kt}"])
                for qs in range(4):
                    bank, bkey = psf[qs % 2], f"p3c_psf{qs % 2}"
                    nk2 = 4 * qc + qs + 1
                    for kt in range(nk2):
                        p.emit("pe", lambda e, kt=kt, qs=qs, bank=bank, nk2=nk2: e.matmul(
                            bank[:, 0:129], expT[:, kt * 512 + qs * 128: kt * 512 + qs * 128 + 128],
                            vE[:, kt * 132: kt * 132 + 129],
                            start=(kt == 0), stop=(kt == nk2 - 1)),
                            reads=[f"p3c_expT{kt}", "p3c_vE", "p3c_vE_ones"], writes=[bkey])
                    p.emit("act", lambda e, bank=bank: e.activation(rs1[:], bank[:, 128:129], AF.Ln),
                           reads=[bkey], writes=["p3c_rs1"])
                    p.emit("act", lambda e: e.activation(rs1[:], rs1[:], AF.Exp, scale=-1.0),
                           reads=["p3c_rs1"], writes=["p3c_rs1"])
                    p.emit("dve", lambda e, bank=bank: e.tensor_scalar(
                        ctmp[:], bank[:, 0:128], rs1[:], None, ALU.mult),
                        reads=[bkey, "p3c_rs1"], writes=["p3c_ctmp"])
                    p.emit("pe", lambda e, qs=qs: e.transpose(
                        psb[2][:, qs * 128:(qs + 1) * 128], ctmp[:], cx.ident_sb[:]),
                        reads=["p3c_ctmp", "ident_sb"], writes=["p3c_psb2"])
                    p.emit("dve", lambda e, qs=qs: e.tensor_copy(
                        cTsb[:, qs * 128:(qs + 1) * 128], psb[2][:, qs * 128:(qs + 1) * 128]),
                        reads=["p3c_psb2"], writes=["p3c_cTsb"])
                p.emit("act", lambda e, row0=row0, qc=qc: e.dma_start(
                    out=d["ctxT1"][row0:row0 + 128, qc * 512:(qc + 1) * 512], in_=cTsb[:]),
                    reads=["p3c_cTsb"], writes=[f"ctxT1_{row0}_{qc}"], dma=True)


def phase3_proj(cx, d):
    nc, p = cx.nc, cx.p
    with contextlib.ExitStack() as sst:
        wsb = sst.enter_context(nc.sbuf_tensor("p3d_w", [128, 16 * 2048], BF16))
        cT = [sst.enter_context(nc.sbuf_tensor(f"p3d_cT{i}", [128, 16 * 512], BF16)) for i in range(2)]
        xo = [sst.enter_context(nc.sbuf_tensor(f"p3d_xo{i}", [128, 4 * 2048], F32)) for i in range(2)]
        x1 = [sst.enter_context(nc.sbuf_tensor("p3d_x10", [128, 4 * 2048], F32))] * 2
        psf = [sst.enter_context(nc.psum_tensor(f"p3d_ps{i}", [128, 512], F32)) for i in range(4)]

        for i in range(4):
            p.emit("pool", lambda e, i=i: e.dma_start(
                out=wsb[:, i * 4 * 2048:(i + 1) * 4 * 2048].rearrange("p (ht o) -> p ht o", o=2048),
                in_=d["wo1T"][i * 512:(i + 1) * 512, :].rearrange("(ht p) o -> p ht o", p=128)),
                writes=["p3d_w"], dma=True)
        for tc in range(8):
            bi = tc % 2
            ck, xok, x1k = f"p3d_cT{bi}", f"p3d_xo{bi}", "p3d_x10"
            p.emit("sp", lambda e, tc=tc, bi=bi: e.dma_start(
                out=cT[bi][:].rearrange("p (ht t) -> p ht t", t=512),
                in_=d["ctxT1"][:, tc * 512:(tc + 1) * 512].rearrange("(ht p) t -> p ht t", p=128)),
                writes=[ck], dma=True)
            p.emit("sp", lambda e, tc=tc, bi=bi: e.dma_start(
                out=xo[bi][:].rearrange("p (j h) -> p j h", h=2048),
                in_=d["x2"][tc * 512:(tc + 1) * 512, :].rearrange("(j p) h -> p j h", p=128)),
                writes=[xok], dma=True)
            for ts in range(4):
                for oc in range(4):
                    bank = psf[oc]
                    bkey = f"p3d_ps{oc}"
                    for kt in range(16):
                        p.emit("pe", lambda e, ts=ts, oc=oc, kt=kt, bank=bank, bi=bi: e.matmul(
                            bank[:], cT[bi][:, kt * 512 + ts * 128: kt * 512 + ts * 128 + 128],
                            wsb[:, kt * 2048 + oc * 512: kt * 2048 + (oc + 1) * 512],
                            start=(kt == 0), stop=(kt == 15)),
                            reads=[ck, "p3d_w"], writes=[bkey])
                    p.emit("dve", lambda e, ts=ts, oc=oc, bank=bank, bi=bi: e.tensor_add(
                        x1[bi][:, ts * 2048 + oc * 512: ts * 2048 + (oc + 1) * 512],
                        bank[:], xo[bi][:, ts * 2048 + oc * 512: ts * 2048 + (oc + 1) * 512]),
                        reads=[f"p3d_ps{oc}", xok], writes=[x1k])
            p.emit("act", lambda e, tc=tc, bi=bi: e.dma_start(
                out=d["x3"][tc * 512:(tc + 1) * 512, :].rearrange("(j p) h -> p j h", p=128),
                in_=x1[bi][:].rearrange("p (j h) -> p j h", h=2048)),
                reads=[x1k], writes=[f"x3_{tc}"], dma=True)




def emit_nvfp4_decode_pool(cx, pk_ap, pk_key, sc_ap, sc_key, out_ap, out_key,
                           it_ap, it_key, tf, tf_key):
    p = cx.p
    sc_b = sc_ap.rearrange("p (b one) -> p b one", one=1).broadcast_to([128, 64, 16])
    cf, st_, rr, vv = (tf[:, i * 1024:(i + 1) * 1024] for i in range(4))
    for which in range(2):
        if which == 0:
            p.emit("dve", lambda e, pk_ap=pk_ap, it_ap=it_ap: e.tensor_scalar(
                it_ap, pk_ap, 15, None, ALU.bitwise_and), reads=[pk_key], writes=[it_key])
        else:
            p.emit("dve", lambda e, pk_ap=pk_ap, it_ap=it_ap: e.tensor_scalar(
                it_ap, pk_ap, 4, 15, ALU.logical_shift_right, ALU.bitwise_and),
                reads=[pk_key], writes=[it_key])
        E = "pool"
        p.emit(E, lambda e, a=cf, b=it_ap: e.tensor_copy(a[:], b), reads=[it_key], writes=[tf_key])
        p.emit(E, lambda e, a=rr, b=cf: e.tensor_scalar(a[:], b[:], -7.0, 0.0, ALU.add, ALU.max),
               reads=[tf_key], writes=[tf_key])
        p.emit(E, lambda e, a=st_, b=rr: e.tensor_scalar(a[:], b[:], 1.0, None, ALU.min),
               reads=[tf_key], writes=[tf_key])
        p.emit(E, lambda e, a=rr, b=st_: e.tensor_scalar(a[:], b[:], -8.0, None, ALU.mult),
               reads=[tf_key], writes=[tf_key])
        p.emit(E, lambda e, a=cf, b=rr: e.tensor_add(a[:], a[:], b[:]),
               reads=[tf_key], writes=[tf_key])
        p.emit(E, lambda e, a=rr, b=cf: e.tensor_scalar(a[:], b[:], -4.0, 0.0, ALU.add, ALU.max),
               reads=[tf_key], writes=[tf_key])
        p.emit(E, lambda e, a=vv, b=cf: e.tensor_scalar(a[:], b[:], -6.0, 0.0, ALU.add, ALU.max),
               reads=[tf_key], writes=[tf_key])
        p.emit(E, lambda e, a=rr: e.tensor_scalar(a[:], a[:], 0.5, None, ALU.mult),
               reads=[tf_key], writes=[tf_key])
        p.emit(E, lambda e, a=rr, b=vv: e.tensor_add(a[:], a[:], b[:]),
               reads=[tf_key], writes=[tf_key])
        p.emit(E, lambda e, a=cf: e.tensor_scalar(a[:], a[:], 0.5, None, ALU.mult),
               reads=[tf_key], writes=[tf_key])
        p.emit(E, lambda e, a=rr, b=cf: e.tensor_add(a[:], a[:], b[:]),
               reads=[tf_key], writes=[tf_key])
        p.emit(E, lambda e, a=st_: e.tensor_scalar(a[:], a[:], -2.0, 1.0, ALU.mult, ALU.add),
               reads=[tf_key], writes=[tf_key])
        p.emit(E, lambda e, a=vv, b=rr, c=st_: e.tensor_mul(a[:], b[:], c[:]),
               reads=[tf_key], writes=[tf_key])
        p.emit(E, lambda e, w=which, a=vv, scb=sc_b, o=out_ap: e.tensor_tensor(
            o[:, w:2048:2].rearrange("p (b k) -> p b k", k=16),
            a[:].rearrange("p (b k) -> p b k", k=16), scb, ALU.mult),
            reads=[tf_key, sc_key], writes=[out_key])


def emit_nvfp4_decode(cx, eng, pk_ap, pk_key, sc_ap, sc_key, out_ap, out_key,
                      it_ap, it_key, tf, tf_key):
    p = cx.p
    sc_b = sc_ap.rearrange("p (b one) -> p b one", one=1).broadcast_to([128, 64, 16])
    cf, st_, rr, vv = (tf[:, i * 1024:(i + 1) * 1024] for i in range(4))
    for which in range(2):
        if which == 0:
            p.emit(eng, lambda e, pk_ap=pk_ap, it_ap=it_ap: e.tensor_scalar(
                it_ap, pk_ap, 15, None, ALU.bitwise_and), reads=[pk_key], writes=[it_key])
        else:
            p.emit(eng, lambda e, pk_ap=pk_ap, it_ap=it_ap: e.tensor_scalar(
                it_ap, pk_ap, 4, 15, ALU.logical_shift_right, ALU.bitwise_and),
                reads=[pk_key], writes=[it_key])
        p.emit(eng, lambda e, a=cf, b=it_ap: e.tensor_copy(a[:], b), reads=[it_key], writes=[tf_key])
        p.emit(eng, lambda e, a=st_, b=cf: e.tensor_scalar(a[:], b[:], 7.5, None, ALU.is_ge),
               reads=[tf_key], writes=[tf_key])
        p.emit(eng, lambda e, a=cf, b=st_: e.scalar_tensor_tensor(a[:], b[:], -8.0, a[:], ALU.mult, ALU.add),
               reads=[tf_key], writes=[tf_key])
        p.emit(eng, lambda e, a=rr, b=cf: e.tensor_scalar(a[:], b[:], -4.0, 0.0, ALU.add, ALU.max),
               reads=[tf_key], writes=[tf_key])
        p.emit(eng, lambda e, a=vv, b=cf: e.tensor_scalar(a[:], b[:], -6.0, 0.0, ALU.add, ALU.max),
               reads=[tf_key], writes=[tf_key])
        p.emit(eng, lambda e, a=rr, b=vv: e.scalar_tensor_tensor(a[:], a[:], 0.5, b[:], ALU.mult, ALU.add),
               reads=[tf_key], writes=[tf_key])
        p.emit(eng, lambda e, a=rr, b=cf: e.scalar_tensor_tensor(a[:], b[:], 0.5, a[:], ALU.mult, ALU.add),
               reads=[tf_key], writes=[tf_key])
        p.emit(eng, lambda e, a=st_: e.tensor_scalar(a[:], a[:], -2.0, 1.0, ALU.mult, ALU.add),
               reads=[tf_key], writes=[tf_key])
        p.emit(eng, lambda e, a=vv, b=rr, c=st_: e.tensor_mul(a[:], b[:], c[:]),
               reads=[tf_key], writes=[tf_key])
        p.emit(eng, lambda e, w=which, a=vv, scb=sc_b, o=out_ap: e.tensor_tensor(
            o[:, w:2048:2].rearrange("p (b k) -> p b k", k=16),
            a[:].rearrange("p (b k) -> p b k", k=16), scb, ALU.mult),
            reads=[tf_key, sc_key], writes=[out_key])



def emit_nvfp4_decode_T(cx, pk_ap, pk_key, scs_ap, scs_key, out_lo, out_hi, out_key,
                        it_ap, it_key, tf, tf_key, W):
    p = cx.p
    cf, st_, rr, vv = (tf[:, i * W:(i + 1) * W] for i in range(4))
    for which, out in ((0, out_lo), (1, out_hi)):
        if which == 0:
            p.emit("dve", lambda e, a=it_ap, b=pk_ap: e.tensor_scalar(
                a, b, 15, None, ALU.bitwise_and), reads=[pk_key], writes=[it_key])
        else:
            p.emit("dve", lambda e, a=it_ap, b=pk_ap: e.tensor_scalar(
                a, b, 4, 15, ALU.logical_shift_right, ALU.bitwise_and),
                reads=[pk_key], writes=[it_key])
        p.emit("dve", lambda e, a=cf, b=it_ap: e.tensor_copy(a[:], b), reads=[it_key], writes=[tf_key])
        p.emit("dve", lambda e, a=st_, b=cf: e.tensor_scalar(a[:], b[:], 7.5, None, ALU.is_ge),
               reads=[tf_key], writes=[tf_key])
        p.emit("dve", lambda e, a=cf, b=st_: e.scalar_tensor_tensor(a[:], b[:], -8.0, a[:], ALU.mult, ALU.add),
               reads=[tf_key], writes=[tf_key])
        p.emit("dve", lambda e, a=rr, b=cf: e.tensor_scalar(a[:], b[:], -4.0, 0.0, ALU.add, ALU.max),
               reads=[tf_key], writes=[tf_key])
        p.emit("dve", lambda e, a=vv, b=cf: e.tensor_scalar(a[:], b[:], -6.0, 0.0, ALU.add, ALU.max),
               reads=[tf_key], writes=[tf_key])
        p.emit("dve", lambda e, a=rr, b=vv: e.scalar_tensor_tensor(a[:], a[:], 0.5, b[:], ALU.mult, ALU.add),
               reads=[tf_key], writes=[tf_key])
        p.emit("dve", lambda e, a=rr, b=cf: e.scalar_tensor_tensor(a[:], b[:], 0.5, a[:], ALU.mult, ALU.add),
               reads=[tf_key], writes=[tf_key])
        p.emit("dve", lambda e, a=st_: e.tensor_scalar(a[:], a[:], -2.0, 1.0, ALU.mult, ALU.add),
               reads=[tf_key], writes=[tf_key])
        p.emit("dve", lambda e, a=vv, b=rr, c=st_: e.tensor_mul(a[:], b[:], c[:]),
               reads=[tf_key], writes=[tf_key])
        p.emit("dve", lambda e, a=vv, s=scs_ap, o=out: e.tensor_tensor(o, a[:], s, ALU.mult),
               reads=[tf_key, scs_key], writes=[out_key])


def scale_bcast_ap(sc_dram, b0, n0, N):
    return (sc_dram[b0:b0 + 8, n0:n0 + N]
            .rearrange("b (one n) -> b one n", one=1)
            .broadcast_to([8, 16, N]))


def phase4_gateup(cx, d):
    nc, p = cx.nc, cx.p
    NC = 256
    with contextlib.ExitStack() as sst:
        xlnF = sst.enter_context(nc.sbuf_tensor("p4_xlnF", [128, 16 * 4096], BF16))
        psbln = [sst.enter_context(nc.psum_tensor(f"p4_psbln{i}", [128, 1024], BF16)) for i in range(2)]
        with contextlib.ExitStack() as lnst:
            xin = [lnst.enter_context(nc.sbuf_tensor("p4_xin0", [128, 4 * 2048], F32))] * 2
            xbf = lnst.enter_context(nc.sbuf_tensor("p4_xbf", [128, 4 * 2048], BF16))
            stats = lnst.enter_context(nc.sbuf_tensor("p4_stats", [128, 16], F32))
            trash = lnst.enter_context(nc.sbuf_tensor("p4_trash", [128, H], BF16))
            for tc in range(8):
                xink = "p4_xin0"
                p.emit("sp", lambda e, tc=tc: e.dma_start(
                    out=xin[0][:].rearrange("p (j h) -> p j h", h=H),
                    in_=d["x3"][tc * 512:(tc + 1) * 512, :].rearrange("(j p) h -> p j h", p=128)),
                    writes=[xink], dma=True)
                emit_ln_chunk(cx, xink, xin[0][:], xbf, "p4_xbf", stats, "p4_stats", trash)
                for kt in range(16):
                    wt, nib = kt // 2, kt % 2
                    pk2 = f"p4_psbln{kt % 2}"
                    for j in range(4):
                        base = j * 2048 + wt * 256 + nib
                        src2 = xbf[:, base: base + 255: 2]
                        p.emit("pe", lambda e, s=src2, j=j, kt=kt: e.transpose(
                            psbln[kt % 2][:, j * 128:(j + 1) * 128], s, cx.ident_sb[:]),
                            reads=["p4_xbf", "ident_sb"], writes=[pk2])
                    p.emit("dve", lambda e, kt=kt, tc=tc: e.tensor_copy(
                        xlnF[:, kt * 4096 + tc * 512: kt * 4096 + tc * 512 + 512],
                        psbln[kt % 2][:, 0:512]),
                        reads=[pk2], writes=["p4_xlnF"])
        p.barrier()
        pkb = [sst.enter_context(nc.sbuf_tensor(f"p4_pk{i}", [128, NC], I32)) for i in range(4)]
        scb = [sst.enter_context(nc.sbuf_tensor(f"p4_sc{i}", [128, NC], F32)) for i in range(4)]
        it = sst.enter_context(nc.sbuf_tensor("p4_it", [128, NC], I32))
        tf = sst.enter_context(nc.sbuf_tensor("p4_tf", [128, 4 * NC], BF16))
        wgT = [sst.enter_context(nc.sbuf_tensor(f"p4_wgT{i}", [128, 16 * NC], BF16)) for i in range(4)]
        ge = sst.enter_context(nc.sbuf_tensor("p4_ge", [128, 512], F32))
        ue = sst.enter_context(nc.sbuf_tensor("p4_ue", [128, 512], F32))
        g2 = sst.enter_context(nc.sbuf_tensor("p4_g2", [128, 512], F32))
        g3 = sst.enter_context(nc.sbuf_tensor("p4_g3", [128, 512], F32))
        itb = [sst.enter_context(nc.sbuf_tensor(f"p4_itb{i}", [128, 512], BF16)) for i in range(2)]
        psf = [sst.enter_context(nc.psum_tensor(f"p4_psf{i}", [128, 512], F32)) for i in range(4)]

        NG = MLP // NC

        def prep(ng):
            par = ng % 2
            for mi, (pnm, snm) in enumerate((("pkg", "sg"), ("pku", "su"))):
                bu = 2 * par + mi
                wk = f"p4_wgT{bu}"
                for wt in range(8):
                    pki, sci = f"p4_pk{bu}", f"p4_sc{bu}"
                    q = "sp" if wt % 2 else "pool"
                    p.emit(q, lambda e, ng=ng, wt=wt, pnm=pnm, bu=bu: e.dma_start(
                        out=pkb[bu][:], in_=d[pnm][wt * 128:(wt + 1) * 128, ng * NC:(ng + 1) * NC]),
                        writes=[pki], dma=True)
                    p.emit(q, lambda e, ng=ng, wt=wt, snm=snm, bu=bu: e.dma_start(
                        out=scb[bu][:], in_=d[snm][wt * 128:(wt + 1) * 128, ng * NC:(ng + 1) * NC]),
                        writes=[sci], dma=True)
                    emit_nvfp4_decode_T(
                        cx, pkb[bu][:], pki, scb[bu][:], sci,
                        wgT[bu][:, (2 * wt) * NC:(2 * wt + 1) * NC],
                        wgT[bu][:, (2 * wt + 1) * NC:(2 * wt + 2) * NC],
                        wk, it[:], "p4_it", tf, "p4_tf", NC)

        prep(0)
        for ng in range(NG):
            par = ng % 2
            for ns in range(2):
                nt = ng * 2 + ns
                for tg in range(8):
                    for mi in range(2):
                        bu = 2 * par + mi
                        bank, bkey = psf[mi], f"p4_psf{mi}"
                        for kt in range(16):
                            p.emit("pe", lambda e, kt=kt, tg=tg, bu=bu, ns=ns, bank=bank: e.matmul(
                                bank[:], wgT[bu][:, kt * NC + ns * 128: kt * NC + ns * 128 + 128],
                                xlnF[:, kt * 4096 + tg * 512: kt * 4096 + tg * 512 + 512],
                                start=(kt == 0), stop=(kt == 15)),
                                reads=[f"p4_wgT{bu}", "p4_xlnF"], writes=[bkey])
                    p.emit("act", lambda e: e.copy(ge[:], psf[0][:]), reads=["p4_psf0"], writes=["p4_ge"])
                    p.emit("act", lambda e: e.copy(ue[:], psf[1][:]), reads=["p4_psf1"], writes=["p4_ue"])
                    p.emit("act", lambda e: e.activation(g2[:], ge[:], AF.Square),
                           reads=["p4_ge"], writes=["p4_g2"])
                    p.emit("act", lambda e: e.activation(
                        g2[:], g2[:], AF.Copy, scale=0.7978845608028654 * 0.044715,
                        bias=0.7978845608028654),
                        reads=["p4_g2", "consts"], writes=["p4_g2"])
                    p.emit("dve", lambda e: e.tensor_mul(g3[:], ge[:], g2[:]),
                           reads=["p4_ge", "p4_g2"], writes=["p4_g3"])
                    p.emit("act", lambda e: e.activation(g3[:], g3[:], AF.Tanh),
                           reads=["p4_g3"], writes=["p4_g3"])
                    p.emit("act", lambda e: e.activation(g3[:], g3[:], AF.Copy, scale=0.5, bias=0.5),
                           reads=["p4_g3", "consts"], writes=["p4_g3"])
                    p.emit("dve", lambda e: e.tensor_mul(g2[:], ge[:], g3[:]),
                           reads=["p4_ge", "p4_g3"], writes=["p4_g2"])
                    ib = tg % 2
                    p.emit("dve", lambda e, ib=ib: e.tensor_mul(itb[ib][:], g2[:], ue[:]),
                           reads=["p4_g2", "p4_ue"], writes=[f"p4_itb{ib}"])
                    p.emit("act", lambda e, nt=nt, tg=tg, ib=ib: e.dma_start(
                        out=d["interT"][nt * 128:(nt + 1) * 128, tg * 512:(tg + 1) * 512], in_=itb[ib][:]),
                        reads=[f"p4_itb{ib}"], writes=[f"interT_{nt}_{tg}"], dma=True)
                if ns == 0 and ng + 1 < NG:
                    prep(ng + 1)


def phase4_wd(cx, d):
    nc, p = cx.nc, cx.p
    W = 512
    with contextlib.ExitStack() as sst:
        pkb = [sst.enter_context(nc.sbuf_tensor(f"p4c_pk{i}", [128, W], I32)) for i in range(2)]
        scb = [sst.enter_context(nc.sbuf_tensor(f"p4c_sc{i}", [128, W], F32)) for i in range(2)]
        it = sst.enter_context(nc.sbuf_tensor("p4c_it", [128, W], I32))
        tf = sst.enter_context(nc.sbuf_tensor("p4c_tf", [128, 4 * W], BF16))
        wde = [sst.enter_context(nc.sbuf_tensor(f"p4c_wde{i}", [128, 2 * W], BF16)) for i in range(2)]

        for wt in range(64):
            for hc in range(4):
                bi = (wt * 4 + hc) % 2
                pki, sci, wk = f"p4c_pk{bi}", f"p4c_sc{bi}", f"p4c_wde{bi}"
                q = "sp" if bi else "pool"
                p.emit(q, lambda e, wt=wt, hc=hc, bi=bi: e.dma_start(
                    out=pkb[bi][:], in_=d["pkd"][wt * 128:(wt + 1) * 128, hc * W:(hc + 1) * W]),
                    writes=[pki], dma=True)
                p.emit(q, lambda e, wt=wt, hc=hc, bi=bi: e.dma_start(
                    out=scb[bi][:], in_=d["sd"][wt * 128:(wt + 1) * 128, hc * W:(hc + 1) * W]),
                    writes=[sci], dma=True)
                emit_nvfp4_decode_T(
                    cx, pkb[bi][:], pki, scb[bi][:], sci,
                    wde[bi][:, 0:W], wde[bi][:, W:2 * W],
                    wk, it[:], "p4c_it", tf, "p4c_tf", W)
                for nib in range(2):
                    r0 = (2 * wt + nib) * 128
                    p.emit("act", lambda e, r0=r0, hc=hc, bi=bi, nib=nib: e.dma_start(
                        out=d["wdT"][r0:r0 + 128, hc * W:(hc + 1) * W],
                        in_=wde[bi][:, nib * W:(nib + 1) * W]),
                        reads=[wk], writes=[f"wdT_{r0}_{hc}"], dma=True)


def phase4_down(cx, d):
    nc, p = cx.nc, cx.p
    with contextlib.ExitStack() as sst:
        itF = sst.enter_context(nc.sbuf_tensor("p4d_itF", [128, 128 * 512], BF16))
        wdt = [sst.enter_context(nc.sbuf_tensor(f"p4d_wdt{i}", [128, 512], BF16)) for i in range(6)]
        xo = sst.enter_context(nc.sbuf_tensor("p4d_xo", [128, 4 * 2048], F32))
        x4 = sst.enter_context(nc.sbuf_tensor("p4d_x4", [128, 4 * 2048], F32))
        psf = [sst.enter_context(nc.psum_tensor(f"p4d_ps{i}", [128, 512], F32)) for i in range(4)]

        for tb in range(8):
            for sb8 in range(8):
                for nib in range(2):
                    p.emit("sp", lambda e, tb=tb, sb8=sb8, nib=nib: e.dma_start(
                        out=itF[:].rearrange("p (kt t) -> p kt t", t=512)[
                            :, sb8 * 16 + nib: sb8 * 16 + 16: 2, :],
                        in_=d["interT"][sb8 * 2048 + nib:(sb8 + 1) * 2048: 2,
                                        tb * 512:(tb + 1) * 512].rearrange(
                            "(wt p) t -> p wt t", p=128)),
                        writes=["p4d_itF"], dma=True)
            p.emit("sp", lambda e, tb=tb: e.dma_start(
                out=xo[:].rearrange("p (j h) -> p j h", h=H),
                in_=d["x3"][tb * 512:(tb + 1) * 512, :].rearrange("(j p) h -> p j h", p=128)),
                writes=["p4d_xo"], dma=True)
            for hc in range(4):
                for nt in range(128):
                    wb = nt % 6
                    wk = f"p4d_wdt{wb}"
                    q = "pool" if nt % 2 else "sp"
                    p.emit(q, lambda e, nt=nt, hc=hc, wb=wb: e.dma_start(
                        out=wdt[wb][:], in_=d["wdT"][nt * 128:(nt + 1) * 128, hc * 512:(hc + 1) * 512]),
                        writes=[wk], dma=True)
                    for ts in range(4):
                        bank, bkey = psf[ts], f"p4d_ps{ts}"
                        p.emit("pe", lambda e, nt=nt, ts=ts, wb=wb, bank=bank: e.matmul(
                            bank[:], itF[:, nt * 512 + ts * 128: nt * 512 + ts * 128 + 128],
                            wdt[wb][:], start=(nt == 0), stop=(nt == 127)),
                            reads=[wk, "p4d_itF"], writes=[bkey])
                for ts in range(4):
                    p.emit("dve", lambda e, ts=ts, hc=hc: e.tensor_add(
                        x4[:, ts * 2048 + hc * 512: ts * 2048 + hc * 512 + 512],
                        psf[ts][:], xo[:, ts * 2048 + hc * 512: ts * 2048 + hc * 512 + 512]),
                        reads=[f"p4d_ps{ts}", "p4d_xo"], writes=["p4d_x4"])
            p.emit("act", lambda e, tb=tb: e.dma_start(
                out=d["x4"][tb * 512:(tb + 1) * 512, :].rearrange("(j p) h -> p j h", p=128),
                in_=x4[:].rearrange("p (j h) -> p j h", h=H)),
                reads=["p4d_x4"], writes=[f"x4_{tb}"], dma=True)



def _prep_inputs(inp):
    unused = (inp["L0_attn_key"], inp["L0_attn_value"])
    del unused
    f = np.ascontiguousarray
    x = np.asarray(inp["L0_attn_query"], dtype=np.float32)
    Wqkv = np.asarray(inp["L0_attn_Wqkv"], dtype=np.float32)
    per_core = []
    shared = {
        "wproj0T": f(np.asarray(inp["L0_attn_Wproj"], np.float32).T),
        "bproj0": f(np.asarray(inp["L0_attn_bproj"], np.float32)[None, :]),
        "w0T": f(np.asarray(inp["L0_ffn_weight"], np.float32).T),
        "gatewT": f(np.asarray(inp["L0_ffn_gate_w"], np.float32).T),
        "A2": f(np.asarray(inp["L0_ffn_A_w"], np.float32).reshape(128, 2048).T),
        "B2": f(np.asarray(inp["L0_ffn_B_w"], np.float32).transpose(0, 2, 1).reshape(128, 2048)),
        "wqkv1T": f(np.asarray(inp["L1_attn_w_qkv"], np.float32).T),
        "wo1T": f(np.asarray(inp["L1_attn_w_o"], np.float32).T),
        "pkg": f(np.asarray(inp["L1_ffn_gate_W_packed"], np.int32).T),
        "sg": f(np.repeat(np.asarray(inp["L1_ffn_gate_scales"], np.float32).T, 16, axis=0)),
        "pku": f(np.asarray(inp["L1_ffn_up_W_packed"], np.int32).T),
        "su": f(np.repeat(np.asarray(inp["L1_ffn_up_scales"], np.float32).T, 16, axis=0)),
        "pkd": f(np.asarray(inp["L1_ffn_down_W_packed"], np.int32).T),
        "sd": f(np.repeat(np.asarray(inp["L1_ffn_down_scales"], np.float32).T, 16, axis=0)),
    }
    xfull = f(x.reshape(B * S, H))
    for c in range(8):
        sub = np.concatenate([Wqkv[384 * c:384 * c + 384], Wqkv[384 * (c + 8):384 * (c + 8) + 384]], axis=0)
        m = dict(shared)
        m["x_full"] = xfull
        m["x_own"] = f(x[c])
        m["wqkv0T"] = f(sub.T)
        per_core.append(m)
    return per_core



IN_SPECS = {
    "x_full": ([B * S, H], F32), "x_own": ([S, H], F32),
    "wqkv0T": ([H, 768], F32), "wproj0T": ([H, H], F32), "bproj0": ([1, H], F32),
    "w0T": ([H, H], F32), "gatewT": ([H, 8], F32), "A2": ([H, 128], F32), "B2": ([128, H], F32),
    "wqkv1T": ([H, 3 * H], F32), "wo1T": ([H, H], F32),
    "pkg": ([H // 2, MLP], I32), "sg": ([H // 2, MLP], F32),
    "pku": ([H // 2, MLP], I32), "su": ([H // 2, MLP], F32),
    "pkd": ([MLP // 2, H], I32), "sd": ([MLP // 2, H], F32),
}
PHASE_IO = {
    "p1a": (["x_full", "wqkv0T"], [], ["ctxT0"]),
    "p1b": (["x_own", "wproj0T", "bproj0"], ["ctxT0"], ["x1"]),
    "p2": (["w0T", "gatewT", "A2", "B2"], ["x1"], ["x2"]),
    "p3a": (["wqkv1T"], ["x2"], ["qkvT1"]),
    "p3c": ([], ["qkvT1"], ["ctxT1"]),
    "p3d": (["wo1T"], ["ctxT1", "x2"], ["x3"]),
    "p4a": (["pkg", "sg", "pku", "su"], ["x3"], ["interT"]),
    "p4c": (["pkd", "sd"], [], ["wdT"]),
    "p4d": ([], ["interT", "wdT", "x3"], ["x4"]),
}
PHASE_FN = {}
SCRATCH_SPECS = {
    "ctxT0": ([H, S], BF16), "x1": ([S, H], F32), "x2": ([S, H], F32),
    "x3": ([S, H], F32), "x4": ([S, H], F32),
    "ctxT1": ([H, S], BF16), "xlnT3": ([H, S], BF16), "qkvT1": ([3 * H, S], BF16),
    "interT": ([MLP, S], BF16), "wdT": ([MLP, H], BF16),
}


def build_program(phase_names, out_name):
    nc = bass.Bass("TRN2", target_bir_lowering=False)
    d = {}
    needed_in, needed_scr = set(), set()
    for pn in phase_names:
        ins, rs, ws = PHASE_IO[pn]
        needed_in |= set(ins)
        needed_scr |= set(rs) | set(ws)
    produced = set()
    for pn in phase_names:
        produced |= set(PHASE_IO[pn][2])
    for n in sorted(needed_in):
        shp, dt = IN_SPECS[n]
        d[n] = nc.dram_tensor(n, shp, dt, kind="ExternalInput").ap()
    for n in sorted(needed_scr):
        shp, dt = SCRATCH_SPECS.get(n, ([S, H], F32))
        if n == out_name:
            kind = "ExternalOutput"
        elif n in produced:
            kind = "Internal"
        else:
            kind = "ExternalInput"
            needed_in.add(n)
        d[n] = nc.dram_tensor(n, shp, dt, kind=kind).ap()

    st = contextlib.ExitStack()
    with st:
        prog = Prog(nc)
        cx = Ctx(nc, prog, st)
        prog.barrier()
        for i, pn in enumerate(phase_names):
            PHASE_FN[pn](cx, d)
            if i < len(phase_names) - 1:
                prog.barrier()
        prog.check()
        prog.realize(st)
    return nc, sorted(needed_in)


def run_phases(per_core, phase_names, out_name):
    nc, in_names = build_program(phase_names, out_name)
    in_maps = [{k: m[k] for k in in_names} for m in per_core]
    res = run_bass_kernel_spmd(nc, in_maps, core_ids=list(range(8)))
    return [res.results[c][out_name] for c in range(8)]


ALL_PHASES = ["p1a", "p1b", "p2", "p3a", "p3c", "p3d", "p4a", "p4c", "p4d"]


def kernel(**inp):
    per_core = _prep_inputs(inp)
    out = run_phases(per_core, ALL_PHASES, "x4")
    return np.stack(out, axis=0)


PHASE_FN.update({"p1a": phase1_qkv_attn, "p1b": phase1_proj, "p2": phase2_ffn0,
                 "p3a": phase3_qkv, "p3c": phase3_attn, "p3d": phase3_proj,
                 "p4a": phase4_gateup, "p4c": phase4_wd, "p4d": phase4_down})


# version 12
# speedup vs baseline: 1.0210x; 1.0210x over previous
import contextlib
import os
import numpy as np
import ml_dtypes
import concourse.bass as bass
import concourse.mybir as mybir
from concourse.bass_utils import run_bass_kernel_spmd

F32 = mybir.dt.float32
BF16 = mybir.dt.bfloat16
I32 = mybir.dt.int32
AF = mybir.ActivationFunctionType
ALU = mybir.AluOpType
AXX = mybir.AxisListType.X

B, S, H = 8, 4096, 2048
NH, HD = 16, 128
MLP = 16384
NORM = 1.0 / np.sqrt(128.0)
EXP_SHIFT = -12.0
LN_EPS = 1e-5


ENGINES = ("pe", "act", "dve", "pool", "sp")
_BASS_ENG = {"pe": "tensor", "act": "scalar", "dve": "vector", "pool": "gpsimd", "sp": "sync"}


class Prog:

    NDMA = 8

    def __init__(self, nc):
        self.nc = nc
        self.items = {e: [] for e in ENGINES}
        self.ccount = {e: 0 for e in ENGINES}
        self.dcount = {e: 0 for e in ENGINES}
        self.dslot = {e: [0] * self.NDMA for e in ENGINES}
        self.regions = {}
        self.sem_names = ([f"cs_{e}" for e in ENGINES]
                          + [f"dq_{e}{j}" for e in ENGINES for j in range(self.NDMA)])

    def _deps(self, eng, reads, writes):
        waits = {}
        def add(tok):
            if tok is None:
                return
            s, v = tok
            if s == f"cs_{eng}" and eng == "pe":
                return
            if waits.get(s, 0) < v:
                waits[s] = v
        for k in reads:
            r = self.regions.get(k)
            if r:
                add(r["w"])
        for k in writes:
            r = self.regions.get(k)
            if r:
                add(r["w"])
                for s, v in r["r"].items():
                    add((s, v))
        return waits

    def _finish(self, eng, reads, writes, tok):
        s, v = tok
        for k in reads:
            r = self.regions.setdefault(k, {"w": None, "r": {}})["r"]
            if r.get(s, 0) < v:
                r[s] = v
        for k in writes:
            self.regions[k] = {"w": tok, "r": {}}

    def emit(self, eng, fn, reads=(), writes=(), dma=False):
        waits = self._deps(eng, reads, writes)
        if dma:
            slot = self.dcount[eng] % self.NDMA
            sem = f"dq_{eng}{slot}"
            prev = 16 * self.dslot[eng][slot]
            if prev and waits.get(sem, 0) < prev:
                waits[sem] = prev
            self.dcount[eng] += 1
            self.dslot[eng][slot] += 1
            inc = 16
            tok = (sem, 16 * self.dslot[eng][slot])
        else:
            self.ccount[eng] += 1
            sem, inc = f"cs_{eng}", 1
            tok = (sem, self.ccount[eng])
        self.items[eng].append((waits, fn, sem, inc))
        self._finish(eng, reads, writes, tok)
        return tok

    def barrier(self):
        arrive = {}
        for e in ENGINES:
            waits = {}
            for j in range(self.NDMA):
                if self.dslot[e][j]:
                    waits[f"dq_{e}{j}"] = 16 * self.dslot[e][j]
            self.ccount[e] += 1
            self.items[e].append((waits, lambda eng: eng.nop(nofuse=True), f"cs_{e}", 1))
            arrive[e] = self.ccount[e]
        for e in ENGINES:
            waits = {f"cs_{f}": arrive[f] for f in ENGINES if f != e}
            self.ccount[e] += 1
            self.items[e].append((waits, lambda eng: eng.nop(nofuse=True), f"cs_{e}", 1))
        bar = {e: (f"cs_{e}", self.ccount[e]) for e in ENGINES}
        for k in self.regions:
            w = self.regions[k]["w"]
            e = None
            if w is not None:
                s = w[0]
                e = s.split("_", 1)[1]
            self.regions[k] = {"w": bar.get(e) if e else None, "r": {}}

    def check(self):
        ptr = {e: 0 for e in ENGINES}
        sem = {n: 0 for n in self.sem_names}
        progress = True
        while progress:
            progress = False
            for e in ENGINES:
                while ptr[e] < len(self.items[e]):
                    waits, fn, s, inc = self.items[e][ptr[e]]
                    if any(sem[k] < v for k, v in waits.items()):
                        break
                    sem[s] += inc
                    ptr[e] += 1
                    progress = True
        stuck = {e: ptr[e] for e in ENGINES if ptr[e] < len(self.items[e])}
        if stuck:
            msgs = []
            for e, i in stuck.items():
                waits = self.items[e][i][0]
                bad = {k: (v, sem[k]) for k, v in waits.items() if sem[k] < v}
                msgs.append(f"{e}@{i}/{len(self.items[e])} waits {bad}")
            raise RuntimeError("DEADLOCK: " + "; ".join(msgs))

    def realize(self, st):
        nc = self.nc
        sems = {n: st.enter_context(nc.semaphore(n)) for n in self.sem_names}
        block = st.enter_context(nc.Block())
        clocks = {e: {} for e in ENGINES}
        waited = {e: set() for e in ENGINES}
        for f in ENGINES:
            for waits, _, _, _ in self.items[f]:
                for s, v in waits.items():
                    if s.startswith("cs_"):
                        waited[s[3:]].add(v)

        def run_engine(e):
            def body(engobj):
                clock = clocks[e]
                cnt = 0
                for waits, fn, sem, inc in self.items[e]:
                    n_emitted = 0
                    for s in sorted(waits):
                        v = waits[s]
                        if clock.get(s, 0) < v:
                            engobj.wait_ge(sems[s], v)
                            clock[s] = v
                            n_emitted += 1
                    if sem == f"cs_{e}":
                        cnt += 1
                        if (n_emitted == 0 and (cnt - 1) in waited[e]
                                and cnt - 1 > clock.get(sem, 0)):
                            engobj.wait_ge(sems[sem], cnt - 1)
                            clock[sem] = cnt - 1
                    inst = fn(engobj)
                    inst.then_inc(sems[sem], inc)
                for j in range(self.NDMA):
                    if self.dslot[e][j]:
                        engobj.wait_ge(sems[f"dq_{e}{j}"], 16 * self.dslot[e][j])
            return body

        for e in ENGINES:
            getattr(block, _BASS_ENG[e])(run_engine(e))



class Ctx:
    def __init__(self, nc, prog, st):
        self.nc, self.p, self.st = nc, prog, st
        ident_np = np.eye(128, dtype=ml_dtypes.bfloat16)
        self.ident = nc.inline_tensor(ident_np, "ident128")
        self.ident_sb = st.enter_context(nc.sbuf_tensor("ident_sb", [128, 128], BF16))
        self.consts = st.enter_context(nc.sbuf_tensor("consts", [128, 8], F32))
        p = prog
        p.emit("sp", lambda e: e.dma_start(out=self.ident_sb[:], in_=self.ident.ap()),
               writes=["ident_sb"], dma=True)
        p.emit("dve", lambda e: e.memset(self.consts[:, 0:1], LN_EPS), writes=["consts"])
        p.emit("dve", lambda e: e.memset(self.consts[:, 1:2], EXP_SHIFT), writes=["consts"])
        p.emit("dve", lambda e: e.memset(self.consts[:, 2:3], 1.0), writes=["consts"])
        p.emit("dve", lambda e: e.memset(self.consts[:, 3:4], -1e30), writes=["consts"])
        p.emit("dve", lambda e: e.memset(self.consts[:, 4:5], 0.5), writes=["consts"])
        p.emit("dve", lambda e: e.memset(self.consts[:, 5:6], 0.7978845608028654), writes=["consts"])
        nc.const_aps.register(F32, 0.5, self.consts[:, 4:5])
        nc.const_aps.register(F32, 0.7978845608028654, self.consts[:, 5:6])
        p.emit("dve", lambda e: e.memset(self.consts[:, 6:7], 0.7978845608028654 * 0.044715), writes=["consts"])
        nc.const_aps.register(F32, 0.7978845608028654 * 0.044715, self.consts[:, 6:7])

    @property
    def eps(self):
        return self.consts[:, 0:1]

    @property
    def eshift(self):
        return self.consts[:, 1:2]

    @property
    def half(self):
        return self.consts[:, 4:5]


def emit_ln_chunk(cx, key_in, xin_ap, xbf, xbf_key, stats, stats_key, trash, ntile=4, hdim=H):
    p = cx.p
    n = ntile
    c = 1.0 / hdim
    for j in range(n):
        tile = xin_ap[:, j * hdim:(j + 1) * hdim]
        p.emit("dve", lambda e, t=tile, j=j: e.reduce_sum(stats[:, j:j + 1], t, axis=AXX),
               reads=[key_in], writes=[stats_key])
        p.emit("act", lambda e, t=tile, j=j: e.activation(
            trash[:, 0:hdim], t, AF.Square, accum_out=stats[:, n + j:n + j + 1]),
            reads=[key_in], writes=[stats_key, "trash"])
    p.emit("dve", lambda e: e.tensor_scalar(stats[:, 2 * n:3 * n], stats[:, 0:n], c, None, ALU.mult),
           reads=[stats_key], writes=[stats_key])
    p.emit("dve", lambda e: e.tensor_scalar(stats[:, 3 * n:4 * n], stats[:, n:2 * n], c, None, ALU.mult),
           reads=[stats_key], writes=[stats_key])
    p.emit("dve", lambda e: e.tensor_mul(stats[:, 0:n], stats[:, 2 * n:3 * n], stats[:, 2 * n:3 * n]),
           reads=[stats_key], writes=[stats_key])
    p.emit("dve", lambda e: e.tensor_sub(stats[:, 3 * n:4 * n], stats[:, 3 * n:4 * n], stats[:, 0:n]),
           reads=[stats_key], writes=[stats_key])
    p.emit("act", lambda e: e.activation(stats[:, 3 * n:4 * n], stats[:, 3 * n:4 * n], AF.Ln, bias=cx.eps),
           reads=[stats_key, "consts"], writes=[stats_key])
    p.emit("act", lambda e: e.activation(stats[:, 3 * n:4 * n], stats[:, 3 * n:4 * n], AF.Exp, scale=-0.5),
           reads=[stats_key], writes=[stats_key])
    for j in range(n):
        tile = xin_ap[:, j * hdim:(j + 1) * hdim]
        p.emit("dve", lambda e, t=tile, j=j: e.tensor_scalar(
            xbf[:, j * hdim:(j + 1) * hdim], t, stats[:, 2 * n + j:2 * n + j + 1],
            stats[:, 3 * n + j:3 * n + j + 1], ALU.subtract, ALU.mult),
            reads=[key_in, stats_key], writes=[xbf_key])


def emit_transpose_chunk(cx, xbf, xbf_key, xlnT, xlnT_key, psbf, psbf_key, ntile=4, nht=16):
    p = cx.p
    tw = ntile * 128
    for ht in range(nht):
        for j in range(ntile):
            src = xbf[:, j * (nht * 128) + ht * 128: j * (nht * 128) + ht * 128 + 128]
            dst = psbf[:, j * 128:(j + 1) * 128]
            p.emit("pe", lambda e, s=src, d=dst: e.transpose(d, s, cx.ident_sb[:]),
                   reads=[xbf_key, "ident_sb"], writes=[psbf_key])
        p.emit("dve", lambda e, ht=ht: e.tensor_copy(xlnT[:, ht * tw:(ht + 1) * tw], psbf[:, 0:tw]),
               reads=[psbf_key], writes=[xlnT_key])



def phase1_qkv_attn(cx, d):
    nc, p, st = cx.nc, cx.p, cx.st
    with contextlib.ExitStack() as sst:
        xin = [sst.enter_context(nc.sbuf_tensor(f"p1_xin{i}", [128, 4 * H], F32)) for i in range(2)]
        xbf = sst.enter_context(nc.sbuf_tensor("p1_xbf", [128, 4 * H], BF16))
        xlnT = [sst.enter_context(nc.sbuf_tensor("p1_xlnT0", [128, 16 * 512], BF16))] * 2
        stats = sst.enter_context(nc.sbuf_tensor("p1_stats", [128, 16], F32))
        trash = sst.enter_context(nc.sbuf_tensor("p1_trash", [128, H], BF16))
        wsb = sst.enter_context(nc.sbuf_tensor("p1_wsb", [128, 16 * 768], BF16))
        qT = [sst.enter_context(nc.sbuf_tensor(f"p1_qT{i}", [128, 4096], BF16)) for i in range(2)]
        kT = [sst.enter_context(nc.sbuf_tensor(f"p1_kT{i}", [128, 4096], BF16)) for i in range(2)]
        vE = [sst.enter_context(nc.sbuf_tensor(f"p1_vE{i}", [128, 32 * 132], BF16)) for i in range(2)]
        vtmp = sst.enter_context(nc.sbuf_tensor("p1_vtmp", [128, 512], BF16))
        expT = sst.enter_context(nc.sbuf_tensor("p1_expT", [128, 32 * 512], BF16))
        ctmp = sst.enter_context(nc.sbuf_tensor("p1_ctmp", [128, 128], BF16))
        cTsb = sst.enter_context(nc.sbuf_tensor("p1_cTsb", [128, 512], BF16))
        rs1 = sst.enter_context(nc.sbuf_tensor("p1_rs1", [128, 1], F32))
        psf = [sst.enter_context(nc.psum_tensor(f"p1_psf{i}", [128, 512], F32)) for i in range(6)]
        psb = [sst.enter_context(nc.psum_tensor(f"p1_psb{i}", [128, 1024], BF16)) for i in range(2)]

        p.emit("pool", lambda e: e.dma_start(
            out=wsb[:].rearrange("p (ht dd) -> p ht dd", dd=768),
            in_=d["wqkv0T"].rearrange("(ht p) dd -> p ht dd", p=128)),
            writes=["p1_wsb"], dma=True)
        for i in range(2):
            p.emit("dve", lambda e, i=i: e.memset(
                vE[i][:].rearrange("p (kt c) -> p kt c", c=132)[:, :, 128:129], 1.0),
                writes=[f"p1_vE{i}_ones"])

        for m in range(B):
            for tc in range(8):
                bi = tc % 2
                xink, xlnk = f"p1_xin{bi}", "p1_xlnT0"
                src = d["x_full"][(m * S + tc * 512):(m * S + (tc + 1) * 512), :]
                p.emit("sp", lambda e, s=src, bi=bi: e.dma_start(
                    out=xin[bi][:].rearrange("p (j h) -> p j h", h=H),
                    in_=s.rearrange("(j p) h -> p j h", p=128)),
                    writes=[xink], dma=True)
                emit_ln_chunk(cx, xink, xin[bi][:], xbf, "p1_xbf", stats, "p1_stats", trash)
                emit_transpose_chunk(cx, xbf, "p1_xbf", xlnT[bi], xlnk, psb[tc % 2], f"p1_psb{tc % 2}")
                for dt in range(6):
                    bank = psf[dt % 2]
                    bkey = f"p1_psf{dt % 2}"
                    for ht in range(16):
                        p.emit("pe", lambda e, dt=dt, ht=ht, bank=bank, bi=bi: e.matmul(
                            bank[:], wsb[:, ht * 768 + dt * 128: ht * 768 + (dt + 1) * 128],
                            xlnT[bi][:, ht * 512:(ht + 1) * 512],
                            start=(ht == 0), stop=(ht == 15)),
                            reads=["p1_wsb", xlnk], writes=[bkey])
                    hp = dt // 3
                    kind = dt % 3
                    if kind == 0:
                        p.emit("dve", lambda e, hp=hp, bank=bank, tc=tc: e.tensor_scalar(
                            qT[hp][:, tc * 512:(tc + 1) * 512], bank[:], NORM, None, ALU.mult),
                            reads=[bkey], writes=[f"p1_qT{hp}"])
                    elif kind == 1:
                        p.emit("dve", lambda e, hp=hp, bank=bank, tc=tc: e.tensor_copy(
                            kT[hp][:, tc * 512:(tc + 1) * 512], bank[:]),
                            reads=[bkey], writes=[f"p1_kT{hp}"])
                    else:
                        p.emit("dve", lambda e, bank=bank: e.tensor_copy(vtmp[:], bank[:]),
                               reads=[bkey], writes=["p1_vtmp"])
                        for q4 in range(4):
                            p.emit("pe", lambda e, q4=q4: e.transpose(
                                psb[1][:, q4 * 128:(q4 + 1) * 128], vtmp[:, q4 * 128:(q4 + 1) * 128],
                                cx.ident_sb[:]),
                                reads=["p1_vtmp", "ident_sb"], writes=["p1_psb1"])
                        p.emit("dve", lambda e, hp=hp, tc=tc: e.tensor_copy(
                            vE[hp][:].rearrange("p (kt c) -> p kt c", c=132)[:, tc * 4:(tc + 1) * 4, 0:128],
                            psb[1][:].rearrange("p (q4 c) -> p q4 c", c=128)[:, 0:4, :]),
                            reads=["p1_psb1"], writes=[f"p1_vE{hp}"])
            for hp in range(2):
                row0 = 256 * m + 128 * hp
                for qc in range(8):
                    for kt in range(32):
                        bank = psf[2 + kt % 2]
                        bkey = f"p1_psf{2 + kt % 2}"
                        p.emit("pe", lambda e, hp=hp, kt=kt, qc=qc, bank=bank: e.matmul(
                            bank[:], kT[hp][:, kt * 128:(kt + 1) * 128],
                            qT[hp][:, qc * 512:(qc + 1) * 512], start=True, stop=True),
                            reads=[f"p1_kT{hp}", f"p1_qT{hp}"], writes=[bkey])
                        p.emit("act", lambda e, kt=kt, bank=bank: e.activation(
                            expT[:, kt * 512:(kt + 1) * 512], bank[:], AF.Exp, bias=cx.eshift),
                            reads=[bkey, "consts"], writes=[f"p1_expT{kt}"])
                    for qs in range(4):
                        bank = psf[4 + qs % 2]
                        bkey = f"p1_psf{4 + qs % 2}"
                        for kt in range(32):
                            p.emit("pe", lambda e, hp=hp, kt=kt, qs=qs, bank=bank: e.matmul(
                                bank[:, 0:129], expT[:, kt * 512 + qs * 128: kt * 512 + qs * 128 + 128],
                                vE[hp][:, kt * 132: kt * 132 + 129],
                                start=(kt == 0), stop=(kt == 31)),
                                reads=[f"p1_expT{kt}", f"p1_vE{hp}", f"p1_vE{hp}_ones"], writes=[bkey])
                        p.emit("act", lambda e, bank=bank: e.activation(rs1[:], bank[:, 128:129], AF.Ln),
                               reads=[bkey], writes=["p1_rs1"])
                        p.emit("act", lambda e: e.activation(rs1[:], rs1[:], AF.Exp, scale=-1.0),
                               reads=["p1_rs1"], writes=["p1_rs1"])
                        p.emit("dve", lambda e, bank=bank: e.tensor_scalar(
                            ctmp[:], bank[:, 0:128], rs1[:], None, ALU.mult),
                            reads=[bkey, "p1_rs1"], writes=["p1_ctmp"])
                        p.emit("pe", lambda e, qs=qs: e.transpose(
                            psb[0][:, qs * 128:(qs + 1) * 128], ctmp[:], cx.ident_sb[:]),
                            reads=["p1_ctmp", "ident_sb"], writes=["p1_psb0"])
                        p.emit("dve", lambda e, qs=qs: e.tensor_copy(
                            cTsb[:, qs * 128:(qs + 1) * 128], psb[0][:, qs * 128:(qs + 1) * 128]),
                            reads=["p1_psb0"], writes=["p1_cTsb"])
                    p.emit("act", lambda e, row0=row0, qc=qc: e.dma_start(
                        out=d["ctxT0"][row0:row0 + 128, qc * 512:(qc + 1) * 512], in_=cTsb[:]),
                        reads=["p1_cTsb"], writes=[f"ctxT0_{row0}_{qc}"], dma=True)


def phase1_proj(cx, d):
    nc, p = cx.nc, cx.p
    with contextlib.ExitStack() as sst:
        wsb = sst.enter_context(nc.sbuf_tensor("p1b_w", [128, 16 * 2048], BF16))
        bbc = sst.enter_context(nc.sbuf_tensor("p1b_bbc", [128, 2048], F32))
        cT = [sst.enter_context(nc.sbuf_tensor(f"p1b_cT{i}", [128, 16 * 512], BF16)) for i in range(2)]
        xo = [sst.enter_context(nc.sbuf_tensor(f"p1b_xo{i}", [128, 4 * 2048], F32)) for i in range(2)]
        x1 = [sst.enter_context(nc.sbuf_tensor("p1b_x10", [128, 4 * 2048], F32))] * 2
        psf = [sst.enter_context(nc.psum_tensor(f"p1b_ps{i}", [128, 512], F32)) for i in range(4)]

        for i in range(4):
            p.emit("pool", lambda e, i=i: e.dma_start(
                out=wsb[:, i * 4 * 2048:(i + 1) * 4 * 2048].rearrange("p (ht o) -> p ht o", o=2048),
                in_=d["wproj0T"][i * 512:(i + 1) * 512, :].rearrange("(ht p) o -> p ht o", p=128)),
                writes=["p1b_w"], dma=True)
        p.emit("sp", lambda e: e.dma_start(
            out=bbc[:], in_=d["bproj0"].partition_broadcast(128)),
            writes=["p1b_bbc"], dma=True)

        for tc in range(8):
            bi = tc % 2
            ck, xok, x1k = f"p1b_cT{bi}", f"p1b_xo{bi}", "p1b_x10"
            p.emit("sp", lambda e, tc=tc, bi=bi: e.dma_start(
                out=cT[bi][:].rearrange("p (ht t) -> p ht t", t=512),
                in_=d["ctxT0"][:, tc * 512:(tc + 1) * 512].rearrange("(ht p) t -> p ht t", p=128)),
                reads=[f"ctxT0_all"], writes=[ck], dma=True)
            p.emit("sp", lambda e, tc=tc, bi=bi: e.dma_start(
                out=xo[bi][:].rearrange("p (j h) -> p j h", h=2048),
                in_=d["x_own"][tc * 512:(tc + 1) * 512, :].rearrange("(j p) h -> p j h", p=128)),
                writes=[xok], dma=True)
            for j in range(4):
                p.emit("dve", lambda e, j=j, bi=bi: e.tensor_add(
                    xo[bi][:, j * 2048:(j + 1) * 2048], xo[bi][:, j * 2048:(j + 1) * 2048], bbc[:]),
                    reads=[xok, "p1b_bbc"], writes=[xok])
            for ts in range(4):
                for oc in range(4):
                    bank = psf[oc]
                    bkey = f"p1b_ps{oc}"
                    for kt in range(16):
                        p.emit("pe", lambda e, ts=ts, oc=oc, kt=kt, bank=bank, bi=bi: e.matmul(
                            bank[:], cT[bi][:, kt * 512 + ts * 128: kt * 512 + ts * 128 + 128],
                            wsb[:, kt * 2048 + oc * 512: kt * 2048 + (oc + 1) * 512],
                            start=(kt == 0), stop=(kt == 15)),
                            reads=[ck, "p1b_w"], writes=[bkey])
                    p.emit("dve", lambda e, ts=ts, oc=oc, bank=bank, bi=bi: e.tensor_add(
                        x1[bi][:, ts * 2048 + oc * 512: ts * 2048 + (oc + 1) * 512],
                        bank[:], xo[bi][:, ts * 2048 + oc * 512: ts * 2048 + (oc + 1) * 512]),
                        reads=[f"p1b_ps{oc}", xok], writes=[x1k])
            p.emit("act", lambda e, tc=tc, bi=bi: e.dma_start(
                out=d["x1"][tc * 512:(tc + 1) * 512, :].rearrange("(j p) h -> p j h", p=128),
                in_=x1[bi][:].rearrange("p (j h) -> p j h", h=2048)),
                reads=[x1k], writes=[f"x1_{tc}"], dma=True)



def phase2_ffn0(cx, d):
    lvl = int(os.environ.get("P2LVL", "4"))
    nc, p = cx.nc, cx.p
    with contextlib.ExitStack() as sst:
        w0sb = sst.enter_context(nc.sbuf_tensor("p2_w0", [128, 16 * 2048], BF16))
        a2sb = sst.enter_context(nc.sbuf_tensor("p2_a2", [128, 16 * 128], BF16))
        b2sb = sst.enter_context(nc.sbuf_tensor("p2_b2", [128, 2048], BF16))
        gwsb = sst.enter_context(nc.sbuf_tensor("p2_gw", [128, 16 * 8], BF16))
        xin = [sst.enter_context(nc.sbuf_tensor(f"p2_xin{i}", [128, 4 * 2048], F32)) for i in range(2)]
        xbf = sst.enter_context(nc.sbuf_tensor("p2_xbf", [128, 4 * 2048], BF16))
        xlnT = sst.enter_context(nc.sbuf_tensor("p2_xlnT", [128, 16 * 512], BF16))
        x2b = sst.enter_context(nc.sbuf_tensor("p2_x2", [128, 4 * 2048], F32))
        stats = sst.enter_context(nc.sbuf_tensor("p2_stats", [128, 16], F32))
        trash = sst.enter_context(nc.sbuf_tensor("p2_trash", [128, H], BF16))
        gl = sst.enter_context(nc.sbuf_tensor("p2_gl", [128, 8], F32))
        el = sst.enter_context(nc.sbuf_tensor("p2_el", [128, 8], F32))
        sel = sst.enter_context(nc.sbuf_tensor("p2_sel", [128, 8], F32))
        msk = sst.enter_context(nc.sbuf_tensor("p2_msk", [128, 8], F32))
        cw = sst.enter_context(nc.sbuf_tensor("p2_cw", [128, 8], F32))
        mm = sst.enter_context(nc.sbuf_tensor("p2_mm", [128, 4], F32))
        tmw = sst.enter_context(nc.sbuf_tensor("p2_tmw", [128, 128], BF16))
        tcwT = sst.enter_context(nc.sbuf_tensor("p2_tcwT", [128, 4 * 128], BF16))
        psf = [sst.enter_context(nc.psum_tensor(f"p2_psf{i}", [128, 512], F32)) for i in range(6)]
        psb = [sst.enter_context(nc.psum_tensor(f"p2_psb{i}", [128, 1024], BF16)) for i in range(2)]

        for i in range(4):
            p.emit("pool", lambda e, i=i: e.dma_start(
                out=w0sb[:, i * 4 * 2048:(i + 1) * 4 * 2048].rearrange("p (ht o) -> p ht o", o=2048),
                in_=d["w0T"][i * 512:(i + 1) * 512, :].rearrange("(ht p) o -> p ht o", p=128)),
                writes=["p2_w0"], dma=True)
        p.emit("pool", lambda e: e.dma_start(
            out=a2sb[:].rearrange("p (ht c) -> p ht c", c=128),
            in_=d["A2"].rearrange("(ht p) c -> p ht c", p=128)),
            writes=["p2_a2"], dma=True)
        p.emit("pool", lambda e: e.dma_start(out=b2sb[:], in_=d["B2"]),
               writes=["p2_b2"], dma=True)
        p.emit("pool", lambda e: e.dma_start(
            out=gwsb[:].rearrange("p (ht c) -> p ht c", c=8),
            in_=d["gatewT"].rearrange("(ht p) c -> p ht c", p=128)),
            writes=["p2_gw"], dma=True)

        for tc in range(int(os.environ.get("P2TC", "8"))):
            bi = tc % 2
            xink = f"p2_xin{bi}"
            p.emit("sp", lambda e, tc=tc, bi=bi: e.dma_start(
                out=xin[bi][:].rearrange("p (j h) -> p j h", h=H),
                in_=d["x1"][tc * 512:(tc + 1) * 512, :].rearrange("(j p) h -> p j h", p=128)),
                writes=[xink], dma=True)
            emit_ln_chunk(cx, xink, xin[bi][:], xbf, "p2_xbf", stats, "p2_stats", trash)
            emit_transpose_chunk(cx, xbf, "p2_xbf", xlnT, "p2_xlnT", psb[0], "p2_psb0")
            for ts in range(4):
              if lvl >= 3:
                bank, bkey = psf[4], "p2_psf4"
                for kt in range(16):
                    p.emit("pe", lambda e, ts=ts, kt=kt, bank=bank: e.matmul(
                        bank[:, 0:8], xlnT[:, kt * 512 + ts * 128: kt * 512 + ts * 128 + 128],
                        gwsb[:, kt * 8:(kt + 1) * 8], start=(kt == 0), stop=(kt == 15)),
                        reads=["p2_xlnT", "p2_gw"], writes=[bkey])
                p.emit("dve", lambda e, bank=bank: e.tensor_copy(gl[:], bank[:, 0:8]),
                       reads=[bkey], writes=["p2_gl"])
                p.emit("dve", lambda e: e.reduce_max(mm[:, 0:1], gl[:], axis=AXX, negate=True),
                       reads=["p2_gl"], writes=["p2_mm"])
                p.emit("act", lambda e: e.activation(el[:], gl[:], AF.Exp, bias=mm[:, 0:1]),
                       reads=["p2_gl", "p2_mm"], writes=["p2_el"])
                p.emit("dve", lambda e: e.reduce_max(mm[:, 0:1], el[:], axis=AXX),
                       reads=["p2_el"], writes=["p2_mm"])
                p.emit("dve", lambda e: e.tensor_scalar(msk[:], el[:], mm[:, 0:1], None, ALU.is_equal),
                       reads=["p2_el", "p2_mm"], writes=["p2_msk"])
                p.emit("dve", lambda e: e.scalar_tensor_tensor(sel[:], msk[:], -2.0, el[:], ALU.mult, ALU.add),
                       reads=["p2_msk", "p2_el"], writes=["p2_sel"])
                p.emit("dve", lambda e: e.reduce_max(mm[:, 1:2], sel[:], axis=AXX),
                       reads=["p2_sel"], writes=["p2_mm"])
                p.emit("dve", lambda e: e.tensor_add(mm[:, 2:3], mm[:, 0:1], mm[:, 1:2]),
                       reads=["p2_mm"], writes=["p2_mm"])
                p.emit("act", lambda e: e.activation(mm[:, 3:4], mm[:, 2:3], AF.Ln),
                       reads=["p2_mm"], writes=["p2_mm"])
                p.emit("act", lambda e: e.activation(mm[:, 3:4], mm[:, 3:4], AF.Exp, scale=-1.0),
                       reads=["p2_mm"], writes=["p2_mm"])
                p.emit("dve", lambda e: e.tensor_scalar(msk[:], el[:], mm[:, 1:2], None, ALU.is_ge),
                       reads=["p2_el", "p2_mm"], writes=["p2_msk"])
                p.emit("dve", lambda e: e.tensor_mul(sel[:], el[:], msk[:]),
                       reads=["p2_el", "p2_msk"], writes=["p2_sel"])
                p.emit("dve", lambda e: e.tensor_scalar(cw[:], sel[:], mm[:, 3:4], 16.0, ALU.mult, ALU.mult),
                       reads=["p2_sel", "p2_mm"], writes=["p2_cw"])
              if lvl >= 4:
                bank2, bkey2 = psf[5], "p2_psf5"
                for kt in range(16):
                    p.emit("pe", lambda e, ts=ts, kt=kt, bank=bank2: e.matmul(
                        bank[:, 0:128],
                        a2sb[:, kt * 128:(kt + 1) * 128],
                        xlnT[:, kt * 512 + ts * 128: kt * 512 + ts * 128 + 128],
                        start=(kt == 0), stop=(kt == 15)),
                        reads=["p2_a2", "p2_xlnT"], writes=[bkey2])
                p.emit("act", lambda e, bank=bank2: e.copy(tmw[:], bank[:, 0:128]),
                       reads=[bkey2], writes=["p2_tmw"])
                p.emit("pe", lambda e: e.transpose(psb[1][:, 0:128], tmw[:], cx.ident_sb[:]),
                       reads=["p2_tmw", "ident_sb"], writes=["p2_psb1"])
                p.emit("dve", lambda e: e.tensor_copy(tmw[:], psb[1][:, 0:128]),
                       reads=["p2_psb1"], writes=["p2_tmw"])
                for ex in range(8):
                    p.emit("dve", lambda e, ex=ex: e.tensor_scalar(
                        tmw[:, ex * 16:(ex + 1) * 16], tmw[:, ex * 16:(ex + 1) * 16],
                        cw[:, ex:ex + 1], None, ALU.mult),
                        reads=["p2_tmw", "p2_cw"], writes=["p2_tmw"])
                p.emit("pe", lambda e: e.transpose(psb[1][:, 128:256], tmw[:], cx.ident_sb[:]),
                       reads=["p2_tmw", "ident_sb"], writes=["p2_psb1"])
                p.emit("dve", lambda e, ts=ts: e.tensor_copy(tcwT[:, ts * 128:(ts + 1) * 128], psb[1][:, 128:256]),
                       reads=["p2_psb1"], writes=["p2_tcwT"])
              if lvl >= 2:
                for oc in range(4):
                    bank3 = psf[oc]
                    bkey3 = f"p2_psf{oc}"
                    for kt in range(16 if lvl >= 4 else 15):
                        p.emit("pe", lambda e, ts=ts, oc=oc, kt=kt, bank=bank3: e.matmul(
                            bank[:],
                            xlnT[:, kt * 512 + ts * 128: kt * 512 + ts * 128 + 128],
                            w0sb[:, kt * 2048 + oc * 512: kt * 2048 + (oc + 1) * 512],
                            start=(kt == 0), stop=False),
                            reads=["p2_xlnT", "p2_w0"], writes=[bkey3])
                    if lvl >= 4:
                        p.emit("pe", lambda e, ts=ts, oc=oc, bank=bank3: e.matmul(
                            bank[:], tcwT[:, ts * 128:(ts + 1) * 128],
                            b2sb[:, oc * 512:(oc + 1) * 512], start=False, stop=True),
                            reads=["p2_tcwT", "p2_b2"], writes=[bkey3])
                    else:
                        p.emit("pe", lambda e, ts=ts, oc=oc, kt=15, bank=bank3: e.matmul(
                            bank[:],
                            xlnT[:, kt * 512 + ts * 128: kt * 512 + ts * 128 + 128],
                            w0sb[:, kt * 2048 + oc * 512: kt * 2048 + (oc + 1) * 512],
                            start=False, stop=True),
                            reads=["p2_xlnT", "p2_w0"], writes=[bkey3])
                    ev = int(os.environ.get("P2EVAC", "1"))
                    if (lvl >= 3 or ev == 1):
                        p.emit("dve", lambda e, ts=ts, oc=oc, bank=bank3, bi=bi: e.tensor_add(
                            x2b[:, ts * 2048 + oc * 512: ts * 2048 + (oc + 1) * 512],
                            bank[:], xin[bi][:, ts * 2048 + oc * 512: ts * 2048 + (oc + 1) * 512]),
                            reads=[bkey3, xink], writes=["p2_x2"])
                    elif ev == 2:
                        p.emit("dve", lambda e, ts=ts, oc=oc, bank=bank3: e.tensor_copy(
                            x2b[:, ts * 2048 + oc * 512: ts * 2048 + (oc + 1) * 512],
                            bank[:]),
                            reads=[bkey3], writes=["p2_x2"])
                    else:
                        p.emit("dve", lambda e, ts=ts, oc=oc, bank=bank3, bi=bi: e.tensor_copy(
                            x2b[:, ts * 2048 + oc * 512: ts * 2048 + (oc + 1) * 512],
                            xin[bi][:, ts * 2048 + oc * 512: ts * 2048 + (oc + 1) * 512]),
                            reads=[bkey3, xink], writes=["p2_x2"])
              if lvl < 2:
                for j in range(4):
                    p.emit("dve", lambda e, j=j, bi=bi: e.tensor_copy(
                        x2b[:, j * 2048:(j + 1) * 2048], xin[bi][:, j * 2048:(j + 1) * 2048]),
                        reads=[xink], writes=["p2_x2"])
            p.emit("act", lambda e, tc=tc: e.dma_start(
                out=d["x2"][tc * 512:(tc + 1) * 512, :].rearrange("(j p) h -> p j h", p=128),
                in_=x2b[:].rearrange("p (j h) -> p j h", h=H)),
                reads=["p2_x2"], writes=[f"x2_{tc}"], dma=True)



def phase3_qkv(cx, d):
    nc, p = cx.nc, cx.p
    with contextlib.ExitStack() as sst:
        xlnF = sst.enter_context(nc.sbuf_tensor("p3_xlnF", [128, 16 * 4096], BF16))
        xin = [sst.enter_context(nc.sbuf_tensor("p3_xin0", [128, 4 * 2048], F32))] * 2
        xbf = sst.enter_context(nc.sbuf_tensor("p3_xbf", [128, 4 * 2048], BF16))
        stats = sst.enter_context(nc.sbuf_tensor("p3_stats", [128, 16], F32))
        trash = sst.enter_context(nc.sbuf_tensor("p3_trash", [128, H], BF16))
        wbuf = [sst.enter_context(nc.sbuf_tensor(f"p3_w{i}", [128, 16 * 128], BF16)) for i in range(2)]
        qbuf = [sst.enter_context(nc.sbuf_tensor(f"p3_q{i}", [128, 512], BF16)) for i in range(2)]
        psf = [sst.enter_context(nc.psum_tensor(f"p3_psf{i}", [128, 512], F32)) for i in range(2)]
        psb = [sst.enter_context(nc.psum_tensor(f"p3_psb{i}", [128, 1024], BF16)) for i in range(2)]

        for tc in range(8):
            bi = 0
            xink = "p3_xin0"
            p.emit("sp", lambda e, tc=tc, bi=bi: e.dma_start(
                out=xin[bi][:].rearrange("p (j h) -> p j h", h=H),
                in_=d["x2"][tc * 512:(tc + 1) * 512, :].rearrange("(j p) h -> p j h", p=128)),
                writes=[xink], dma=True)
            emit_ln_chunk(cx, xink, xin[bi][:], xbf, "p3_xbf", stats, "p3_stats", trash)
            for ht in range(16):
                pk = f"p3_psb{ht % 2}"
                for j in range(4):
                    src = xbf[:, j * 2048 + ht * 128: j * 2048 + ht * 128 + 128]
                    p.emit("pe", lambda e, s=src, j=j, ht=ht: e.transpose(
                        psb[ht % 2][:, j * 128:(j + 1) * 128], s, cx.ident_sb[:]),
                        reads=["p3_xbf", "ident_sb"], writes=[pk])
                p.emit("dve", lambda e, ht=ht, tc=tc: e.tensor_copy(
                    xlnF[:, ht * 4096 + tc * 512: ht * 4096 + tc * 512 + 512], psb[ht % 2][:, 0:512]),
                    reads=[pk], writes=["p3_xlnF"])
        for dt in range(48):
            wb = dt % 2
            wk = f"p3_w{wb}"
            p.emit("pool", lambda e, dt=dt, wb=wb: e.dma_start(
                out=wbuf[wb][:].rearrange("p (ht c) -> p ht c", c=128),
                in_=d["wqkv1T"][:, dt * 128:(dt + 1) * 128].rearrange("(ht p) c -> p ht c", p=128)),
                writes=[wk], dma=True)
            for tg in range(8):
                bank, bkey = psf[tg % 2], f"p3_psf{tg % 2}"
                for kt in range(16):
                    p.emit("pe", lambda e, dt=dt, tg=tg, kt=kt, bank=bank, wb=wb: e.matmul(
                        bank[:], wbuf[wb][:, kt * 128:(kt + 1) * 128],
                        xlnF[:, kt * 4096 + tg * 512: kt * 4096 + (tg) * 512 + 512],
                        start=(kt == 0), stop=(kt == 15)),
                        reads=[wk, "p3_xlnF"], writes=[bkey])
                qb = tg % 2
                qk = f"p3_q{qb}"
                if dt < 16:
                    p.emit("dve", lambda e, bank=bank, qb=qb: e.tensor_scalar(
                        qbuf[qb][:], bank[:], NORM, None, ALU.mult), reads=[bkey], writes=[qk])
                else:
                    p.emit("dve", lambda e, bank=bank, qb=qb: e.tensor_copy(
                        qbuf[qb][:], bank[:]), reads=[bkey], writes=[qk])
                p.emit("act", lambda e, dt=dt, tg=tg, qb=qb: e.dma_start(
                    out=d["qkvT1"][dt * 128:(dt + 1) * 128, tg * 512:(tg + 1) * 512], in_=qbuf[qb][:]),
                    reads=[qk], writes=[f"qkvT1_{dt}_{tg}"], dma=True)


_CAUSAL_MASK = None


def _causal_mask_np():
    global _CAUSAL_MASK
    if _CAUSAL_MASK is None:
        m = np.zeros((128, 4 * 512), np.float32)
        for dlt in range(4):
            kj = np.arange(128)[:, None]
            qj = np.arange(512)[None, :]
            m[:, dlt * 512:(dlt + 1) * 512] = np.where(128 * dlt + kj <= qj, 0.0, -1e30)
        _CAUSAL_MASK = m
    return _CAUSAL_MASK


def phase3_attn(cx, d):
    nc, p = cx.nc, cx.p
    with contextlib.ExitStack() as sst:
        qT = sst.enter_context(nc.sbuf_tensor("p3c_qT", [128, 4096], BF16))
        kT = sst.enter_context(nc.sbuf_tensor("p3c_kT", [128, 4096], BF16))
        vT = sst.enter_context(nc.sbuf_tensor("p3c_vT", [128, 4096], BF16))
        vE = sst.enter_context(nc.sbuf_tensor("p3c_vE", [128, 32 * 132], BF16))
        expT = sst.enter_context(nc.sbuf_tensor("p3c_expT", [128, 32 * 512], BF16))
        msksb = sst.enter_context(nc.sbuf_tensor("p3c_msk", [128, 4 * 512], F32))
        ctmp = sst.enter_context(nc.sbuf_tensor("p3c_ctmp", [128, 128], BF16))
        cTsb = sst.enter_context(nc.sbuf_tensor("p3c_cTsb", [128, 512], BF16))
        rs1 = sst.enter_context(nc.sbuf_tensor("p3c_rs1", [128, 1], F32))
        psf = [sst.enter_context(nc.psum_tensor(f"p3c_psf{i}", [128, 512], F32)) for i in range(4)]
        psb = [sst.enter_context(nc.psum_tensor(f"p3c_psb{i}", [128, 1024], BF16)) for i in range(3)]

        mask_t = nc.inline_tensor(_causal_mask_np(), "causal_mask")
        p.emit("sp", lambda e: e.dma_start(out=msksb[:], in_=mask_t.ap()),
               writes=["p3c_msk"], dma=True)
        p.emit("dve", lambda e: e.memset(
            vE[:].rearrange("p (kt c) -> p kt c", c=132)[:, :, 128:129], 1.0),
            writes=["p3c_vE_ones"])

        for hd in range(16):
            p.emit("sp", lambda e, hd=hd: e.dma_start(
                out=qT[:], in_=d["qkvT1"][hd * 128:(hd + 1) * 128, :]),
                writes=["p3c_qT"], dma=True)
            p.emit("sp", lambda e, hd=hd: e.dma_start(
                out=kT[:], in_=d["qkvT1"][2048 + hd * 128: 2048 + (hd + 1) * 128, :]),
                writes=["p3c_kT"], dma=True)
            p.emit("sp", lambda e, hd=hd: e.dma_start(
                out=vT[:], in_=d["qkvT1"][4096 + hd * 128: 4096 + (hd + 1) * 128, :]),
                writes=["p3c_vT"], dma=True)
            for g4 in range(8):
                pk = f"p3c_psb{g4 % 2}"
                for q4 in range(4):
                    p.emit("pe", lambda e, g4=g4, q4=q4: e.transpose(
                        psb[g4 % 2][:, q4 * 128:(q4 + 1) * 128],
                        vT[:, g4 * 512 + q4 * 128: g4 * 512 + q4 * 128 + 128], cx.ident_sb[:]),
                        reads=["p3c_vT", "ident_sb"], writes=[pk])
                p.emit("dve", lambda e, g4=g4: e.tensor_copy(
                    vE[:].rearrange("p (kt c) -> p kt c", c=132)[:, g4 * 4:(g4 + 1) * 4, 0:128],
                    psb[g4 % 2][:].rearrange("p (q4 c) -> p q4 c", c=128)[:, 0:4, :]),
                    reads=[pk], writes=["p3c_vE"])
            row0 = hd * 128
            for qc in range(8):
                nkt = 4 * qc + 4
                for kt in range(nkt):
                    bank, bkey = psf[2 + kt % 2], f"p3c_psf{2 + kt % 2}"
                    p.emit("pe", lambda e, kt=kt, qc=qc, bank=bank: e.matmul(
                        bank[:], kT[:, kt * 128:(kt + 1) * 128],
                        qT[:, qc * 512:(qc + 1) * 512], start=True, stop=True),
                        reads=["p3c_kT", "p3c_qT"], writes=[bkey])
                    if kt >= 4 * qc:
                        dlt = kt - 4 * qc
                        p.emit("dve", lambda e, bank=bank, dlt=dlt: e.tensor_add(
                            bank[:], bank[:], msksb[:, dlt * 512:(dlt + 1) * 512]),
                            reads=[bkey, "p3c_msk"], writes=[bkey])
                    p.emit("act", lambda e, kt=kt, bank=bank: e.activation(
                        expT[:, kt * 512:(kt + 1) * 512], bank[:], AF.Exp, bias=cx.eshift),
                        reads=[bkey, "consts"], writes=[f"p3c_expT{kt}"])
                for qs in range(4):
                    bank, bkey = psf[qs % 2], f"p3c_psf{qs % 2}"
                    nk2 = 4 * qc + qs + 1
                    for kt in range(nk2):
                        p.emit("pe", lambda e, kt=kt, qs=qs, bank=bank, nk2=nk2: e.matmul(
                            bank[:, 0:129], expT[:, kt * 512 + qs * 128: kt * 512 + qs * 128 + 128],
                            vE[:, kt * 132: kt * 132 + 129],
                            start=(kt == 0), stop=(kt == nk2 - 1)),
                            reads=[f"p3c_expT{kt}", "p3c_vE", "p3c_vE_ones"], writes=[bkey])
                    p.emit("act", lambda e, bank=bank: e.activation(rs1[:], bank[:, 128:129], AF.Ln),
                           reads=[bkey], writes=["p3c_rs1"])
                    p.emit("act", lambda e: e.activation(rs1[:], rs1[:], AF.Exp, scale=-1.0),
                           reads=["p3c_rs1"], writes=["p3c_rs1"])
                    p.emit("dve", lambda e, bank=bank: e.tensor_scalar(
                        ctmp[:], bank[:, 0:128], rs1[:], None, ALU.mult),
                        reads=[bkey, "p3c_rs1"], writes=["p3c_ctmp"])
                    p.emit("pe", lambda e, qs=qs: e.transpose(
                        psb[2][:, qs * 128:(qs + 1) * 128], ctmp[:], cx.ident_sb[:]),
                        reads=["p3c_ctmp", "ident_sb"], writes=["p3c_psb2"])
                    p.emit("dve", lambda e, qs=qs: e.tensor_copy(
                        cTsb[:, qs * 128:(qs + 1) * 128], psb[2][:, qs * 128:(qs + 1) * 128]),
                        reads=["p3c_psb2"], writes=["p3c_cTsb"])
                p.emit("act", lambda e, row0=row0, qc=qc: e.dma_start(
                    out=d["ctxT1"][row0:row0 + 128, qc * 512:(qc + 1) * 512], in_=cTsb[:]),
                    reads=["p3c_cTsb"], writes=[f"ctxT1_{row0}_{qc}"], dma=True)


def phase3_proj(cx, d):
    nc, p = cx.nc, cx.p
    with contextlib.ExitStack() as sst:
        wsb = sst.enter_context(nc.sbuf_tensor("p3d_w", [128, 16 * 2048], BF16))
        cT = [sst.enter_context(nc.sbuf_tensor(f"p3d_cT{i}", [128, 16 * 512], BF16)) for i in range(2)]
        xo = [sst.enter_context(nc.sbuf_tensor(f"p3d_xo{i}", [128, 4 * 2048], F32)) for i in range(2)]
        x1 = [sst.enter_context(nc.sbuf_tensor("p3d_x10", [128, 4 * 2048], F32))] * 2
        psf = [sst.enter_context(nc.psum_tensor(f"p3d_ps{i}", [128, 512], F32)) for i in range(4)]

        for i in range(4):
            p.emit("pool", lambda e, i=i: e.dma_start(
                out=wsb[:, i * 4 * 2048:(i + 1) * 4 * 2048].rearrange("p (ht o) -> p ht o", o=2048),
                in_=d["wo1T"][i * 512:(i + 1) * 512, :].rearrange("(ht p) o -> p ht o", p=128)),
                writes=["p3d_w"], dma=True)
        for tc in range(8):
            bi = tc % 2
            ck, xok, x1k = f"p3d_cT{bi}", f"p3d_xo{bi}", "p3d_x10"
            p.emit("sp", lambda e, tc=tc, bi=bi: e.dma_start(
                out=cT[bi][:].rearrange("p (ht t) -> p ht t", t=512),
                in_=d["ctxT1"][:, tc * 512:(tc + 1) * 512].rearrange("(ht p) t -> p ht t", p=128)),
                writes=[ck], dma=True)
            p.emit("sp", lambda e, tc=tc, bi=bi: e.dma_start(
                out=xo[bi][:].rearrange("p (j h) -> p j h", h=2048),
                in_=d["x2"][tc * 512:(tc + 1) * 512, :].rearrange("(j p) h -> p j h", p=128)),
                writes=[xok], dma=True)
            for ts in range(4):
                for oc in range(4):
                    bank = psf[oc]
                    bkey = f"p3d_ps{oc}"
                    for kt in range(16):
                        p.emit("pe", lambda e, ts=ts, oc=oc, kt=kt, bank=bank, bi=bi: e.matmul(
                            bank[:], cT[bi][:, kt * 512 + ts * 128: kt * 512 + ts * 128 + 128],
                            wsb[:, kt * 2048 + oc * 512: kt * 2048 + (oc + 1) * 512],
                            start=(kt == 0), stop=(kt == 15)),
                            reads=[ck, "p3d_w"], writes=[bkey])
                    p.emit("dve", lambda e, ts=ts, oc=oc, bank=bank, bi=bi: e.tensor_add(
                        x1[bi][:, ts * 2048 + oc * 512: ts * 2048 + (oc + 1) * 512],
                        bank[:], xo[bi][:, ts * 2048 + oc * 512: ts * 2048 + (oc + 1) * 512]),
                        reads=[f"p3d_ps{oc}", xok], writes=[x1k])
            p.emit("act", lambda e, tc=tc, bi=bi: e.dma_start(
                out=d["x3"][tc * 512:(tc + 1) * 512, :].rearrange("(j p) h -> p j h", p=128),
                in_=x1[bi][:].rearrange("p (j h) -> p j h", h=2048)),
                reads=[x1k], writes=[f"x3_{tc}"], dma=True)




def emit_nvfp4_decode_pool(cx, pk_ap, pk_key, sc_ap, sc_key, out_ap, out_key,
                           it_ap, it_key, tf, tf_key):
    p = cx.p
    sc_b = sc_ap.rearrange("p (b one) -> p b one", one=1).broadcast_to([128, 64, 16])
    cf, st_, rr, vv = (tf[:, i * 1024:(i + 1) * 1024] for i in range(4))
    for which in range(2):
        if which == 0:
            p.emit("dve", lambda e, pk_ap=pk_ap, it_ap=it_ap: e.tensor_scalar(
                it_ap, pk_ap, 15, None, ALU.bitwise_and), reads=[pk_key], writes=[it_key])
        else:
            p.emit("dve", lambda e, pk_ap=pk_ap, it_ap=it_ap: e.tensor_scalar(
                it_ap, pk_ap, 4, 15, ALU.logical_shift_right, ALU.bitwise_and),
                reads=[pk_key], writes=[it_key])
        E = "pool"
        p.emit(E, lambda e, a=cf, b=it_ap: e.tensor_copy(a[:], b), reads=[it_key], writes=[tf_key])
        p.emit(E, lambda e, a=rr, b=cf: e.tensor_scalar(a[:], b[:], -7.0, 0.0, ALU.add, ALU.max),
               reads=[tf_key], writes=[tf_key])
        p.emit(E, lambda e, a=st_, b=rr: e.tensor_scalar(a[:], b[:], 1.0, None, ALU.min),
               reads=[tf_key], writes=[tf_key])
        p.emit(E, lambda e, a=rr, b=st_: e.tensor_scalar(a[:], b[:], -8.0, None, ALU.mult),
               reads=[tf_key], writes=[tf_key])
        p.emit(E, lambda e, a=cf, b=rr: e.tensor_add(a[:], a[:], b[:]),
               reads=[tf_key], writes=[tf_key])
        p.emit(E, lambda e, a=rr, b=cf: e.tensor_scalar(a[:], b[:], -4.0, 0.0, ALU.add, ALU.max),
               reads=[tf_key], writes=[tf_key])
        p.emit(E, lambda e, a=vv, b=cf: e.tensor_scalar(a[:], b[:], -6.0, 0.0, ALU.add, ALU.max),
               reads=[tf_key], writes=[tf_key])
        p.emit(E, lambda e, a=rr: e.tensor_scalar(a[:], a[:], 0.5, None, ALU.mult),
               reads=[tf_key], writes=[tf_key])
        p.emit(E, lambda e, a=rr, b=vv: e.tensor_add(a[:], a[:], b[:]),
               reads=[tf_key], writes=[tf_key])
        p.emit(E, lambda e, a=cf: e.tensor_scalar(a[:], a[:], 0.5, None, ALU.mult),
               reads=[tf_key], writes=[tf_key])
        p.emit(E, lambda e, a=rr, b=cf: e.tensor_add(a[:], a[:], b[:]),
               reads=[tf_key], writes=[tf_key])
        p.emit(E, lambda e, a=st_: e.tensor_scalar(a[:], a[:], -2.0, 1.0, ALU.mult, ALU.add),
               reads=[tf_key], writes=[tf_key])
        p.emit(E, lambda e, a=vv, b=rr, c=st_: e.tensor_mul(a[:], b[:], c[:]),
               reads=[tf_key], writes=[tf_key])
        p.emit(E, lambda e, w=which, a=vv, scb=sc_b, o=out_ap: e.tensor_tensor(
            o[:, w:2048:2].rearrange("p (b k) -> p b k", k=16),
            a[:].rearrange("p (b k) -> p b k", k=16), scb, ALU.mult),
            reads=[tf_key, sc_key], writes=[out_key])


def emit_nvfp4_decode(cx, eng, pk_ap, pk_key, sc_ap, sc_key, out_ap, out_key,
                      it_ap, it_key, tf, tf_key):
    p = cx.p
    sc_b = sc_ap.rearrange("p (b one) -> p b one", one=1).broadcast_to([128, 64, 16])
    cf, st_, rr, vv = (tf[:, i * 1024:(i + 1) * 1024] for i in range(4))
    for which in range(2):
        if which == 0:
            p.emit(eng, lambda e, pk_ap=pk_ap, it_ap=it_ap: e.tensor_scalar(
                it_ap, pk_ap, 15, None, ALU.bitwise_and), reads=[pk_key], writes=[it_key])
        else:
            p.emit(eng, lambda e, pk_ap=pk_ap, it_ap=it_ap: e.tensor_scalar(
                it_ap, pk_ap, 4, 15, ALU.logical_shift_right, ALU.bitwise_and),
                reads=[pk_key], writes=[it_key])
        p.emit(eng, lambda e, a=cf, b=it_ap: e.tensor_copy(a[:], b), reads=[it_key], writes=[tf_key])
        p.emit(eng, lambda e, a=st_, b=cf: e.tensor_scalar(a[:], b[:], 7.5, None, ALU.is_ge),
               reads=[tf_key], writes=[tf_key])
        p.emit(eng, lambda e, a=cf, b=st_: e.scalar_tensor_tensor(a[:], b[:], -8.0, a[:], ALU.mult, ALU.add),
               reads=[tf_key], writes=[tf_key])
        p.emit(eng, lambda e, a=rr, b=cf: e.tensor_scalar(a[:], b[:], -4.0, 0.0, ALU.add, ALU.max),
               reads=[tf_key], writes=[tf_key])
        p.emit(eng, lambda e, a=vv, b=cf: e.tensor_scalar(a[:], b[:], -6.0, 0.0, ALU.add, ALU.max),
               reads=[tf_key], writes=[tf_key])
        p.emit(eng, lambda e, a=rr, b=vv: e.scalar_tensor_tensor(a[:], a[:], 0.5, b[:], ALU.mult, ALU.add),
               reads=[tf_key], writes=[tf_key])
        p.emit(eng, lambda e, a=rr, b=cf: e.scalar_tensor_tensor(a[:], b[:], 0.5, a[:], ALU.mult, ALU.add),
               reads=[tf_key], writes=[tf_key])
        p.emit(eng, lambda e, a=st_: e.tensor_scalar(a[:], a[:], -2.0, 1.0, ALU.mult, ALU.add),
               reads=[tf_key], writes=[tf_key])
        p.emit(eng, lambda e, a=vv, b=rr, c=st_: e.tensor_mul(a[:], b[:], c[:]),
               reads=[tf_key], writes=[tf_key])
        p.emit(eng, lambda e, w=which, a=vv, scb=sc_b, o=out_ap: e.tensor_tensor(
            o[:, w:2048:2].rearrange("p (b k) -> p b k", k=16),
            a[:].rearrange("p (b k) -> p b k", k=16), scb, ALU.mult),
            reads=[tf_key, sc_key], writes=[out_key])


def phase4_gateup(cx, d):
    nc, p = cx.nc, cx.p
    with contextlib.ExitStack() as sst:
        xlnF = sst.enter_context(nc.sbuf_tensor("p4_xlnF", [128, 16 * 4096], BF16))
        psbln = [sst.enter_context(nc.psum_tensor(f"p4_psbln{i}", [128, 1024], BF16)) for i in range(2)]
        with contextlib.ExitStack() as lnst:
            xin = [lnst.enter_context(nc.sbuf_tensor("p4_xin0", [128, 4 * 2048], F32))] * 2
            xbf = lnst.enter_context(nc.sbuf_tensor("p4_xbf", [128, 4 * 2048], BF16))
            stats = lnst.enter_context(nc.sbuf_tensor("p4_stats", [128, 16], F32))
            trash = lnst.enter_context(nc.sbuf_tensor("p4_trash", [128, H], BF16))
            for tc in range(8):
                xink = "p4_xin0"
                p.emit("sp", lambda e, tc=tc: e.dma_start(
                    out=xin[0][:].rearrange("p (j h) -> p j h", h=H),
                    in_=d["x3"][tc * 512:(tc + 1) * 512, :].rearrange("(j p) h -> p j h", p=128)),
                    writes=[xink], dma=True)
                emit_ln_chunk(cx, xink, xin[0][:], xbf, "p4_xbf", stats, "p4_stats", trash)
                for ht in range(16):
                    pk2 = f"p4_psbln{ht % 2}"
                    for j in range(4):
                        src2 = xbf[:, j * 2048 + ht * 128: j * 2048 + ht * 128 + 128]
                        p.emit("pe", lambda e, s=src2, j=j, ht=ht: e.transpose(
                            psbln[ht % 2][:, j * 128:(j + 1) * 128], s, cx.ident_sb[:]),
                            reads=["p4_xbf", "ident_sb"], writes=[pk2])
                    p.emit("dve", lambda e, ht=ht, tc=tc: e.tensor_copy(
                        xlnF[:, ht * 4096 + tc * 512: ht * 4096 + tc * 512 + 512],
                        psbln[ht % 2][:, 0:512]),
                        reads=[pk2], writes=["p4_xlnF"])
        p.barrier()
        pk = [sst.enter_context(nc.sbuf_tensor(f"p4_pk{i}", [128, 1024], I32)) for i in range(4)]
        sc = [sst.enter_context(nc.sbuf_tensor(f"p4_sc{i}", [128, 64], F32)) for i in range(4)]
        it = sst.enter_context(nc.sbuf_tensor("p4_it", [128, 1024], I32))
        tf = sst.enter_context(nc.sbuf_tensor("p4_tf", [128, 4 * 1024], BF16))
        wde = [sst.enter_context(nc.sbuf_tensor(f"p4_wde{i}", [128, 2048], BF16)) for i in range(4)]
        wT = [sst.enter_context(nc.sbuf_tensor(f"p4_wT{i}", [128, 16 * 128], BF16)) for i in range(4)]
        ge = sst.enter_context(nc.sbuf_tensor("p4_ge", [128, 512], F32))
        ue = sst.enter_context(nc.sbuf_tensor("p4_ue", [128, 512], F32))
        g2 = sst.enter_context(nc.sbuf_tensor("p4_g2", [128, 512], F32))
        g3 = sst.enter_context(nc.sbuf_tensor("p4_g3", [128, 512], F32))
        itb = [sst.enter_context(nc.sbuf_tensor(f"p4_itb{i}", [128, 512], BF16)) for i in range(2)]
        psf = [sst.enter_context(nc.psum_tensor(f"p4_psf{i}", [128, 512], F32)) for i in range(4)]
        psb = [sst.enter_context(nc.psum_tensor(f"p4_psb{i}", [128, 1024], BF16)) for i in range(2)]

        def dequant(eng, pki, sci, wdei, nt_key):
            emit_nvfp4_decode(cx, eng, pk[pki][:], f"p4_pk{pki}", sc[sci][:], f"p4_sc{sci}",
                              wde[wdei][:], f"p4_wde{wdei}", it[:], "p4_it", tf, "p4_tf")

        NTT = int(os.environ.get("P4NT", "128"))

        def prep_decode(nt):
            par = nt % 2
            for gi, (pnm, snm) in enumerate((("pkg", "sg"), ("pku", "su"))):
                bu = 2 * par + gi
                p.emit("sp", lambda e, nt=nt, pnm=pnm, bu=bu: e.dma_start(
                    out=pk[bu][:], in_=d[pnm][nt * 128:(nt + 1) * 128, :]),
                    writes=[f"p4_pk{bu}"], dma=True)
                p.emit("sp", lambda e, nt=nt, snm=snm, bu=bu: e.dma_start(
                    out=sc[bu][:], in_=d[snm][nt * 128:(nt + 1) * 128, :]),
                    writes=[f"p4_sc{bu}"], dma=True)
            for gi in range(2):
                bu = 2 * par + gi
                dequant("dve", bu, bu, bu, nt)

        def prep_transpose(nt):
            par = nt % 2
            for gi in range(2):
                bu = 2 * par + gi
                for kt in range(16):
                    pk2 = f"p4_psb{gi}"
                    p.emit("pe", lambda e, kt=kt, bu=bu, gi=gi: e.transpose(
                        psb[gi][:, (kt % 8) * 128:(kt % 8 + 1) * 128],
                        wde[bu][:, kt * 128:(kt + 1) * 128], cx.ident_sb[:]),
                        reads=[f"p4_wde{bu}", "ident_sb"], writes=[pk2])
                    p.emit("act", lambda e, kt=kt, bu=bu, gi=gi: e.copy(
                        wT[bu][:, kt * 128:(kt + 1) * 128], psb[gi][:, (kt % 8) * 128:(kt % 8 + 1) * 128]),
                        reads=[pk2], writes=[f"p4_wT{bu}"])

        prep_decode(0)
        prep_transpose(0)
        if NTT > 1:
            prep_decode(1)
        for nt in range(NTT):
            par = nt % 2
            for tg in range(8):
                for gi in range(2):
                    bu = 2 * par + gi
                    bank, bkey = psf[gi], f"p4_psf{gi}"
                    for kt in range(16):
                        p.emit("pe", lambda e, kt=kt, tg=tg, bu=bu, bank=bank: e.matmul(
                            bank[:], wT[bu][:, kt * 128:(kt + 1) * 128],
                            xlnF[:, kt * 4096 + tg * 512: kt * 4096 + tg * 512 + 512],
                            start=(kt == 0), stop=(kt == 15)),
                            reads=[f"p4_wT{bu}", "p4_xlnF"], writes=[bkey])
                p.emit("act", lambda e: e.copy(ge[:], psf[0][:]), reads=["p4_psf0"], writes=["p4_ge"])
                p.emit("act", lambda e: e.copy(ue[:], psf[1][:]), reads=["p4_psf1"], writes=["p4_ue"])
                p.emit("act", lambda e: e.activation(g2[:], ge[:], AF.Square),
                       reads=["p4_ge"], writes=["p4_g2"])
                p.emit("act", lambda e: e.activation(
                    g2[:], g2[:], AF.Copy, scale=0.7978845608028654 * 0.044715,
                    bias=0.7978845608028654),
                    reads=["p4_g2", "consts"], writes=["p4_g2"])
                p.emit("dve", lambda e: e.tensor_mul(g3[:], ge[:], g2[:]),
                       reads=["p4_ge", "p4_g2"], writes=["p4_g3"])
                p.emit("act", lambda e: e.activation(g3[:], g3[:], AF.Tanh),
                       reads=["p4_g3"], writes=["p4_g3"])
                p.emit("act", lambda e: e.activation(g3[:], g3[:], AF.Copy, scale=0.5, bias=0.5),
                       reads=["p4_g3", "consts"], writes=["p4_g3"])
                p.emit("dve", lambda e: e.tensor_mul(g2[:], ge[:], g3[:]),
                       reads=["p4_ge", "p4_g3"], writes=["p4_g2"])
                ib = tg % 2
                p.emit("dve", lambda e, ib=ib: e.tensor_mul(itb[ib][:], g2[:], ue[:]),
                       reads=["p4_g2", "p4_ue"], writes=[f"p4_itb{ib}"])
                p.emit("act", lambda e, nt=nt, tg=tg, ib=ib: e.dma_start(
                    out=d["interT"][nt * 128:(nt + 1) * 128, tg * 512:(tg + 1) * 512], in_=itb[ib][:]),
                    reads=[f"p4_itb{ib}"], writes=[f"interT_{nt}_{tg}"], dma=True)
                if tg == 0 and nt + 1 < NTT:
                    prep_transpose(nt + 1)
                if tg == 1 and nt + 2 < NTT:
                    prep_decode(nt + 2)


def phase4_wd(cx, d):
    nc, p = cx.nc, cx.p
    with contextlib.ExitStack() as sst:
        pk = [sst.enter_context(nc.sbuf_tensor(f"p4c_pk{i}", [128, 1024], I32)) for i in range(2)]
        sc = [sst.enter_context(nc.sbuf_tensor(f"p4c_sc{i}", [128, 64], F32)) for i in range(2)]
        it = sst.enter_context(nc.sbuf_tensor("p4c_it", [128, 1024], I32))
        tf = sst.enter_context(nc.sbuf_tensor("p4c_tf", [128, 4 * 1024], BF16))
        it2 = sst.enter_context(nc.sbuf_tensor("p4c_it2", [128, 1024], I32))
        tf2 = sst.enter_context(nc.sbuf_tensor("p4c_tf2", [128, 4 * 1024], BF16))
        wde = [sst.enter_context(nc.sbuf_tensor(f"p4c_wde{i}", [128, 2048], BF16)) for i in range(2)]
        wT = [sst.enter_context(nc.sbuf_tensor(f"p4c_wT{i}", [128, 16 * 128], BF16)) for i in range(2)]
        psb = [sst.enter_context(nc.psum_tensor(f"p4c_psb{i}", [128, 1024], BF16)) for i in range(2)]

        for hb in range(16):
            for wc in range(8):
                bi = wc % 2
                p.emit("sp", lambda e, hb=hb, wc=wc, bi=bi: e.dma_start(
                    out=pk[bi][:], in_=d["pkd"][hb * 128:(hb + 1) * 128, wc * 1024:(wc + 1) * 1024]),
                    writes=[f"p4c_pk{bi}"], dma=True)
                p.emit("sp", lambda e, hb=hb, wc=wc, bi=bi: e.dma_start(
                    out=sc[bi][:], in_=d["sd"][hb * 128:(hb + 1) * 128, wc * 64:(wc + 1) * 64]),
                    writes=[f"p4c_sc{bi}"], dma=True)
                itx, itk = (it, "p4c_it") if bi == 0 else (it2, "p4c_it2")
                tfx, tfk = (tf, "p4c_tf") if bi == 0 else (tf2, "p4c_tf2")
                emit_nvfp4_decode(cx, "dve", pk[bi][:], f"p4c_pk{bi}",
                                  sc[bi][:], f"p4c_sc{bi}", wde[bi][:], f"p4c_wde{bi}",
                                  itx[:], itk, tfx, tfk)
                for kt in range(16):
                    pk2 = f"p4c_psb{bi}"
                    p.emit("pe", lambda e, kt=kt, bi=bi: e.transpose(
                        psb[bi][:, (kt % 8) * 128:(kt % 8 + 1) * 128],
                        wde[bi][:, kt * 128:(kt + 1) * 128], cx.ident_sb[:]),
                        reads=[f"p4c_wde{bi}", "ident_sb"], writes=[pk2])
                    p.emit("act", lambda e, kt=kt, bi=bi: e.copy(
                        wT[bi][:, kt * 128:(kt + 1) * 128], psb[bi][:, (kt % 8) * 128:(kt % 8 + 1) * 128]),
                        reads=[pk2], writes=[f"p4c_wT{bi}"])
                p.emit("act", lambda e, hb=hb, wc=wc, bi=bi: e.dma_start(
                    out=d["wdT"][wc * 2048:(wc + 1) * 2048, hb * 128:(hb + 1) * 128].rearrange(
                        "(i p) c -> p i c", p=128),
                    in_=wT[bi][:].rearrange("p (i c) -> p i c", c=128)),
                    reads=[f"p4c_wT{bi}"], writes=[f"wdT_{hb}_{wc}"], dma=True)


def phase4_down(cx, d):
    nc, p = cx.nc, cx.p
    with contextlib.ExitStack() as sst:
        itF = sst.enter_context(nc.sbuf_tensor("p4d_itF", [128, 128 * 512], BF16))
        wdt = [sst.enter_context(nc.sbuf_tensor(f"p4d_wdt{i}", [128, 512], BF16)) for i in range(6)]
        xo = sst.enter_context(nc.sbuf_tensor("p4d_xo", [128, 4 * 2048], F32))
        x4 = sst.enter_context(nc.sbuf_tensor("p4d_x4", [128, 4 * 2048], F32))
        psf = [sst.enter_context(nc.psum_tensor(f"p4d_ps{i}", [128, 512], F32)) for i in range(4)]

        for tb in range(8):
            for sb8 in range(8):
                p.emit("sp", lambda e, tb=tb, sb8=sb8: e.dma_start(
                    out=itF[:, sb8 * 16 * 512:(sb8 + 1) * 16 * 512].rearrange("p (i t) -> p i t", t=512),
                    in_=d["interT"][sb8 * 2048:(sb8 + 1) * 2048, tb * 512:(tb + 1) * 512].rearrange(
                        "(i p) t -> p i t", p=128)),
                    writes=["p4d_itF"], dma=True)
            p.emit("sp", lambda e, tb=tb: e.dma_start(
                out=xo[:].rearrange("p (j h) -> p j h", h=H),
                in_=d["x3"][tb * 512:(tb + 1) * 512, :].rearrange("(j p) h -> p j h", p=128)),
                writes=["p4d_xo"], dma=True)
            for hc in range(4):
                for nt in range(128):
                    wb = nt % 6
                    wk = f"p4d_wdt{wb}"
                    q = "pool" if nt % 2 else "sp"
                    p.emit(q, lambda e, nt=nt, hc=hc, wb=wb: e.dma_start(
                        out=wdt[wb][:], in_=d["wdT"][nt * 128:(nt + 1) * 128, hc * 512:(hc + 1) * 512]),
                        writes=[wk], dma=True)
                    for ts in range(4):
                        bank, bkey = psf[ts], f"p4d_ps{ts}"
                        p.emit("pe", lambda e, nt=nt, ts=ts, wb=wb, bank=bank: e.matmul(
                            bank[:], itF[:, nt * 512 + ts * 128: nt * 512 + ts * 128 + 128],
                            wdt[wb][:], start=(nt == 0), stop=(nt == 127)),
                            reads=[wk, "p4d_itF"], writes=[bkey])
                for ts in range(4):
                    p.emit("dve", lambda e, ts=ts, hc=hc: e.tensor_add(
                        x4[:, ts * 2048 + hc * 512: ts * 2048 + hc * 512 + 512],
                        psf[ts][:], xo[:, ts * 2048 + hc * 512: ts * 2048 + hc * 512 + 512]),
                        reads=[f"p4d_ps{ts}", "p4d_xo"], writes=["p4d_x4"])
            p.emit("act", lambda e, tb=tb: e.dma_start(
                out=d["x4"][tb * 512:(tb + 1) * 512, :].rearrange("(j p) h -> p j h", p=128),
                in_=x4[:].rearrange("p (j h) -> p j h", h=H)),
                reads=["p4d_x4"], writes=[f"x4_{tb}"], dma=True)



def _prep_inputs(inp):
    unused = (inp["L0_attn_key"], inp["L0_attn_value"])
    del unused
    f = np.ascontiguousarray
    x = np.asarray(inp["L0_attn_query"], dtype=np.float32)
    Wqkv = np.asarray(inp["L0_attn_Wqkv"], dtype=np.float32)
    per_core = []
    shared = {
        "wproj0T": f(np.asarray(inp["L0_attn_Wproj"], np.float32).T),
        "bproj0": f(np.asarray(inp["L0_attn_bproj"], np.float32)[None, :]),
        "w0T": f(np.asarray(inp["L0_ffn_weight"], np.float32).T),
        "gatewT": f(np.asarray(inp["L0_ffn_gate_w"], np.float32).T),
        "A2": f(np.asarray(inp["L0_ffn_A_w"], np.float32).reshape(128, 2048).T),
        "B2": f(np.asarray(inp["L0_ffn_B_w"], np.float32).transpose(0, 2, 1).reshape(128, 2048)),
        "wqkv1T": f(np.asarray(inp["L1_attn_w_qkv"], np.float32).T),
        "wo1T": f(np.asarray(inp["L1_attn_w_o"], np.float32).T),
        "pkg": f(np.asarray(inp["L1_ffn_gate_W_packed"], np.int32)),
        "sg": f(np.asarray(inp["L1_ffn_gate_scales"], np.float32)),
        "pku": f(np.asarray(inp["L1_ffn_up_W_packed"], np.int32)),
        "su": f(np.asarray(inp["L1_ffn_up_scales"], np.float32)),
        "pkd": f(np.asarray(inp["L1_ffn_down_W_packed"], np.int32)),
        "sd": f(np.asarray(inp["L1_ffn_down_scales"], np.float32)),
    }
    xfull = f(x.reshape(B * S, H))
    for c in range(8):
        sub = np.concatenate([Wqkv[384 * c:384 * c + 384], Wqkv[384 * (c + 8):384 * (c + 8) + 384]], axis=0)
        m = dict(shared)
        m["x_full"] = xfull
        m["x_own"] = f(x[c])
        m["wqkv0T"] = f(sub.T)
        per_core.append(m)
    return per_core



IN_SPECS = {
    "x_full": ([B * S, H], F32), "x_own": ([S, H], F32),
    "wqkv0T": ([H, 768], F32), "wproj0T": ([H, H], F32), "bproj0": ([1, H], F32),
    "w0T": ([H, H], F32), "gatewT": ([H, 8], F32), "A2": ([H, 128], F32), "B2": ([128, H], F32),
    "wqkv1T": ([H, 3 * H], F32), "wo1T": ([H, H], F32),
    "pkg": ([MLP, H // 2], I32), "sg": ([MLP, 64], F32),
    "pku": ([MLP, H // 2], I32), "su": ([MLP, 64], F32),
    "pkd": ([H, MLP // 2], I32), "sd": ([H, 512], F32),
}
PHASE_IO = {
    "p1a": (["x_full", "wqkv0T"], [], ["ctxT0"]),
    "p1b": (["x_own", "wproj0T", "bproj0"], ["ctxT0"], ["x1"]),
    "p2": (["w0T", "gatewT", "A2", "B2"], ["x1"], ["x2"]),
    "p3a": (["wqkv1T"], ["x2"], ["qkvT1"]),
    "p3c": ([], ["qkvT1"], ["ctxT1"]),
    "p3d": (["wo1T"], ["ctxT1", "x2"], ["x3"]),
    "p4a": (["pkg", "sg", "pku", "su"], ["x3"], ["interT"]),
    "p4c": (["pkd", "sd"], [], ["wdT"]),
    "p4d": ([], ["interT", "wdT", "x3"], ["x4"]),
}
PHASE_FN = {}
SCRATCH_SPECS = {
    "ctxT0": ([H, S], BF16), "x1": ([S, H], F32), "x2": ([S, H], F32),
    "x3": ([S, H], F32), "x4": ([S, H], F32),
    "ctxT1": ([H, S], BF16), "xlnT3": ([H, S], BF16), "qkvT1": ([3 * H, S], BF16),
    "interT": ([MLP, S], BF16), "wdT": ([MLP, H], BF16),
}


def build_program(phase_names, out_name):
    nc = bass.Bass("TRN2", target_bir_lowering=False)
    d = {}
    needed_in, needed_scr = set(), set()
    for pn in phase_names:
        ins, rs, ws = PHASE_IO[pn]
        needed_in |= set(ins)
        needed_scr |= set(rs) | set(ws)
    produced = set()
    for pn in phase_names:
        produced |= set(PHASE_IO[pn][2])
    for n in sorted(needed_in):
        shp, dt = IN_SPECS[n]
        d[n] = nc.dram_tensor(n, shp, dt, kind="ExternalInput").ap()
    for n in sorted(needed_scr):
        shp, dt = SCRATCH_SPECS.get(n, ([S, H], F32))
        if n == out_name:
            kind = "ExternalOutput"
        elif n in produced:
            kind = "Internal"
        else:
            kind = "ExternalInput"
            needed_in.add(n)
        d[n] = nc.dram_tensor(n, shp, dt, kind=kind).ap()

    st = contextlib.ExitStack()
    with st:
        prog = Prog(nc)
        cx = Ctx(nc, prog, st)
        prog.barrier()
        for i, pn in enumerate(phase_names):
            PHASE_FN[pn](cx, d)
            if i < len(phase_names) - 1:
                prog.barrier()
        prog.check()
        prog.realize(st)
    return nc, sorted(needed_in)


def run_phases(per_core, phase_names, out_name):
    nc, in_names = build_program(phase_names, out_name)
    in_maps = [{k: m[k] for k in in_names} for m in per_core]
    res = run_bass_kernel_spmd(nc, in_maps, core_ids=list(range(8)))
    return [res.results[c][out_name] for c in range(8)]


ALL_PHASES = ["p1a", "p1b", "p2", "p3a", "p3c", "p3d", "p4a", "p4c", "p4d"]


def kernel(**inp):
    per_core = _prep_inputs(inp)
    out = run_phases(per_core, ALL_PHASES, "x4")
    return np.stack(out, axis=0)


PHASE_FN.update({"p1a": phase1_qkv_attn, "p1b": phase1_proj, "p2": phase2_ffn0,
                 "p3a": phase3_qkv, "p3c": phase3_attn, "p3d": phase3_proj,
                 "p4a": phase4_gateup, "p4c": phase4_wd, "p4d": phase4_down})


# version 13
# speedup vs baseline: 1.0976x; 1.0750x over previous
import contextlib
import os
import numpy as np
import ml_dtypes
import concourse.bass as bass
import concourse.mybir as mybir
from concourse.bass_utils import run_bass_kernel_spmd

F32 = mybir.dt.float32
BF16 = mybir.dt.bfloat16
I32 = mybir.dt.int32
AF = mybir.ActivationFunctionType
ALU = mybir.AluOpType
AXX = mybir.AxisListType.X

B, S, H = 8, 4096, 2048
NH, HD = 16, 128
MLP = 16384
NORM = 1.0 / np.sqrt(128.0)
EXP_SHIFT = -12.0
LN_EPS = 1e-5


ENGINES = ("pe", "act", "dve", "pool", "sp")
_BASS_ENG = {"pe": "tensor", "act": "scalar", "dve": "vector", "pool": "gpsimd", "sp": "sync"}


class Prog:

    NDMA = 8

    def __init__(self, nc):
        self.nc = nc
        self.items = {e: [] for e in ENGINES}
        self.ccount = {e: 0 for e in ENGINES}
        self.dcount = {e: 0 for e in ENGINES}
        self.dslot = {e: [0] * self.NDMA for e in ENGINES}
        self.regions = {}
        self.sem_names = ([f"cs_{e}" for e in ENGINES]
                          + [f"dq_{e}{j}" for e in ENGINES for j in range(self.NDMA)])

    def _deps(self, eng, reads, writes):
        waits = {}
        def add(tok):
            if tok is None:
                return
            s, v = tok
            if s == f"cs_{eng}" and eng == "pe":
                return
            if waits.get(s, 0) < v:
                waits[s] = v
        for k in reads:
            r = self.regions.get(k)
            if r:
                add(r["w"])
        for k in writes:
            r = self.regions.get(k)
            if r:
                add(r["w"])
                for s, v in r["r"].items():
                    add((s, v))
        return waits

    def _finish(self, eng, reads, writes, tok):
        s, v = tok
        for k in reads:
            r = self.regions.setdefault(k, {"w": None, "r": {}})["r"]
            if r.get(s, 0) < v:
                r[s] = v
        for k in writes:
            self.regions[k] = {"w": tok, "r": {}}

    def emit(self, eng, fn, reads=(), writes=(), dma=False):
        waits = self._deps(eng, reads, writes)
        if dma:
            slot = self.dcount[eng] % self.NDMA
            sem = f"dq_{eng}{slot}"
            prev = 16 * self.dslot[eng][slot]
            if prev and waits.get(sem, 0) < prev:
                waits[sem] = prev
            self.dcount[eng] += 1
            self.dslot[eng][slot] += 1
            inc = 16
            tok = (sem, 16 * self.dslot[eng][slot])
        else:
            self.ccount[eng] += 1
            sem, inc = f"cs_{eng}", 1
            tok = (sem, self.ccount[eng])
        self.items[eng].append((waits, fn, sem, inc))
        self._finish(eng, reads, writes, tok)
        return tok

    def barrier(self):
        arrive = {}
        for e in ENGINES:
            waits = {}
            for j in range(self.NDMA):
                if self.dslot[e][j]:
                    waits[f"dq_{e}{j}"] = 16 * self.dslot[e][j]
            self.ccount[e] += 1
            self.items[e].append((waits, lambda eng: eng.nop(nofuse=True), f"cs_{e}", 1))
            arrive[e] = self.ccount[e]
        for e in ENGINES:
            waits = {f"cs_{f}": arrive[f] for f in ENGINES if f != e}
            self.ccount[e] += 1
            self.items[e].append((waits, lambda eng: eng.nop(nofuse=True), f"cs_{e}", 1))
        bar = {e: (f"cs_{e}", self.ccount[e]) for e in ENGINES}
        for k in self.regions:
            w = self.regions[k]["w"]
            e = None
            if w is not None:
                s = w[0]
                e = s.split("_", 1)[1]
            self.regions[k] = {"w": bar.get(e) if e else None, "r": {}}

    def check(self):
        ptr = {e: 0 for e in ENGINES}
        sem = {n: 0 for n in self.sem_names}
        progress = True
        while progress:
            progress = False
            for e in ENGINES:
                while ptr[e] < len(self.items[e]):
                    waits, fn, s, inc = self.items[e][ptr[e]]
                    if any(sem[k] < v for k, v in waits.items()):
                        break
                    sem[s] += inc
                    ptr[e] += 1
                    progress = True
        stuck = {e: ptr[e] for e in ENGINES if ptr[e] < len(self.items[e])}
        if stuck:
            msgs = []
            for e, i in stuck.items():
                waits = self.items[e][i][0]
                bad = {k: (v, sem[k]) for k, v in waits.items() if sem[k] < v}
                msgs.append(f"{e}@{i}/{len(self.items[e])} waits {bad}")
            raise RuntimeError("DEADLOCK: " + "; ".join(msgs))

    def realize(self, st):
        nc = self.nc
        sems = {n: st.enter_context(nc.semaphore(n)) for n in self.sem_names}
        block = st.enter_context(nc.Block())
        clocks = {e: {} for e in ENGINES}
        waited = {e: set() for e in ENGINES}
        for f in ENGINES:
            for waits, _, _, _ in self.items[f]:
                for s, v in waits.items():
                    if s.startswith("cs_"):
                        waited[s[3:]].add(v)

        def run_engine(e):
            def body(engobj):
                clock = clocks[e]
                cnt = 0
                for waits, fn, sem, inc in self.items[e]:
                    n_emitted = 0
                    for s in sorted(waits):
                        v = waits[s]
                        if clock.get(s, 0) < v:
                            engobj.wait_ge(sems[s], v)
                            clock[s] = v
                            n_emitted += 1
                    if sem == f"cs_{e}":
                        cnt += 1
                        if (n_emitted == 0 and (cnt - 1) in waited[e]
                                and cnt - 1 > clock.get(sem, 0)):
                            engobj.wait_ge(sems[sem], cnt - 1)
                            clock[sem] = cnt - 1
                    inst = fn(engobj)
                    inst.then_inc(sems[sem], inc)
                for j in range(self.NDMA):
                    if self.dslot[e][j]:
                        engobj.wait_ge(sems[f"dq_{e}{j}"], 16 * self.dslot[e][j])
            return body

        for e in ENGINES:
            getattr(block, _BASS_ENG[e])(run_engine(e))



class Ctx:
    def __init__(self, nc, prog, st):
        self.nc, self.p, self.st = nc, prog, st
        ident_np = np.eye(128, dtype=ml_dtypes.bfloat16)
        self.ident = nc.inline_tensor(ident_np, "ident128")
        self.ident_sb = st.enter_context(nc.sbuf_tensor("ident_sb", [128, 128], BF16))
        self.consts = st.enter_context(nc.sbuf_tensor("consts", [128, 8], F32))
        p = prog
        p.emit("sp", lambda e: e.dma_start(out=self.ident_sb[:], in_=self.ident.ap()),
               writes=["ident_sb"], dma=True)
        p.emit("dve", lambda e: e.memset(self.consts[:, 0:1], LN_EPS), writes=["consts"])
        p.emit("dve", lambda e: e.memset(self.consts[:, 1:2], EXP_SHIFT), writes=["consts"])
        p.emit("dve", lambda e: e.memset(self.consts[:, 2:3], 1.0), writes=["consts"])
        p.emit("dve", lambda e: e.memset(self.consts[:, 3:4], -1e30), writes=["consts"])
        p.emit("dve", lambda e: e.memset(self.consts[:, 4:5], 0.5), writes=["consts"])
        p.emit("dve", lambda e: e.memset(self.consts[:, 5:6], 0.7978845608028654), writes=["consts"])
        nc.const_aps.register(F32, 0.5, self.consts[:, 4:5])
        nc.const_aps.register(F32, 0.7978845608028654, self.consts[:, 5:6])
        p.emit("dve", lambda e: e.memset(self.consts[:, 6:7], 0.7978845608028654 * 0.044715), writes=["consts"])
        nc.const_aps.register(F32, 0.7978845608028654 * 0.044715, self.consts[:, 6:7])

    @property
    def eps(self):
        return self.consts[:, 0:1]

    @property
    def eshift(self):
        return self.consts[:, 1:2]

    @property
    def half(self):
        return self.consts[:, 4:5]


def emit_ln_chunk(cx, key_in, xin_ap, xbf, xbf_key, stats, stats_key, trash, ntile=4, hdim=H):
    p = cx.p
    n = ntile
    c = 1.0 / hdim
    for j in range(n):
        tile = xin_ap[:, j * hdim:(j + 1) * hdim]
        p.emit("dve", lambda e, t=tile, j=j: e.reduce_sum(stats[:, j:j + 1], t, axis=AXX),
               reads=[key_in], writes=[stats_key])
        p.emit("act", lambda e, t=tile, j=j: e.activation(
            trash[:, 0:hdim], t, AF.Square, accum_out=stats[:, n + j:n + j + 1]),
            reads=[key_in], writes=[stats_key, "trash"])
    p.emit("dve", lambda e: e.tensor_scalar(stats[:, 2 * n:3 * n], stats[:, 0:n], c, None, ALU.mult),
           reads=[stats_key], writes=[stats_key])
    p.emit("dve", lambda e: e.tensor_scalar(stats[:, 3 * n:4 * n], stats[:, n:2 * n], c, None, ALU.mult),
           reads=[stats_key], writes=[stats_key])
    p.emit("dve", lambda e: e.tensor_mul(stats[:, 0:n], stats[:, 2 * n:3 * n], stats[:, 2 * n:3 * n]),
           reads=[stats_key], writes=[stats_key])
    p.emit("dve", lambda e: e.tensor_sub(stats[:, 3 * n:4 * n], stats[:, 3 * n:4 * n], stats[:, 0:n]),
           reads=[stats_key], writes=[stats_key])
    p.emit("act", lambda e: e.activation(stats[:, 3 * n:4 * n], stats[:, 3 * n:4 * n], AF.Ln, bias=cx.eps),
           reads=[stats_key, "consts"], writes=[stats_key])
    p.emit("act", lambda e: e.activation(stats[:, 3 * n:4 * n], stats[:, 3 * n:4 * n], AF.Exp, scale=-0.5),
           reads=[stats_key], writes=[stats_key])
    for j in range(n):
        tile = xin_ap[:, j * hdim:(j + 1) * hdim]
        p.emit("dve", lambda e, t=tile, j=j: e.tensor_scalar(
            xbf[:, j * hdim:(j + 1) * hdim], t, stats[:, 2 * n + j:2 * n + j + 1],
            stats[:, 3 * n + j:3 * n + j + 1], ALU.subtract, ALU.mult),
            reads=[key_in, stats_key], writes=[xbf_key])


def emit_transpose_chunk(cx, xbf, xbf_key, xlnT, xlnT_key, psbf, psbf_key, ntile=4, nht=16):
    p = cx.p
    tw = ntile * 128
    for ht in range(nht):
        for j in range(ntile):
            src = xbf[:, j * (nht * 128) + ht * 128: j * (nht * 128) + ht * 128 + 128]
            dst = psbf[:, j * 128:(j + 1) * 128]
            p.emit("pe", lambda e, s=src, d=dst: e.transpose(d, s, cx.ident_sb[:]),
                   reads=[xbf_key, "ident_sb"], writes=[psbf_key])
        p.emit("dve", lambda e, ht=ht: e.tensor_copy(xlnT[:, ht * tw:(ht + 1) * tw], psbf[:, 0:tw]),
               reads=[psbf_key], writes=[xlnT_key])



def phase1_qkv_attn(cx, d):
    nc, p, st = cx.nc, cx.p, cx.st
    with contextlib.ExitStack() as sst:
        xin = [sst.enter_context(nc.sbuf_tensor(f"p1_xin{i}", [128, 4 * H], F32)) for i in range(2)]
        xbf = sst.enter_context(nc.sbuf_tensor("p1_xbf", [128, 4 * H], BF16))
        xlnT = [sst.enter_context(nc.sbuf_tensor("p1_xlnT0", [128, 16 * 512], BF16))] * 2
        stats = sst.enter_context(nc.sbuf_tensor("p1_stats", [128, 16], F32))
        trash = sst.enter_context(nc.sbuf_tensor("p1_trash", [128, H], BF16))
        wsb = sst.enter_context(nc.sbuf_tensor("p1_wsb", [128, 16 * 768], BF16))
        qT = [sst.enter_context(nc.sbuf_tensor(f"p1_qT{i}", [128, 4096], BF16)) for i in range(2)]
        kT = [sst.enter_context(nc.sbuf_tensor(f"p1_kT{i}", [128, 4096], BF16)) for i in range(2)]
        vE = [sst.enter_context(nc.sbuf_tensor(f"p1_vE{i}", [128, 32 * 132], BF16)) for i in range(2)]
        vtmp = sst.enter_context(nc.sbuf_tensor("p1_vtmp", [128, 512], BF16))
        expT = sst.enter_context(nc.sbuf_tensor("p1_expT", [128, 32 * 512], BF16))
        ctmp = sst.enter_context(nc.sbuf_tensor("p1_ctmp", [128, 128], BF16))
        cTsb = sst.enter_context(nc.sbuf_tensor("p1_cTsb", [128, 512], BF16))
        rs1 = sst.enter_context(nc.sbuf_tensor("p1_rs1", [128, 1], F32))
        psf = [sst.enter_context(nc.psum_tensor(f"p1_psf{i}", [128, 512], F32)) for i in range(6)]
        psb = [sst.enter_context(nc.psum_tensor(f"p1_psb{i}", [128, 1024], BF16)) for i in range(2)]

        p.emit("pool", lambda e: e.dma_start(
            out=wsb[:].rearrange("p (ht dd) -> p ht dd", dd=768),
            in_=d["wqkv0T"].rearrange("(ht p) dd -> p ht dd", p=128)),
            writes=["p1_wsb"], dma=True)
        for i in range(2):
            p.emit("dve", lambda e, i=i: e.memset(
                vE[i][:].rearrange("p (kt c) -> p kt c", c=132)[:, :, 128:129], 1.0),
                writes=[f"p1_vE{i}_ones"])

        for m in range(B):
            for tc in range(8):
                bi = tc % 2
                xink, xlnk = f"p1_xin{bi}", "p1_xlnT0"
                src = d["x_full"][(m * S + tc * 512):(m * S + (tc + 1) * 512), :]
                p.emit("sp", lambda e, s=src, bi=bi: e.dma_start(
                    out=xin[bi][:].rearrange("p (j h) -> p j h", h=H),
                    in_=s.rearrange("(j p) h -> p j h", p=128)),
                    writes=[xink], dma=True)
                emit_ln_chunk(cx, xink, xin[bi][:], xbf, "p1_xbf", stats, "p1_stats", trash)
                emit_transpose_chunk(cx, xbf, "p1_xbf", xlnT[bi], xlnk, psb[tc % 2], f"p1_psb{tc % 2}")
                for dt in range(6):
                    bank = psf[dt % 2]
                    bkey = f"p1_psf{dt % 2}"
                    for ht in range(16):
                        p.emit("pe", lambda e, dt=dt, ht=ht, bank=bank, bi=bi: e.matmul(
                            bank[:], wsb[:, ht * 768 + dt * 128: ht * 768 + (dt + 1) * 128],
                            xlnT[bi][:, ht * 512:(ht + 1) * 512],
                            start=(ht == 0), stop=(ht == 15)),
                            reads=["p1_wsb", xlnk], writes=[bkey])
                    hp = dt // 3
                    kind = dt % 3
                    if kind == 0:
                        p.emit("dve", lambda e, hp=hp, bank=bank, tc=tc: e.tensor_scalar(
                            qT[hp][:, tc * 512:(tc + 1) * 512], bank[:], NORM, None, ALU.mult),
                            reads=[bkey], writes=[f"p1_qT{hp}"])
                    elif kind == 1:
                        p.emit("dve", lambda e, hp=hp, bank=bank, tc=tc: e.tensor_copy(
                            kT[hp][:, tc * 512:(tc + 1) * 512], bank[:]),
                            reads=[bkey], writes=[f"p1_kT{hp}"])
                    else:
                        p.emit("dve", lambda e, bank=bank: e.tensor_copy(vtmp[:], bank[:]),
                               reads=[bkey], writes=["p1_vtmp"])
                        for q4 in range(4):
                            p.emit("pe", lambda e, q4=q4: e.transpose(
                                psb[1][:, q4 * 128:(q4 + 1) * 128], vtmp[:, q4 * 128:(q4 + 1) * 128],
                                cx.ident_sb[:]),
                                reads=["p1_vtmp", "ident_sb"], writes=["p1_psb1"])
                        p.emit("dve", lambda e, hp=hp, tc=tc: e.tensor_copy(
                            vE[hp][:].rearrange("p (kt c) -> p kt c", c=132)[:, tc * 4:(tc + 1) * 4, 0:128],
                            psb[1][:].rearrange("p (q4 c) -> p q4 c", c=128)[:, 0:4, :]),
                            reads=["p1_psb1"], writes=[f"p1_vE{hp}"])
            for hp in range(2):
                row0 = 256 * m + 128 * hp
                for qc in range(8):
                    for kt in range(32):
                        bank = psf[2 + kt % 2]
                        bkey = f"p1_psf{2 + kt % 2}"
                        p.emit("pe", lambda e, hp=hp, kt=kt, qc=qc, bank=bank: e.matmul(
                            bank[:], kT[hp][:, kt * 128:(kt + 1) * 128],
                            qT[hp][:, qc * 512:(qc + 1) * 512], start=True, stop=True),
                            reads=[f"p1_kT{hp}", f"p1_qT{hp}"], writes=[bkey])
                        p.emit("act", lambda e, kt=kt, bank=bank: e.activation(
                            expT[:, kt * 512:(kt + 1) * 512], bank[:], AF.Exp, bias=cx.eshift),
                            reads=[bkey, "consts"], writes=[f"p1_expT{kt}"])
                    for qs in range(4):
                        bank = psf[4 + qs % 2]
                        bkey = f"p1_psf{4 + qs % 2}"
                        for kt in range(32):
                            p.emit("pe", lambda e, hp=hp, kt=kt, qs=qs, bank=bank: e.matmul(
                                bank[:, 0:129], expT[:, kt * 512 + qs * 128: kt * 512 + qs * 128 + 128],
                                vE[hp][:, kt * 132: kt * 132 + 129],
                                start=(kt == 0), stop=(kt == 31)),
                                reads=[f"p1_expT{kt}", f"p1_vE{hp}", f"p1_vE{hp}_ones"], writes=[bkey])
                        p.emit("act", lambda e, bank=bank: e.activation(rs1[:], bank[:, 128:129], AF.Ln),
                               reads=[bkey], writes=["p1_rs1"])
                        p.emit("act", lambda e: e.activation(rs1[:], rs1[:], AF.Exp, scale=-1.0),
                               reads=["p1_rs1"], writes=["p1_rs1"])
                        p.emit("dve", lambda e, bank=bank: e.tensor_scalar(
                            ctmp[:], bank[:, 0:128], rs1[:], None, ALU.mult),
                            reads=[bkey, "p1_rs1"], writes=["p1_ctmp"])
                        p.emit("pe", lambda e, qs=qs: e.transpose(
                            psb[0][:, qs * 128:(qs + 1) * 128], ctmp[:], cx.ident_sb[:]),
                            reads=["p1_ctmp", "ident_sb"], writes=["p1_psb0"])
                        p.emit("dve", lambda e, qs=qs: e.tensor_copy(
                            cTsb[:, qs * 128:(qs + 1) * 128], psb[0][:, qs * 128:(qs + 1) * 128]),
                            reads=["p1_psb0"], writes=["p1_cTsb"])
                    p.emit("act", lambda e, row0=row0, qc=qc: e.dma_start(
                        out=d["ctxT0"][row0:row0 + 128, qc * 512:(qc + 1) * 512], in_=cTsb[:]),
                        reads=["p1_cTsb"], writes=[f"ctxT0_{row0}_{qc}"], dma=True)


def phase1_proj(cx, d):
    nc, p = cx.nc, cx.p
    with contextlib.ExitStack() as sst:
        wsb = sst.enter_context(nc.sbuf_tensor("p1b_w", [128, 16 * 2048], BF16))
        bbc = sst.enter_context(nc.sbuf_tensor("p1b_bbc", [128, 2048], F32))
        cT = [sst.enter_context(nc.sbuf_tensor(f"p1b_cT{i}", [128, 16 * 512], BF16)) for i in range(2)]
        xo = [sst.enter_context(nc.sbuf_tensor(f"p1b_xo{i}", [128, 4 * 2048], F32)) for i in range(2)]
        x1 = [sst.enter_context(nc.sbuf_tensor("p1b_x10", [128, 4 * 2048], F32))] * 2
        psf = [sst.enter_context(nc.psum_tensor(f"p1b_ps{i}", [128, 512], F32)) for i in range(4)]

        for i in range(4):
            p.emit("pool", lambda e, i=i: e.dma_start(
                out=wsb[:, i * 4 * 2048:(i + 1) * 4 * 2048].rearrange("p (ht o) -> p ht o", o=2048),
                in_=d["wproj0T"][i * 512:(i + 1) * 512, :].rearrange("(ht p) o -> p ht o", p=128)),
                writes=["p1b_w"], dma=True)
        p.emit("sp", lambda e: e.dma_start(
            out=bbc[:], in_=d["bproj0"].partition_broadcast(128)),
            writes=["p1b_bbc"], dma=True)

        for tc in range(8):
            bi = tc % 2
            ck, xok, x1k = f"p1b_cT{bi}", f"p1b_xo{bi}", "p1b_x10"
            p.emit("sp", lambda e, tc=tc, bi=bi: e.dma_start(
                out=cT[bi][:].rearrange("p (ht t) -> p ht t", t=512),
                in_=d["ctxT0"][:, tc * 512:(tc + 1) * 512].rearrange("(ht p) t -> p ht t", p=128)),
                reads=[f"ctxT0_all"], writes=[ck], dma=True)
            p.emit("sp", lambda e, tc=tc, bi=bi: e.dma_start(
                out=xo[bi][:].rearrange("p (j h) -> p j h", h=2048),
                in_=d["x_own"][tc * 512:(tc + 1) * 512, :].rearrange("(j p) h -> p j h", p=128)),
                writes=[xok], dma=True)
            for j in range(4):
                p.emit("dve", lambda e, j=j, bi=bi: e.tensor_add(
                    xo[bi][:, j * 2048:(j + 1) * 2048], xo[bi][:, j * 2048:(j + 1) * 2048], bbc[:]),
                    reads=[xok, "p1b_bbc"], writes=[xok])
            for ts in range(4):
                for oc in range(4):
                    bank = psf[oc]
                    bkey = f"p1b_ps{oc}"
                    for kt in range(16):
                        p.emit("pe", lambda e, ts=ts, oc=oc, kt=kt, bank=bank, bi=bi: e.matmul(
                            bank[:], cT[bi][:, kt * 512 + ts * 128: kt * 512 + ts * 128 + 128],
                            wsb[:, kt * 2048 + oc * 512: kt * 2048 + (oc + 1) * 512],
                            start=(kt == 0), stop=(kt == 15)),
                            reads=[ck, "p1b_w"], writes=[bkey])
                    p.emit("dve", lambda e, ts=ts, oc=oc, bank=bank, bi=bi: e.tensor_add(
                        x1[bi][:, ts * 2048 + oc * 512: ts * 2048 + (oc + 1) * 512],
                        bank[:], xo[bi][:, ts * 2048 + oc * 512: ts * 2048 + (oc + 1) * 512]),
                        reads=[f"p1b_ps{oc}", xok], writes=[x1k])
            p.emit("act", lambda e, tc=tc, bi=bi: e.dma_start(
                out=d["x1"][tc * 512:(tc + 1) * 512, :].rearrange("(j p) h -> p j h", p=128),
                in_=x1[bi][:].rearrange("p (j h) -> p j h", h=2048)),
                reads=[x1k], writes=[f"x1_{tc}"], dma=True)



def phase2_ffn0(cx, d):
    lvl = int(os.environ.get("P2LVL", "4"))
    nc, p = cx.nc, cx.p
    with contextlib.ExitStack() as sst:
        w0sb = sst.enter_context(nc.sbuf_tensor("p2_w0", [128, 16 * 2048], BF16))
        a2sb = sst.enter_context(nc.sbuf_tensor("p2_a2", [128, 16 * 128], BF16))
        b2sb = sst.enter_context(nc.sbuf_tensor("p2_b2", [128, 2048], BF16))
        gwsb = sst.enter_context(nc.sbuf_tensor("p2_gw", [128, 16 * 8], BF16))
        xin = [sst.enter_context(nc.sbuf_tensor(f"p2_xin{i}", [128, 4 * 2048], F32)) for i in range(2)]
        xbf = sst.enter_context(nc.sbuf_tensor("p2_xbf", [128, 4 * 2048], BF16))
        xlnT = sst.enter_context(nc.sbuf_tensor("p2_xlnT", [128, 16 * 512], BF16))
        x2b = sst.enter_context(nc.sbuf_tensor("p2_x2", [128, 4 * 2048], F32))
        stats = sst.enter_context(nc.sbuf_tensor("p2_stats", [128, 16], F32))
        trash = sst.enter_context(nc.sbuf_tensor("p2_trash", [128, H], BF16))
        gl = sst.enter_context(nc.sbuf_tensor("p2_gl", [128, 8], F32))
        el = sst.enter_context(nc.sbuf_tensor("p2_el", [128, 8], F32))
        sel = sst.enter_context(nc.sbuf_tensor("p2_sel", [128, 8], F32))
        msk = sst.enter_context(nc.sbuf_tensor("p2_msk", [128, 8], F32))
        cw = sst.enter_context(nc.sbuf_tensor("p2_cw", [128, 8], F32))
        mm = sst.enter_context(nc.sbuf_tensor("p2_mm", [128, 4], F32))
        tmw = sst.enter_context(nc.sbuf_tensor("p2_tmw", [128, 128], BF16))
        tcwT = sst.enter_context(nc.sbuf_tensor("p2_tcwT", [128, 4 * 128], BF16))
        psf = [sst.enter_context(nc.psum_tensor(f"p2_psf{i}", [128, 512], F32)) for i in range(6)]
        psb = [sst.enter_context(nc.psum_tensor(f"p2_psb{i}", [128, 1024], BF16)) for i in range(2)]

        for i in range(4):
            p.emit("pool", lambda e, i=i: e.dma_start(
                out=w0sb[:, i * 4 * 2048:(i + 1) * 4 * 2048].rearrange("p (ht o) -> p ht o", o=2048),
                in_=d["w0T"][i * 512:(i + 1) * 512, :].rearrange("(ht p) o -> p ht o", p=128)),
                writes=["p2_w0"], dma=True)
        p.emit("pool", lambda e: e.dma_start(
            out=a2sb[:].rearrange("p (ht c) -> p ht c", c=128),
            in_=d["A2"].rearrange("(ht p) c -> p ht c", p=128)),
            writes=["p2_a2"], dma=True)
        p.emit("pool", lambda e: e.dma_start(out=b2sb[:], in_=d["B2"]),
               writes=["p2_b2"], dma=True)
        p.emit("pool", lambda e: e.dma_start(
            out=gwsb[:].rearrange("p (ht c) -> p ht c", c=8),
            in_=d["gatewT"].rearrange("(ht p) c -> p ht c", p=128)),
            writes=["p2_gw"], dma=True)

        for tc in range(int(os.environ.get("P2TC", "8"))):
            bi = tc % 2
            xink = f"p2_xin{bi}"
            p.emit("sp", lambda e, tc=tc, bi=bi: e.dma_start(
                out=xin[bi][:].rearrange("p (j h) -> p j h", h=H),
                in_=d["x1"][tc * 512:(tc + 1) * 512, :].rearrange("(j p) h -> p j h", p=128)),
                writes=[xink], dma=True)
            emit_ln_chunk(cx, xink, xin[bi][:], xbf, "p2_xbf", stats, "p2_stats", trash)
            emit_transpose_chunk(cx, xbf, "p2_xbf", xlnT, "p2_xlnT", psb[0], "p2_psb0")
            for ts in range(4):
              if lvl >= 3:
                bank, bkey = psf[4], "p2_psf4"
                for kt in range(16):
                    p.emit("pe", lambda e, ts=ts, kt=kt, bank=bank: e.matmul(
                        bank[:, 0:8], xlnT[:, kt * 512 + ts * 128: kt * 512 + ts * 128 + 128],
                        gwsb[:, kt * 8:(kt + 1) * 8], start=(kt == 0), stop=(kt == 15)),
                        reads=["p2_xlnT", "p2_gw"], writes=[bkey])
                p.emit("dve", lambda e, bank=bank: e.tensor_copy(gl[:], bank[:, 0:8]),
                       reads=[bkey], writes=["p2_gl"])
                p.emit("dve", lambda e: e.reduce_max(mm[:, 0:1], gl[:], axis=AXX, negate=True),
                       reads=["p2_gl"], writes=["p2_mm"])
                p.emit("act", lambda e: e.activation(el[:], gl[:], AF.Exp, bias=mm[:, 0:1]),
                       reads=["p2_gl", "p2_mm"], writes=["p2_el"])
                p.emit("dve", lambda e: e.reduce_max(mm[:, 0:1], el[:], axis=AXX),
                       reads=["p2_el"], writes=["p2_mm"])
                p.emit("dve", lambda e: e.tensor_scalar(msk[:], el[:], mm[:, 0:1], None, ALU.is_equal),
                       reads=["p2_el", "p2_mm"], writes=["p2_msk"])
                p.emit("dve", lambda e: e.scalar_tensor_tensor(sel[:], msk[:], -2.0, el[:], ALU.mult, ALU.add),
                       reads=["p2_msk", "p2_el"], writes=["p2_sel"])
                p.emit("dve", lambda e: e.reduce_max(mm[:, 1:2], sel[:], axis=AXX),
                       reads=["p2_sel"], writes=["p2_mm"])
                p.emit("dve", lambda e: e.tensor_add(mm[:, 2:3], mm[:, 0:1], mm[:, 1:2]),
                       reads=["p2_mm"], writes=["p2_mm"])
                p.emit("act", lambda e: e.activation(mm[:, 3:4], mm[:, 2:3], AF.Ln),
                       reads=["p2_mm"], writes=["p2_mm"])
                p.emit("act", lambda e: e.activation(mm[:, 3:4], mm[:, 3:4], AF.Exp, scale=-1.0),
                       reads=["p2_mm"], writes=["p2_mm"])
                p.emit("dve", lambda e: e.tensor_scalar(msk[:], el[:], mm[:, 1:2], None, ALU.is_ge),
                       reads=["p2_el", "p2_mm"], writes=["p2_msk"])
                p.emit("dve", lambda e: e.tensor_mul(sel[:], el[:], msk[:]),
                       reads=["p2_el", "p2_msk"], writes=["p2_sel"])
                p.emit("dve", lambda e: e.tensor_scalar(cw[:], sel[:], mm[:, 3:4], 16.0, ALU.mult, ALU.mult),
                       reads=["p2_sel", "p2_mm"], writes=["p2_cw"])
              if lvl >= 4:
                bank2, bkey2 = psf[5], "p2_psf5"
                for kt in range(16):
                    p.emit("pe", lambda e, ts=ts, kt=kt, bank=bank2: e.matmul(
                        bank[:, 0:128],
                        a2sb[:, kt * 128:(kt + 1) * 128],
                        xlnT[:, kt * 512 + ts * 128: kt * 512 + ts * 128 + 128],
                        start=(kt == 0), stop=(kt == 15)),
                        reads=["p2_a2", "p2_xlnT"], writes=[bkey2])
                p.emit("act", lambda e, bank=bank2: e.copy(tmw[:], bank[:, 0:128]),
                       reads=[bkey2], writes=["p2_tmw"])
                p.emit("pe", lambda e: e.transpose(psb[1][:, 0:128], tmw[:], cx.ident_sb[:]),
                       reads=["p2_tmw", "ident_sb"], writes=["p2_psb1"])
                p.emit("dve", lambda e: e.tensor_copy(tmw[:], psb[1][:, 0:128]),
                       reads=["p2_psb1"], writes=["p2_tmw"])
                for ex in range(8):
                    p.emit("dve", lambda e, ex=ex: e.tensor_scalar(
                        tmw[:, ex * 16:(ex + 1) * 16], tmw[:, ex * 16:(ex + 1) * 16],
                        cw[:, ex:ex + 1], None, ALU.mult),
                        reads=["p2_tmw", "p2_cw"], writes=["p2_tmw"])
                p.emit("pe", lambda e: e.transpose(psb[1][:, 128:256], tmw[:], cx.ident_sb[:]),
                       reads=["p2_tmw", "ident_sb"], writes=["p2_psb1"])
                p.emit("dve", lambda e, ts=ts: e.tensor_copy(tcwT[:, ts * 128:(ts + 1) * 128], psb[1][:, 128:256]),
                       reads=["p2_psb1"], writes=["p2_tcwT"])
              if lvl >= 2:
                for oc in range(4):
                    bank3 = psf[oc]
                    bkey3 = f"p2_psf{oc}"
                    for kt in range(16 if lvl >= 4 else 15):
                        p.emit("pe", lambda e, ts=ts, oc=oc, kt=kt, bank=bank3: e.matmul(
                            bank[:],
                            xlnT[:, kt * 512 + ts * 128: kt * 512 + ts * 128 + 128],
                            w0sb[:, kt * 2048 + oc * 512: kt * 2048 + (oc + 1) * 512],
                            start=(kt == 0), stop=False),
                            reads=["p2_xlnT", "p2_w0"], writes=[bkey3])
                    if lvl >= 4:
                        p.emit("pe", lambda e, ts=ts, oc=oc, bank=bank3: e.matmul(
                            bank[:], tcwT[:, ts * 128:(ts + 1) * 128],
                            b2sb[:, oc * 512:(oc + 1) * 512], start=False, stop=True),
                            reads=["p2_tcwT", "p2_b2"], writes=[bkey3])
                    else:
                        p.emit("pe", lambda e, ts=ts, oc=oc, kt=15, bank=bank3: e.matmul(
                            bank[:],
                            xlnT[:, kt * 512 + ts * 128: kt * 512 + ts * 128 + 128],
                            w0sb[:, kt * 2048 + oc * 512: kt * 2048 + (oc + 1) * 512],
                            start=False, stop=True),
                            reads=["p2_xlnT", "p2_w0"], writes=[bkey3])
                    ev = int(os.environ.get("P2EVAC", "1"))
                    if (lvl >= 3 or ev == 1):
                        p.emit("dve", lambda e, ts=ts, oc=oc, bank=bank3, bi=bi: e.tensor_add(
                            x2b[:, ts * 2048 + oc * 512: ts * 2048 + (oc + 1) * 512],
                            bank[:], xin[bi][:, ts * 2048 + oc * 512: ts * 2048 + (oc + 1) * 512]),
                            reads=[bkey3, xink], writes=["p2_x2"])
                    elif ev == 2:
                        p.emit("dve", lambda e, ts=ts, oc=oc, bank=bank3: e.tensor_copy(
                            x2b[:, ts * 2048 + oc * 512: ts * 2048 + (oc + 1) * 512],
                            bank[:]),
                            reads=[bkey3], writes=["p2_x2"])
                    else:
                        p.emit("dve", lambda e, ts=ts, oc=oc, bank=bank3, bi=bi: e.tensor_copy(
                            x2b[:, ts * 2048 + oc * 512: ts * 2048 + (oc + 1) * 512],
                            xin[bi][:, ts * 2048 + oc * 512: ts * 2048 + (oc + 1) * 512]),
                            reads=[bkey3, xink], writes=["p2_x2"])
              if lvl < 2:
                for j in range(4):
                    p.emit("dve", lambda e, j=j, bi=bi: e.tensor_copy(
                        x2b[:, j * 2048:(j + 1) * 2048], xin[bi][:, j * 2048:(j + 1) * 2048]),
                        reads=[xink], writes=["p2_x2"])
            p.emit("act", lambda e, tc=tc: e.dma_start(
                out=d["x2"][tc * 512:(tc + 1) * 512, :].rearrange("(j p) h -> p j h", p=128),
                in_=x2b[:].rearrange("p (j h) -> p j h", h=H)),
                reads=["p2_x2"], writes=[f"x2_{tc}"], dma=True)



def phase3_qkv(cx, d):
    nc, p = cx.nc, cx.p
    with contextlib.ExitStack() as sst:
        xlnF = sst.enter_context(nc.sbuf_tensor("p3_xlnF", [128, 16 * 4096], BF16))
        xin = [sst.enter_context(nc.sbuf_tensor("p3_xin0", [128, 4 * 2048], F32))] * 2
        xbf = sst.enter_context(nc.sbuf_tensor("p3_xbf", [128, 4 * 2048], BF16))
        stats = sst.enter_context(nc.sbuf_tensor("p3_stats", [128, 16], F32))
        trash = sst.enter_context(nc.sbuf_tensor("p3_trash", [128, H], BF16))
        wbuf = [sst.enter_context(nc.sbuf_tensor(f"p3_w{i}", [128, 16 * 128], BF16)) for i in range(2)]
        qbuf = [sst.enter_context(nc.sbuf_tensor(f"p3_q{i}", [128, 512], BF16)) for i in range(2)]
        psf = [sst.enter_context(nc.psum_tensor(f"p3_psf{i}", [128, 512], F32)) for i in range(2)]
        psb = [sst.enter_context(nc.psum_tensor(f"p3_psb{i}", [128, 1024], BF16)) for i in range(2)]

        for tc in range(8):
            bi = 0
            xink = "p3_xin0"
            p.emit("sp", lambda e, tc=tc, bi=bi: e.dma_start(
                out=xin[bi][:].rearrange("p (j h) -> p j h", h=H),
                in_=d["x2"][tc * 512:(tc + 1) * 512, :].rearrange("(j p) h -> p j h", p=128)),
                writes=[xink], dma=True)
            emit_ln_chunk(cx, xink, xin[bi][:], xbf, "p3_xbf", stats, "p3_stats", trash)
            for ht in range(16):
                pk = f"p3_psb{ht % 2}"
                for j in range(4):
                    src = xbf[:, j * 2048 + ht * 128: j * 2048 + ht * 128 + 128]
                    p.emit("pe", lambda e, s=src, j=j, ht=ht: e.transpose(
                        psb[ht % 2][:, j * 128:(j + 1) * 128], s, cx.ident_sb[:]),
                        reads=["p3_xbf", "ident_sb"], writes=[pk])
                p.emit("dve", lambda e, ht=ht, tc=tc: e.tensor_copy(
                    xlnF[:, ht * 4096 + tc * 512: ht * 4096 + tc * 512 + 512], psb[ht % 2][:, 0:512]),
                    reads=[pk], writes=["p3_xlnF"])
        for dt in range(48):
            wb = dt % 2
            wk = f"p3_w{wb}"
            p.emit("pool", lambda e, dt=dt, wb=wb: e.dma_start(
                out=wbuf[wb][:].rearrange("p (ht c) -> p ht c", c=128),
                in_=d["wqkv1T"][:, dt * 128:(dt + 1) * 128].rearrange("(ht p) c -> p ht c", p=128)),
                writes=[wk], dma=True)
            for tg in range(8):
                bank, bkey = psf[tg % 2], f"p3_psf{tg % 2}"
                for kt in range(16):
                    p.emit("pe", lambda e, dt=dt, tg=tg, kt=kt, bank=bank, wb=wb: e.matmul(
                        bank[:], wbuf[wb][:, kt * 128:(kt + 1) * 128],
                        xlnF[:, kt * 4096 + tg * 512: kt * 4096 + (tg) * 512 + 512],
                        start=(kt == 0), stop=(kt == 15)),
                        reads=[wk, "p3_xlnF"], writes=[bkey])
                qb = tg % 2
                qk = f"p3_q{qb}"
                if dt < 16:
                    p.emit("dve", lambda e, bank=bank, qb=qb: e.tensor_scalar(
                        qbuf[qb][:], bank[:], NORM, None, ALU.mult), reads=[bkey], writes=[qk])
                else:
                    p.emit("dve", lambda e, bank=bank, qb=qb: e.tensor_copy(
                        qbuf[qb][:], bank[:]), reads=[bkey], writes=[qk])
                p.emit("act", lambda e, dt=dt, tg=tg, qb=qb: e.dma_start(
                    out=d["qkvT1"][dt * 128:(dt + 1) * 128, tg * 512:(tg + 1) * 512], in_=qbuf[qb][:]),
                    reads=[qk], writes=[f"qkvT1_{dt}_{tg}"], dma=True)


_CAUSAL_MASK = None


def _causal_mask_np():
    global _CAUSAL_MASK
    if _CAUSAL_MASK is None:
        m = np.zeros((128, 4 * 512), np.float32)
        for dlt in range(4):
            kj = np.arange(128)[:, None]
            qj = np.arange(512)[None, :]
            m[:, dlt * 512:(dlt + 1) * 512] = np.where(128 * dlt + kj <= qj, 0.0, -1e30)
        _CAUSAL_MASK = m
    return _CAUSAL_MASK


def phase3_attn(cx, d):
    nc, p = cx.nc, cx.p
    with contextlib.ExitStack() as sst:
        wpk = [sst.enter_context(nc.sbuf_tensor(f"p4c_pk{i}", [128, 1024], I32)) for i in range(2)]
        wsc = [sst.enter_context(nc.sbuf_tensor(f"p4c_sc{i}", [128, 64], F32)) for i in range(2)]
        wit = sst.enter_context(nc.sbuf_tensor("p4c_it", [128, 1024], I32))
        wtf = sst.enter_context(nc.sbuf_tensor("p4c_tf", [128, 4 * 1024], BF16))
        wwde = [sst.enter_context(nc.sbuf_tensor(f"p4c_wde{i}", [128, 2048], BF16)) for i in range(2)]
        wwT = [sst.enter_context(nc.sbuf_tensor(f"p4c_wT{i}", [128, 16 * 128], BF16)) for i in range(2)]
        psb3 = sst.enter_context(nc.psum_tensor("p3c_psb3", [128, 1024], BF16))

        def wd_chunk(ci):
            hb, wc = ci // 8, ci % 8
            bi = ci % 2
            p.emit("sp", lambda e, hb=hb, wc=wc, bi=bi: e.dma_start(
                out=wpk[bi][:], in_=d["pkd"][hb * 128:(hb + 1) * 128, wc * 1024:(wc + 1) * 1024]),
                writes=[f"p4c_pk{bi}"], dma=True)
            p.emit("sp", lambda e, hb=hb, wc=wc, bi=bi: e.dma_start(
                out=wsc[bi][:], in_=d["sd"][hb * 128:(hb + 1) * 128, wc * 64:(wc + 1) * 64]),
                writes=[f"p4c_sc{bi}"], dma=True)
            emit_nvfp4_decode(cx, "dve", wpk[bi][:], f"p4c_pk{bi}",
                              wsc[bi][:], f"p4c_sc{bi}", wwde[bi][:], f"p4c_wde{bi}",
                              wit[:], "p4c_it", wtf, "p4c_tf")
            for kt in range(16):
                p.emit("pe", lambda e, kt=kt, bi=bi: e.transpose(
                    psb3[:, (kt % 8) * 128:(kt % 8 + 1) * 128],
                    wwde[bi][:, kt * 128:(kt + 1) * 128], cx.ident_sb[:]),
                    reads=[f"p4c_wde{bi}", "ident_sb"], writes=["p3c_psb3"])
                p.emit("dve", lambda e, kt=kt, bi=bi: e.tensor_copy(
                    wwT[bi][:, kt * 128:(kt + 1) * 128], psb3[:, (kt % 8) * 128:(kt % 8 + 1) * 128]),
                    reads=["p3c_psb3"], writes=[f"p4c_wT{bi}"])
            p.emit("act", lambda e, hb=hb, wc=wc, bi=bi: e.dma_start(
                out=d["wdT"][wc * 2048:(wc + 1) * 2048, hb * 128:(hb + 1) * 128].rearrange(
                    "(i p) c -> p i c", p=128),
                in_=wwT[bi][:].rearrange("p (i c) -> p i c", c=128)),
                reads=[f"p4c_wT{bi}"], writes=[f"wdT_{hb}_{wc}"], dma=True)

        qT = sst.enter_context(nc.sbuf_tensor("p3c_qT", [128, 4096], BF16))
        kT = sst.enter_context(nc.sbuf_tensor("p3c_kT", [128, 4096], BF16))
        vT = sst.enter_context(nc.sbuf_tensor("p3c_vT", [128, 4096], BF16))
        vE = sst.enter_context(nc.sbuf_tensor("p3c_vE", [128, 32 * 132], BF16))
        expT = sst.enter_context(nc.sbuf_tensor("p3c_expT", [128, 32 * 512], BF16))
        msksb = sst.enter_context(nc.sbuf_tensor("p3c_msk", [128, 4 * 512], F32))
        ctmp = sst.enter_context(nc.sbuf_tensor("p3c_ctmp", [128, 128], BF16))
        cTsb = sst.enter_context(nc.sbuf_tensor("p3c_cTsb", [128, 512], BF16))
        rs1 = sst.enter_context(nc.sbuf_tensor("p3c_rs1", [128, 1], F32))
        psf = [sst.enter_context(nc.psum_tensor(f"p3c_psf{i}", [128, 512], F32)) for i in range(4)]
        psb = [sst.enter_context(nc.psum_tensor(f"p3c_psb{i}", [128, 1024], BF16)) for i in range(3)]

        mask_t = nc.inline_tensor(_causal_mask_np(), "causal_mask")
        p.emit("sp", lambda e: e.dma_start(out=msksb[:], in_=mask_t.ap()),
               writes=["p3c_msk"], dma=True)
        p.emit("dve", lambda e: e.memset(
            vE[:].rearrange("p (kt c) -> p kt c", c=132)[:, :, 128:129], 1.0),
            writes=["p3c_vE_ones"])

        for hd in range(16):
            p.emit("sp", lambda e, hd=hd: e.dma_start(
                out=qT[:], in_=d["qkvT1"][hd * 128:(hd + 1) * 128, :]),
                writes=["p3c_qT"], dma=True)
            p.emit("sp", lambda e, hd=hd: e.dma_start(
                out=kT[:], in_=d["qkvT1"][2048 + hd * 128: 2048 + (hd + 1) * 128, :]),
                writes=["p3c_kT"], dma=True)
            p.emit("sp", lambda e, hd=hd: e.dma_start(
                out=vT[:], in_=d["qkvT1"][4096 + hd * 128: 4096 + (hd + 1) * 128, :]),
                writes=["p3c_vT"], dma=True)
            for g4 in range(8):
                pk = f"p3c_psb{g4 % 2}"
                for q4 in range(4):
                    p.emit("pe", lambda e, g4=g4, q4=q4: e.transpose(
                        psb[g4 % 2][:, q4 * 128:(q4 + 1) * 128],
                        vT[:, g4 * 512 + q4 * 128: g4 * 512 + q4 * 128 + 128], cx.ident_sb[:]),
                        reads=["p3c_vT", "ident_sb"], writes=[pk])
                p.emit("dve", lambda e, g4=g4: e.tensor_copy(
                    vE[:].rearrange("p (kt c) -> p kt c", c=132)[:, g4 * 4:(g4 + 1) * 4, 0:128],
                    psb[g4 % 2][:].rearrange("p (q4 c) -> p q4 c", c=128)[:, 0:4, :]),
                    reads=[pk], writes=["p3c_vE"])
            row0 = hd * 128
            for qc in range(8):
                nkt = 4 * qc + 4
                for kt in range(nkt):
                    bank, bkey = psf[2 + kt % 2], f"p3c_psf{2 + kt % 2}"
                    p.emit("pe", lambda e, kt=kt, qc=qc, bank=bank: e.matmul(
                        bank[:], kT[:, kt * 128:(kt + 1) * 128],
                        qT[:, qc * 512:(qc + 1) * 512], start=True, stop=True),
                        reads=["p3c_kT", "p3c_qT"], writes=[bkey])
                    if kt >= 4 * qc:
                        dlt = kt - 4 * qc
                        p.emit("dve", lambda e, bank=bank, dlt=dlt: e.tensor_add(
                            bank[:], bank[:], msksb[:, dlt * 512:(dlt + 1) * 512]),
                            reads=[bkey, "p3c_msk"], writes=[bkey])
                    p.emit("act", lambda e, kt=kt, bank=bank: e.activation(
                        expT[:, kt * 512:(kt + 1) * 512], bank[:], AF.Exp, bias=cx.eshift),
                        reads=[bkey, "consts"], writes=[f"p3c_expT{kt}"])
                for qs in range(4):
                    bank, bkey = psf[qs % 2], f"p3c_psf{qs % 2}"
                    nk2 = 4 * qc + qs + 1
                    for kt in range(nk2):
                        p.emit("pe", lambda e, kt=kt, qs=qs, bank=bank, nk2=nk2: e.matmul(
                            bank[:, 0:129], expT[:, kt * 512 + qs * 128: kt * 512 + qs * 128 + 128],
                            vE[:, kt * 132: kt * 132 + 129],
                            start=(kt == 0), stop=(kt == nk2 - 1)),
                            reads=[f"p3c_expT{kt}", "p3c_vE", "p3c_vE_ones"], writes=[bkey])
                    p.emit("act", lambda e, bank=bank: e.activation(rs1[:], bank[:, 128:129], AF.Ln),
                           reads=[bkey], writes=["p3c_rs1"])
                    p.emit("act", lambda e: e.activation(rs1[:], rs1[:], AF.Exp, scale=-1.0),
                           reads=["p3c_rs1"], writes=["p3c_rs1"])
                    p.emit("dve", lambda e, bank=bank: e.tensor_scalar(
                        ctmp[:], bank[:, 0:128], rs1[:], None, ALU.mult),
                        reads=[bkey, "p3c_rs1"], writes=["p3c_ctmp"])
                    p.emit("pe", lambda e, qs=qs: e.transpose(
                        psb[2][:, qs * 128:(qs + 1) * 128], ctmp[:], cx.ident_sb[:]),
                        reads=["p3c_ctmp", "ident_sb"], writes=["p3c_psb2"])
                    p.emit("dve", lambda e, qs=qs: e.tensor_copy(
                        cTsb[:, qs * 128:(qs + 1) * 128], psb[2][:, qs * 128:(qs + 1) * 128]),
                        reads=["p3c_psb2"], writes=["p3c_cTsb"])
                p.emit("act", lambda e, row0=row0, qc=qc: e.dma_start(
                    out=d["ctxT1"][row0:row0 + 128, qc * 512:(qc + 1) * 512], in_=cTsb[:]),
                    reads=["p3c_cTsb"], writes=[f"ctxT1_{row0}_{qc}"], dma=True)
                wd_ci = hd * 8 + qc
                if wd_ci < 128:
                    wd_chunk(wd_ci)


def phase3_proj(cx, d):
    nc, p = cx.nc, cx.p
    with contextlib.ExitStack() as sst:
        wsb = sst.enter_context(nc.sbuf_tensor("p3d_w", [128, 16 * 2048], BF16))
        cT = [sst.enter_context(nc.sbuf_tensor(f"p3d_cT{i}", [128, 16 * 512], BF16)) for i in range(2)]
        xo = [sst.enter_context(nc.sbuf_tensor(f"p3d_xo{i}", [128, 4 * 2048], F32)) for i in range(2)]
        x1 = [sst.enter_context(nc.sbuf_tensor("p3d_x10", [128, 4 * 2048], F32))] * 2
        psf = [sst.enter_context(nc.psum_tensor(f"p3d_ps{i}", [128, 512], F32)) for i in range(4)]

        for i in range(4):
            p.emit("pool", lambda e, i=i: e.dma_start(
                out=wsb[:, i * 4 * 2048:(i + 1) * 4 * 2048].rearrange("p (ht o) -> p ht o", o=2048),
                in_=d["wo1T"][i * 512:(i + 1) * 512, :].rearrange("(ht p) o -> p ht o", p=128)),
                writes=["p3d_w"], dma=True)
        for tc in range(8):
            bi = tc % 2
            ck, xok, x1k = f"p3d_cT{bi}", f"p3d_xo{bi}", "p3d_x10"
            p.emit("sp", lambda e, tc=tc, bi=bi: e.dma_start(
                out=cT[bi][:].rearrange("p (ht t) -> p ht t", t=512),
                in_=d["ctxT1"][:, tc * 512:(tc + 1) * 512].rearrange("(ht p) t -> p ht t", p=128)),
                writes=[ck], dma=True)
            p.emit("sp", lambda e, tc=tc, bi=bi: e.dma_start(
                out=xo[bi][:].rearrange("p (j h) -> p j h", h=2048),
                in_=d["x2"][tc * 512:(tc + 1) * 512, :].rearrange("(j p) h -> p j h", p=128)),
                writes=[xok], dma=True)
            for ts in range(4):
                for oc in range(4):
                    bank = psf[oc]
                    bkey = f"p3d_ps{oc}"
                    for kt in range(16):
                        p.emit("pe", lambda e, ts=ts, oc=oc, kt=kt, bank=bank, bi=bi: e.matmul(
                            bank[:], cT[bi][:, kt * 512 + ts * 128: kt * 512 + ts * 128 + 128],
                            wsb[:, kt * 2048 + oc * 512: kt * 2048 + (oc + 1) * 512],
                            start=(kt == 0), stop=(kt == 15)),
                            reads=[ck, "p3d_w"], writes=[bkey])
                    p.emit("dve", lambda e, ts=ts, oc=oc, bank=bank, bi=bi: e.tensor_add(
                        x1[bi][:, ts * 2048 + oc * 512: ts * 2048 + (oc + 1) * 512],
                        bank[:], xo[bi][:, ts * 2048 + oc * 512: ts * 2048 + (oc + 1) * 512]),
                        reads=[f"p3d_ps{oc}", xok], writes=[x1k])
            p.emit("act", lambda e, tc=tc, bi=bi: e.dma_start(
                out=d["x3"][tc * 512:(tc + 1) * 512, :].rearrange("(j p) h -> p j h", p=128),
                in_=x1[bi][:].rearrange("p (j h) -> p j h", h=2048)),
                reads=[x1k], writes=[f"x3_{tc}"], dma=True)




def emit_nvfp4_decode_pool(cx, pk_ap, pk_key, sc_ap, sc_key, out_ap, out_key,
                           it_ap, it_key, tf, tf_key):
    p = cx.p
    sc_b = sc_ap.rearrange("p (b one) -> p b one", one=1).broadcast_to([128, 64, 16])
    cf, st_, rr, vv = (tf[:, i * 1024:(i + 1) * 1024] for i in range(4))
    for which in range(2):
        if which == 0:
            p.emit("dve", lambda e, pk_ap=pk_ap, it_ap=it_ap: e.tensor_scalar(
                it_ap, pk_ap, 15, None, ALU.bitwise_and), reads=[pk_key], writes=[it_key])
        else:
            p.emit("dve", lambda e, pk_ap=pk_ap, it_ap=it_ap: e.tensor_scalar(
                it_ap, pk_ap, 4, 15, ALU.logical_shift_right, ALU.bitwise_and),
                reads=[pk_key], writes=[it_key])
        E = "pool"
        p.emit(E, lambda e, a=cf, b=it_ap: e.tensor_copy(a[:], b), reads=[it_key], writes=[tf_key])
        p.emit(E, lambda e, a=rr, b=cf: e.tensor_scalar(a[:], b[:], -7.0, 0.0, ALU.add, ALU.max),
               reads=[tf_key], writes=[tf_key])
        p.emit(E, lambda e, a=st_, b=rr: e.tensor_scalar(a[:], b[:], 1.0, None, ALU.min),
               reads=[tf_key], writes=[tf_key])
        p.emit(E, lambda e, a=rr, b=st_: e.tensor_scalar(a[:], b[:], -8.0, None, ALU.mult),
               reads=[tf_key], writes=[tf_key])
        p.emit(E, lambda e, a=cf, b=rr: e.tensor_add(a[:], a[:], b[:]),
               reads=[tf_key], writes=[tf_key])
        p.emit(E, lambda e, a=rr, b=cf: e.tensor_scalar(a[:], b[:], -4.0, 0.0, ALU.add, ALU.max),
               reads=[tf_key], writes=[tf_key])
        p.emit(E, lambda e, a=vv, b=cf: e.tensor_scalar(a[:], b[:], -6.0, 0.0, ALU.add, ALU.max),
               reads=[tf_key], writes=[tf_key])
        p.emit(E, lambda e, a=rr: e.tensor_scalar(a[:], a[:], 0.5, None, ALU.mult),
               reads=[tf_key], writes=[tf_key])
        p.emit(E, lambda e, a=rr, b=vv: e.tensor_add(a[:], a[:], b[:]),
               reads=[tf_key], writes=[tf_key])
        p.emit(E, lambda e, a=cf: e.tensor_scalar(a[:], a[:], 0.5, None, ALU.mult),
               reads=[tf_key], writes=[tf_key])
        p.emit(E, lambda e, a=rr, b=cf: e.tensor_add(a[:], a[:], b[:]),
               reads=[tf_key], writes=[tf_key])
        p.emit(E, lambda e, a=st_: e.tensor_scalar(a[:], a[:], -2.0, 1.0, ALU.mult, ALU.add),
               reads=[tf_key], writes=[tf_key])
        p.emit(E, lambda e, a=vv, b=rr, c=st_: e.tensor_mul(a[:], b[:], c[:]),
               reads=[tf_key], writes=[tf_key])
        p.emit(E, lambda e, w=which, a=vv, scb=sc_b, o=out_ap: e.tensor_tensor(
            o[:, w:2048:2].rearrange("p (b k) -> p b k", k=16),
            a[:].rearrange("p (b k) -> p b k", k=16), scb, ALU.mult),
            reads=[tf_key, sc_key], writes=[out_key])


def emit_nvfp4_decode(cx, eng, pk_ap, pk_key, sc_ap, sc_key, out_ap, out_key,
                      it_ap, it_key, tf, tf_key):
    p = cx.p
    sc_b = sc_ap.rearrange("p (b one) -> p b one", one=1).broadcast_to([128, 64, 16])
    cf, st_, rr, vv = (tf[:, i * 1024:(i + 1) * 1024] for i in range(4))
    for which in range(2):
        if which == 0:
            p.emit(eng, lambda e, pk_ap=pk_ap, it_ap=it_ap: e.tensor_scalar(
                it_ap, pk_ap, 15, None, ALU.bitwise_and), reads=[pk_key], writes=[it_key])
        else:
            p.emit(eng, lambda e, pk_ap=pk_ap, it_ap=it_ap: e.tensor_scalar(
                it_ap, pk_ap, 4, 15, ALU.logical_shift_right, ALU.bitwise_and),
                reads=[pk_key], writes=[it_key])
        p.emit(eng, lambda e, a=cf, b=it_ap: e.tensor_copy(a[:], b), reads=[it_key], writes=[tf_key])
        p.emit(eng, lambda e, a=st_, b=cf: e.tensor_scalar(a[:], b[:], 7.5, None, ALU.is_ge),
               reads=[tf_key], writes=[tf_key])
        p.emit(eng, lambda e, a=cf, b=st_: e.scalar_tensor_tensor(a[:], b[:], -8.0, a[:], ALU.mult, ALU.add),
               reads=[tf_key], writes=[tf_key])
        p.emit(eng, lambda e, a=rr, b=cf: e.tensor_scalar(a[:], b[:], -4.0, 0.0, ALU.add, ALU.max),
               reads=[tf_key], writes=[tf_key])
        p.emit(eng, lambda e, a=vv, b=cf: e.tensor_scalar(a[:], b[:], -6.0, 0.0, ALU.add, ALU.max),
               reads=[tf_key], writes=[tf_key])
        p.emit(eng, lambda e, a=rr, b=vv: e.scalar_tensor_tensor(a[:], a[:], 0.5, b[:], ALU.mult, ALU.add),
               reads=[tf_key], writes=[tf_key])
        p.emit(eng, lambda e, a=rr, b=cf: e.scalar_tensor_tensor(a[:], b[:], 0.5, a[:], ALU.mult, ALU.add),
               reads=[tf_key], writes=[tf_key])
        p.emit(eng, lambda e, a=st_: e.tensor_scalar(a[:], a[:], -2.0, 1.0, ALU.mult, ALU.add),
               reads=[tf_key], writes=[tf_key])
        p.emit(eng, lambda e, a=vv, b=rr, c=st_: e.tensor_mul(a[:], b[:], c[:]),
               reads=[tf_key], writes=[tf_key])
        p.emit(eng, lambda e, w=which, a=vv, scb=sc_b, o=out_ap: e.tensor_tensor(
            o[:, w:2048:2].rearrange("p (b k) -> p b k", k=16),
            a[:].rearrange("p (b k) -> p b k", k=16), scb, ALU.mult),
            reads=[tf_key, sc_key], writes=[out_key])


def phase4_gateup(cx, d):
    nc, p = cx.nc, cx.p
    with contextlib.ExitStack() as sst:
        xlnF = sst.enter_context(nc.sbuf_tensor("p4_xlnF", [128, 16 * 4096], BF16))
        psbln = [sst.enter_context(nc.psum_tensor(f"p4_psbln{i}", [128, 1024], BF16)) for i in range(2)]
        with contextlib.ExitStack() as lnst:
            xin = [lnst.enter_context(nc.sbuf_tensor("p4_xin0", [128, 4 * 2048], F32))] * 2
            xbf = lnst.enter_context(nc.sbuf_tensor("p4_xbf", [128, 4 * 2048], BF16))
            stats = lnst.enter_context(nc.sbuf_tensor("p4_stats", [128, 16], F32))
            trash = lnst.enter_context(nc.sbuf_tensor("p4_trash", [128, H], BF16))
            for tc in range(8):
                xink = "p4_xin0"
                p.emit("sp", lambda e, tc=tc: e.dma_start(
                    out=xin[0][:].rearrange("p (j h) -> p j h", h=H),
                    in_=d["x3"][tc * 512:(tc + 1) * 512, :].rearrange("(j p) h -> p j h", p=128)),
                    writes=[xink], dma=True)
                emit_ln_chunk(cx, xink, xin[0][:], xbf, "p4_xbf", stats, "p4_stats", trash)
                for ht in range(16):
                    pk2 = f"p4_psbln{ht % 2}"
                    for j in range(4):
                        src2 = xbf[:, j * 2048 + ht * 128: j * 2048 + ht * 128 + 128]
                        p.emit("pe", lambda e, s=src2, j=j, ht=ht: e.transpose(
                            psbln[ht % 2][:, j * 128:(j + 1) * 128], s, cx.ident_sb[:]),
                            reads=["p4_xbf", "ident_sb"], writes=[pk2])
                    p.emit("dve", lambda e, ht=ht, tc=tc: e.tensor_copy(
                        xlnF[:, ht * 4096 + tc * 512: ht * 4096 + tc * 512 + 512],
                        psbln[ht % 2][:, 0:512]),
                        reads=[pk2], writes=["p4_xlnF"])
        p.barrier()
        pk = [sst.enter_context(nc.sbuf_tensor(f"p4_pk{i}", [128, 1024], I32)) for i in range(4)]
        sc = [sst.enter_context(nc.sbuf_tensor(f"p4_sc{i}", [128, 64], F32)) for i in range(4)]
        it = sst.enter_context(nc.sbuf_tensor("p4_it", [128, 1024], I32))
        tf = sst.enter_context(nc.sbuf_tensor("p4_tf", [128, 4 * 1024], BF16))
        wde = [sst.enter_context(nc.sbuf_tensor(f"p4_wde{i}", [128, 2048], BF16)) for i in range(4)]
        wT = [sst.enter_context(nc.sbuf_tensor(f"p4_wT{i}", [128, 16 * 128], BF16)) for i in range(4)]
        ge = sst.enter_context(nc.sbuf_tensor("p4_ge", [128, 512], F32))
        ue = sst.enter_context(nc.sbuf_tensor("p4_ue", [128, 512], F32))
        g2 = sst.enter_context(nc.sbuf_tensor("p4_g2", [128, 512], F32))
        g3 = sst.enter_context(nc.sbuf_tensor("p4_g3", [128, 512], F32))
        itb = [sst.enter_context(nc.sbuf_tensor(f"p4_itb{i}", [128, 512], BF16)) for i in range(2)]
        psf = [sst.enter_context(nc.psum_tensor(f"p4_psf{i}", [128, 512], F32)) for i in range(4)]
        psb = [sst.enter_context(nc.psum_tensor(f"p4_psb{i}", [128, 1024], BF16)) for i in range(2)]

        def dequant(eng, pki, sci, wdei, nt_key):
            emit_nvfp4_decode(cx, eng, pk[pki][:], f"p4_pk{pki}", sc[sci][:], f"p4_sc{sci}",
                              wde[wdei][:], f"p4_wde{wdei}", it[:], "p4_it", tf, "p4_tf")

        NTT = int(os.environ.get("P4NT", "128"))

        def prep_decode(nt):
            par = nt % 2
            for gi, (pnm, snm) in enumerate((("pkg", "sg"), ("pku", "su"))):
                bu = 2 * par + gi
                p.emit("sp", lambda e, nt=nt, pnm=pnm, bu=bu: e.dma_start(
                    out=pk[bu][:], in_=d[pnm][nt * 128:(nt + 1) * 128, :]),
                    writes=[f"p4_pk{bu}"], dma=True)
                p.emit("sp", lambda e, nt=nt, snm=snm, bu=bu: e.dma_start(
                    out=sc[bu][:], in_=d[snm][nt * 128:(nt + 1) * 128, :]),
                    writes=[f"p4_sc{bu}"], dma=True)
            for gi in range(2):
                bu = 2 * par + gi
                dequant("dve", bu, bu, bu, nt)

        def prep_transpose(nt):
            par = nt % 2
            for gi in range(2):
                bu = 2 * par + gi
                for kt in range(16):
                    pk2 = f"p4_psb{gi}"
                    p.emit("pe", lambda e, kt=kt, bu=bu, gi=gi: e.transpose(
                        psb[gi][:, (kt % 8) * 128:(kt % 8 + 1) * 128],
                        wde[bu][:, kt * 128:(kt + 1) * 128], cx.ident_sb[:]),
                        reads=[f"p4_wde{bu}", "ident_sb"], writes=[pk2])
                    p.emit("act", lambda e, kt=kt, bu=bu, gi=gi: e.copy(
                        wT[bu][:, kt * 128:(kt + 1) * 128], psb[gi][:, (kt % 8) * 128:(kt % 8 + 1) * 128]),
                        reads=[pk2], writes=[f"p4_wT{bu}"])

        prep_decode(0)
        prep_transpose(0)
        if NTT > 1:
            prep_decode(1)
        for nt in range(NTT):
            par = nt % 2
            for tg in range(8):
                for gi in range(2):
                    bu = 2 * par + gi
                    bank, bkey = psf[gi], f"p4_psf{gi}"
                    for kt in range(16):
                        p.emit("pe", lambda e, kt=kt, tg=tg, bu=bu, bank=bank: e.matmul(
                            bank[:], wT[bu][:, kt * 128:(kt + 1) * 128],
                            xlnF[:, kt * 4096 + tg * 512: kt * 4096 + tg * 512 + 512],
                            start=(kt == 0), stop=(kt == 15)),
                            reads=[f"p4_wT{bu}", "p4_xlnF"], writes=[bkey])
                p.emit("act", lambda e: e.copy(ge[:], psf[0][:]), reads=["p4_psf0"], writes=["p4_ge"])
                p.emit("act", lambda e: e.copy(ue[:], psf[1][:]), reads=["p4_psf1"], writes=["p4_ue"])
                p.emit("act", lambda e: e.activation(g2[:], ge[:], AF.Square),
                       reads=["p4_ge"], writes=["p4_g2"])
                p.emit("act", lambda e: e.activation(
                    g2[:], g2[:], AF.Copy, scale=0.7978845608028654 * 0.044715,
                    bias=0.7978845608028654),
                    reads=["p4_g2", "consts"], writes=["p4_g2"])
                p.emit("dve", lambda e: e.tensor_mul(g3[:], ge[:], g2[:]),
                       reads=["p4_ge", "p4_g2"], writes=["p4_g3"])
                p.emit("act", lambda e: e.activation(g3[:], g3[:], AF.Tanh),
                       reads=["p4_g3"], writes=["p4_g3"])
                p.emit("act", lambda e: e.activation(g3[:], g3[:], AF.Copy, scale=0.5, bias=0.5),
                       reads=["p4_g3", "consts"], writes=["p4_g3"])
                p.emit("dve", lambda e: e.tensor_mul(g2[:], ge[:], g3[:]),
                       reads=["p4_ge", "p4_g3"], writes=["p4_g2"])
                ib = tg % 2
                p.emit("dve", lambda e, ib=ib: e.tensor_mul(itb[ib][:], g2[:], ue[:]),
                       reads=["p4_g2", "p4_ue"], writes=[f"p4_itb{ib}"])
                p.emit("act", lambda e, nt=nt, tg=tg, ib=ib: e.dma_start(
                    out=d["interT"][nt * 128:(nt + 1) * 128, tg * 512:(tg + 1) * 512], in_=itb[ib][:]),
                    reads=[f"p4_itb{ib}"], writes=[f"interT_{nt}_{tg}"], dma=True)
            if nt + 1 < NTT:
                prep_transpose(nt + 1)
            if nt + 2 < NTT:
                prep_decode(nt + 2)


def phase4_wd(cx, d):
    nc, p = cx.nc, cx.p
    with contextlib.ExitStack() as sst:
        pk = [sst.enter_context(nc.sbuf_tensor(f"p4c_pk{i}", [128, 1024], I32)) for i in range(2)]
        sc = [sst.enter_context(nc.sbuf_tensor(f"p4c_sc{i}", [128, 64], F32)) for i in range(2)]
        it = sst.enter_context(nc.sbuf_tensor("p4c_it", [128, 1024], I32))
        tf = sst.enter_context(nc.sbuf_tensor("p4c_tf", [128, 4 * 1024], BF16))
        it2 = sst.enter_context(nc.sbuf_tensor("p4c_it2", [128, 1024], I32))
        tf2 = sst.enter_context(nc.sbuf_tensor("p4c_tf2", [128, 4 * 1024], BF16))
        wde = [sst.enter_context(nc.sbuf_tensor(f"p4c_wde{i}", [128, 2048], BF16)) for i in range(2)]
        wT = [sst.enter_context(nc.sbuf_tensor(f"p4c_wT{i}", [128, 16 * 128], BF16)) for i in range(2)]
        psb = [sst.enter_context(nc.psum_tensor(f"p4c_psb{i}", [128, 1024], BF16)) for i in range(2)]

        for hb in range(16):
            for wc in range(8):
                bi = wc % 2
                p.emit("sp", lambda e, hb=hb, wc=wc, bi=bi: e.dma_start(
                    out=pk[bi][:], in_=d["pkd"][hb * 128:(hb + 1) * 128, wc * 1024:(wc + 1) * 1024]),
                    writes=[f"p4c_pk{bi}"], dma=True)
                p.emit("sp", lambda e, hb=hb, wc=wc, bi=bi: e.dma_start(
                    out=sc[bi][:], in_=d["sd"][hb * 128:(hb + 1) * 128, wc * 64:(wc + 1) * 64]),
                    writes=[f"p4c_sc{bi}"], dma=True)
                itx, itk = (it, "p4c_it") if bi == 0 else (it2, "p4c_it2")
                tfx, tfk = (tf, "p4c_tf") if bi == 0 else (tf2, "p4c_tf2")
                emit_nvfp4_decode(cx, "dve", pk[bi][:], f"p4c_pk{bi}",
                                  sc[bi][:], f"p4c_sc{bi}", wde[bi][:], f"p4c_wde{bi}",
                                  itx[:], itk, tfx, tfk)
                for kt in range(16):
                    pk2 = f"p4c_psb{bi}"
                    p.emit("pe", lambda e, kt=kt, bi=bi: e.transpose(
                        psb[bi][:, (kt % 8) * 128:(kt % 8 + 1) * 128],
                        wde[bi][:, kt * 128:(kt + 1) * 128], cx.ident_sb[:]),
                        reads=[f"p4c_wde{bi}", "ident_sb"], writes=[pk2])
                    p.emit("act", lambda e, kt=kt, bi=bi: e.copy(
                        wT[bi][:, kt * 128:(kt + 1) * 128], psb[bi][:, (kt % 8) * 128:(kt % 8 + 1) * 128]),
                        reads=[pk2], writes=[f"p4c_wT{bi}"])
                p.emit("act", lambda e, hb=hb, wc=wc, bi=bi: e.dma_start(
                    out=d["wdT"][wc * 2048:(wc + 1) * 2048, hb * 128:(hb + 1) * 128].rearrange(
                        "(i p) c -> p i c", p=128),
                    in_=wT[bi][:].rearrange("p (i c) -> p i c", c=128)),
                    reads=[f"p4c_wT{bi}"], writes=[f"wdT_{hb}_{wc}"], dma=True)


def phase4_down(cx, d):
    nc, p = cx.nc, cx.p
    with contextlib.ExitStack() as sst:
        itF = sst.enter_context(nc.sbuf_tensor("p4d_itF", [128, 128 * 512], BF16))
        wdt = [sst.enter_context(nc.sbuf_tensor(f"p4d_wdt{i}", [128, 512], BF16)) for i in range(6)]
        xo = sst.enter_context(nc.sbuf_tensor("p4d_xo", [128, 4 * 2048], F32))
        x4 = sst.enter_context(nc.sbuf_tensor("p4d_x4", [128, 4 * 2048], F32))
        psf = [sst.enter_context(nc.psum_tensor(f"p4d_ps{i}", [128, 512], F32)) for i in range(4)]

        for tb in range(8):
            for sb8 in range(8):
                p.emit("sp", lambda e, tb=tb, sb8=sb8: e.dma_start(
                    out=itF[:, sb8 * 16 * 512:(sb8 + 1) * 16 * 512].rearrange("p (i t) -> p i t", t=512),
                    in_=d["interT"][sb8 * 2048:(sb8 + 1) * 2048, tb * 512:(tb + 1) * 512].rearrange(
                        "(i p) t -> p i t", p=128)),
                    writes=["p4d_itF"], dma=True)
            p.emit("sp", lambda e, tb=tb: e.dma_start(
                out=xo[:].rearrange("p (j h) -> p j h", h=H),
                in_=d["x3"][tb * 512:(tb + 1) * 512, :].rearrange("(j p) h -> p j h", p=128)),
                writes=["p4d_xo"], dma=True)
            for hc in range(4):
                for nt in range(128):
                    wb = nt % 6
                    wk = f"p4d_wdt{wb}"
                    q = "pool" if nt % 2 else "sp"
                    p.emit(q, lambda e, nt=nt, hc=hc, wb=wb: e.dma_start(
                        out=wdt[wb][:], in_=d["wdT"][nt * 128:(nt + 1) * 128, hc * 512:(hc + 1) * 512]),
                        writes=[wk], dma=True)
                    for ts in range(4):
                        bank, bkey = psf[ts], f"p4d_ps{ts}"
                        p.emit("pe", lambda e, nt=nt, ts=ts, wb=wb, bank=bank: e.matmul(
                            bank[:], itF[:, nt * 512 + ts * 128: nt * 512 + ts * 128 + 128],
                            wdt[wb][:], start=(nt == 0), stop=(nt == 127)),
                            reads=[wk, "p4d_itF"], writes=[bkey])
                for ts in range(4):
                    p.emit("dve", lambda e, ts=ts, hc=hc: e.tensor_add(
                        x4[:, ts * 2048 + hc * 512: ts * 2048 + hc * 512 + 512],
                        psf[ts][:], xo[:, ts * 2048 + hc * 512: ts * 2048 + hc * 512 + 512]),
                        reads=[f"p4d_ps{ts}", "p4d_xo"], writes=["p4d_x4"])
            p.emit("act", lambda e, tb=tb: e.dma_start(
                out=d["x4"][tb * 512:(tb + 1) * 512, :].rearrange("(j p) h -> p j h", p=128),
                in_=x4[:].rearrange("p (j h) -> p j h", h=H)),
                reads=["p4d_x4"], writes=[f"x4_{tb}"], dma=True)



def _prep_inputs(inp):
    unused = (inp["L0_attn_key"], inp["L0_attn_value"])
    del unused
    f = np.ascontiguousarray
    x = np.asarray(inp["L0_attn_query"], dtype=np.float32)
    Wqkv = np.asarray(inp["L0_attn_Wqkv"], dtype=np.float32)
    per_core = []
    shared = {
        "wproj0T": f(np.asarray(inp["L0_attn_Wproj"], np.float32).T),
        "bproj0": f(np.asarray(inp["L0_attn_bproj"], np.float32)[None, :]),
        "w0T": f(np.asarray(inp["L0_ffn_weight"], np.float32).T),
        "gatewT": f(np.asarray(inp["L0_ffn_gate_w"], np.float32).T),
        "A2": f(np.asarray(inp["L0_ffn_A_w"], np.float32).reshape(128, 2048).T),
        "B2": f(np.asarray(inp["L0_ffn_B_w"], np.float32).transpose(0, 2, 1).reshape(128, 2048)),
        "wqkv1T": f(np.asarray(inp["L1_attn_w_qkv"], np.float32).T),
        "wo1T": f(np.asarray(inp["L1_attn_w_o"], np.float32).T),
        "pkg": f(np.asarray(inp["L1_ffn_gate_W_packed"], np.int32)),
        "sg": f(np.asarray(inp["L1_ffn_gate_scales"], np.float32)),
        "pku": f(np.asarray(inp["L1_ffn_up_W_packed"], np.int32)),
        "su": f(np.asarray(inp["L1_ffn_up_scales"], np.float32)),
        "pkd": f(np.asarray(inp["L1_ffn_down_W_packed"], np.int32)),
        "sd": f(np.asarray(inp["L1_ffn_down_scales"], np.float32)),
    }
    xfull = f(x.reshape(B * S, H))
    for c in range(8):
        sub = np.concatenate([Wqkv[384 * c:384 * c + 384], Wqkv[384 * (c + 8):384 * (c + 8) + 384]], axis=0)
        m = dict(shared)
        m["x_full"] = xfull
        m["x_own"] = f(x[c])
        m["wqkv0T"] = f(sub.T)
        per_core.append(m)
    return per_core



IN_SPECS = {
    "x_full": ([B * S, H], F32), "x_own": ([S, H], F32),
    "wqkv0T": ([H, 768], F32), "wproj0T": ([H, H], F32), "bproj0": ([1, H], F32),
    "w0T": ([H, H], F32), "gatewT": ([H, 8], F32), "A2": ([H, 128], F32), "B2": ([128, H], F32),
    "wqkv1T": ([H, 3 * H], F32), "wo1T": ([H, H], F32),
    "pkg": ([MLP, H // 2], I32), "sg": ([MLP, 64], F32),
    "pku": ([MLP, H // 2], I32), "su": ([MLP, 64], F32),
    "pkd": ([H, MLP // 2], I32), "sd": ([H, 512], F32),
}
PHASE_IO = {
    "p1a": (["x_full", "wqkv0T"], [], ["ctxT0"]),
    "p1b": (["x_own", "wproj0T", "bproj0"], ["ctxT0"], ["x1"]),
    "p2": (["w0T", "gatewT", "A2", "B2"], ["x1"], ["x2"]),
    "p3a": (["wqkv1T"], ["x2"], ["qkvT1"]),
    "p3c": (["pkd", "sd"], ["qkvT1"], ["ctxT1", "wdT"]),
    "p3d": (["wo1T"], ["ctxT1", "x2"], ["x3"]),
    "p4a": (["pkg", "sg", "pku", "su"], ["x3"], ["interT"]),
    "p4c": (["pkd", "sd"], [], ["wdT"]),
    "p4d": ([], ["interT", "wdT", "x3"], ["x4"]),
}
PHASE_FN = {}
SCRATCH_SPECS = {
    "ctxT0": ([H, S], BF16), "x1": ([S, H], F32), "x2": ([S, H], F32),
    "x3": ([S, H], F32), "x4": ([S, H], F32),
    "ctxT1": ([H, S], BF16), "xlnT3": ([H, S], BF16), "qkvT1": ([3 * H, S], BF16),
    "interT": ([MLP, S], BF16), "wdT": ([MLP, H], BF16),
}


def build_program(phase_names, out_name):
    nc = bass.Bass("TRN2", target_bir_lowering=False)
    d = {}
    needed_in, needed_scr = set(), set()
    for pn in phase_names:
        ins, rs, ws = PHASE_IO[pn]
        needed_in |= set(ins)
        needed_scr |= set(rs) | set(ws)
    produced = set()
    for pn in phase_names:
        produced |= set(PHASE_IO[pn][2])
    for n in sorted(needed_in):
        shp, dt = IN_SPECS[n]
        d[n] = nc.dram_tensor(n, shp, dt, kind="ExternalInput").ap()
    for n in sorted(needed_scr):
        shp, dt = SCRATCH_SPECS.get(n, ([S, H], F32))
        if n == out_name:
            kind = "ExternalOutput"
        elif n in produced:
            kind = "Internal"
        else:
            kind = "ExternalInput"
            needed_in.add(n)
        d[n] = nc.dram_tensor(n, shp, dt, kind=kind).ap()

    st = contextlib.ExitStack()
    with st:
        prog = Prog(nc)
        cx = Ctx(nc, prog, st)
        prog.barrier()
        for i, pn in enumerate(phase_names):
            PHASE_FN[pn](cx, d)
            if i < len(phase_names) - 1:
                prog.barrier()
        prog.check()
        prog.realize(st)
    return nc, sorted(needed_in)


def run_phases(per_core, phase_names, out_name):
    nc, in_names = build_program(phase_names, out_name)
    in_maps = [{k: m[k] for k in in_names} for m in per_core]
    res = run_bass_kernel_spmd(nc, in_maps, core_ids=list(range(8)))
    return [res.results[c][out_name] for c in range(8)]


ALL_PHASES = ["p1a", "p1b", "p2", "p3a", "p3c", "p3d", "p4a", "p4d"]


def kernel(**inp):
    per_core = _prep_inputs(inp)
    out = run_phases(per_core, ALL_PHASES, "x4")
    return np.stack(out, axis=0)


PHASE_FN.update({"p1a": phase1_qkv_attn, "p1b": phase1_proj, "p2": phase2_ffn0,
                 "p3a": phase3_qkv, "p3c": phase3_attn, "p3d": phase3_proj,
                 "p4a": phase4_gateup, "p4c": phase4_wd, "p4d": phase4_down})


# version 14
# speedup vs baseline: 1.1445x; 1.0427x over previous
import contextlib
import os
import numpy as np
import ml_dtypes
import concourse.bass as bass
import concourse.mybir as mybir
from concourse.bass_utils import run_bass_kernel_spmd

F32 = mybir.dt.float32
BF16 = mybir.dt.bfloat16
I32 = mybir.dt.int32
AF = mybir.ActivationFunctionType
ALU = mybir.AluOpType
AXX = mybir.AxisListType.X

B, S, H = 8, 4096, 2048
NH, HD = 16, 128
MLP = 16384
NORM = 1.0 / np.sqrt(128.0)
EXP_SHIFT = -12.0
LN_EPS = 1e-5


ENGINES = ("pe", "act", "dve", "pool", "sp")
_BASS_ENG = {"pe": "tensor", "act": "scalar", "dve": "vector", "pool": "gpsimd", "sp": "sync"}


class Prog:

    NDMA = 8

    def __init__(self, nc):
        self.nc = nc
        self.items = {e: [] for e in ENGINES}
        self.ccount = {e: 0 for e in ENGINES}
        self.dcount = {e: 0 for e in ENGINES}
        self.dslot = {e: [0] * self.NDMA for e in ENGINES}
        self.regions = {}
        self.sem_names = ([f"cs_{e}" for e in ENGINES]
                          + [f"dq_{e}{j}" for e in ENGINES for j in range(self.NDMA)])

    def _deps(self, eng, reads, writes):
        waits = {}
        def add(tok):
            if tok is None:
                return
            s, v = tok
            if s == f"cs_{eng}" and eng == "pe":
                return
            if waits.get(s, 0) < v:
                waits[s] = v
        for k in reads:
            r = self.regions.get(k)
            if r:
                add(r["w"])
        for k in writes:
            r = self.regions.get(k)
            if r:
                add(r["w"])
                for s, v in r["r"].items():
                    add((s, v))
        return waits

    def _finish(self, eng, reads, writes, tok):
        s, v = tok
        for k in reads:
            r = self.regions.setdefault(k, {"w": None, "r": {}})["r"]
            if r.get(s, 0) < v:
                r[s] = v
        for k in writes:
            self.regions[k] = {"w": tok, "r": {}}

    def emit(self, eng, fn, reads=(), writes=(), dma=False):
        waits = self._deps(eng, reads, writes)
        if dma:
            slot = self.dcount[eng] % self.NDMA
            sem = f"dq_{eng}{slot}"
            prev = 16 * self.dslot[eng][slot]
            if prev and waits.get(sem, 0) < prev:
                waits[sem] = prev
            self.dcount[eng] += 1
            self.dslot[eng][slot] += 1
            inc = 16
            tok = (sem, 16 * self.dslot[eng][slot])
        else:
            self.ccount[eng] += 1
            sem, inc = f"cs_{eng}", 1
            tok = (sem, self.ccount[eng])
        self.items[eng].append((waits, fn, sem, inc))
        self._finish(eng, reads, writes, tok)
        return tok

    def barrier(self):
        arrive = {}
        for e in ENGINES:
            waits = {}
            for j in range(self.NDMA):
                if self.dslot[e][j]:
                    waits[f"dq_{e}{j}"] = 16 * self.dslot[e][j]
            self.ccount[e] += 1
            self.items[e].append((waits, lambda eng: eng.nop(nofuse=True), f"cs_{e}", 1))
            arrive[e] = self.ccount[e]
        for e in ENGINES:
            waits = {f"cs_{f}": arrive[f] for f in ENGINES if f != e}
            self.ccount[e] += 1
            self.items[e].append((waits, lambda eng: eng.nop(nofuse=True), f"cs_{e}", 1))
        bar = {e: (f"cs_{e}", self.ccount[e]) for e in ENGINES}
        for k in self.regions:
            w = self.regions[k]["w"]
            e = None
            if w is not None:
                s = w[0]
                e = s.split("_", 1)[1]
            self.regions[k] = {"w": bar.get(e) if e else None, "r": {}}

    def check(self):
        ptr = {e: 0 for e in ENGINES}
        sem = {n: 0 for n in self.sem_names}
        progress = True
        while progress:
            progress = False
            for e in ENGINES:
                while ptr[e] < len(self.items[e]):
                    waits, fn, s, inc = self.items[e][ptr[e]]
                    if any(sem[k] < v for k, v in waits.items()):
                        break
                    sem[s] += inc
                    ptr[e] += 1
                    progress = True
        stuck = {e: ptr[e] for e in ENGINES if ptr[e] < len(self.items[e])}
        if stuck:
            msgs = []
            for e, i in stuck.items():
                waits = self.items[e][i][0]
                bad = {k: (v, sem[k]) for k, v in waits.items() if sem[k] < v}
                msgs.append(f"{e}@{i}/{len(self.items[e])} waits {bad}")
            raise RuntimeError("DEADLOCK: " + "; ".join(msgs))

    def realize(self, st):
        nc = self.nc
        sems = {n: st.enter_context(nc.semaphore(n)) for n in self.sem_names}
        block = st.enter_context(nc.Block())
        clocks = {e: {} for e in ENGINES}
        waited = {e: set() for e in ENGINES}
        for f in ENGINES:
            for waits, _, _, _ in self.items[f]:
                for s, v in waits.items():
                    if s.startswith("cs_"):
                        waited[s[3:]].add(v)

        def run_engine(e):
            def body(engobj):
                clock = clocks[e]
                cnt = 0
                for waits, fn, sem, inc in self.items[e]:
                    n_emitted = 0
                    for s in sorted(waits):
                        v = waits[s]
                        if clock.get(s, 0) < v:
                            engobj.wait_ge(sems[s], v)
                            clock[s] = v
                            n_emitted += 1
                    if sem == f"cs_{e}":
                        cnt += 1
                        if (n_emitted == 0 and (cnt - 1) in waited[e]
                                and cnt - 1 > clock.get(sem, 0)):
                            engobj.wait_ge(sems[sem], cnt - 1)
                            clock[sem] = cnt - 1
                    inst = fn(engobj)
                    inst.then_inc(sems[sem], inc)
                for j in range(self.NDMA):
                    if self.dslot[e][j]:
                        engobj.wait_ge(sems[f"dq_{e}{j}"], 16 * self.dslot[e][j])
            return body

        for e in ENGINES:
            getattr(block, _BASS_ENG[e])(run_engine(e))



class Ctx:
    def __init__(self, nc, prog, st):
        self.nc, self.p, self.st = nc, prog, st
        ident_np = np.eye(128, dtype=ml_dtypes.bfloat16)
        self.ident = nc.inline_tensor(ident_np, "ident128")
        self.ident_sb = st.enter_context(nc.sbuf_tensor("ident_sb", [128, 128], BF16))
        self.consts = st.enter_context(nc.sbuf_tensor("consts", [128, 8], F32))
        p = prog
        p.emit("sp", lambda e: e.dma_start(out=self.ident_sb[:], in_=self.ident.ap()),
               writes=["ident_sb"], dma=True)
        p.emit("dve", lambda e: e.memset(self.consts[:, 0:1], LN_EPS), writes=["consts"])
        p.emit("dve", lambda e: e.memset(self.consts[:, 1:2], EXP_SHIFT), writes=["consts"])
        p.emit("dve", lambda e: e.memset(self.consts[:, 2:3], 1.0), writes=["consts"])
        p.emit("dve", lambda e: e.memset(self.consts[:, 3:4], -1e30), writes=["consts"])
        p.emit("dve", lambda e: e.memset(self.consts[:, 4:5], 0.5), writes=["consts"])
        p.emit("dve", lambda e: e.memset(self.consts[:, 5:6], 0.7978845608028654), writes=["consts"])
        nc.const_aps.register(F32, 0.5, self.consts[:, 4:5])
        nc.const_aps.register(F32, 0.7978845608028654, self.consts[:, 5:6])
        p.emit("dve", lambda e: e.memset(self.consts[:, 6:7], 0.7978845608028654 * 0.044715), writes=["consts"])
        nc.const_aps.register(F32, 0.7978845608028654 * 0.044715, self.consts[:, 6:7])

    @property
    def eps(self):
        return self.consts[:, 0:1]

    @property
    def eshift(self):
        return self.consts[:, 1:2]

    @property
    def half(self):
        return self.consts[:, 4:5]


def emit_ln_chunk(cx, key_in, xin_ap, xbf, xbf_key, stats, stats_key, trash, ntile=4, hdim=H):
    p = cx.p
    n = ntile
    c = 1.0 / hdim
    for j in range(n):
        tile = xin_ap[:, j * hdim:(j + 1) * hdim]
        p.emit("dve", lambda e, t=tile, j=j: e.reduce_sum(stats[:, j:j + 1], t, axis=AXX),
               reads=[key_in], writes=[stats_key])
        p.emit("act", lambda e, t=tile, j=j: e.activation(
            trash[:, 0:hdim], t, AF.Square, accum_out=stats[:, n + j:n + j + 1]),
            reads=[key_in], writes=[stats_key, "trash"])
    p.emit("dve", lambda e: e.tensor_scalar(stats[:, 2 * n:3 * n], stats[:, 0:n], c, None, ALU.mult),
           reads=[stats_key], writes=[stats_key])
    p.emit("dve", lambda e: e.tensor_scalar(stats[:, 3 * n:4 * n], stats[:, n:2 * n], c, None, ALU.mult),
           reads=[stats_key], writes=[stats_key])
    p.emit("dve", lambda e: e.tensor_mul(stats[:, 0:n], stats[:, 2 * n:3 * n], stats[:, 2 * n:3 * n]),
           reads=[stats_key], writes=[stats_key])
    p.emit("dve", lambda e: e.tensor_sub(stats[:, 3 * n:4 * n], stats[:, 3 * n:4 * n], stats[:, 0:n]),
           reads=[stats_key], writes=[stats_key])
    p.emit("act", lambda e: e.activation(stats[:, 3 * n:4 * n], stats[:, 3 * n:4 * n], AF.Ln, bias=cx.eps),
           reads=[stats_key, "consts"], writes=[stats_key])
    p.emit("act", lambda e: e.activation(stats[:, 3 * n:4 * n], stats[:, 3 * n:4 * n], AF.Exp, scale=-0.5),
           reads=[stats_key], writes=[stats_key])
    for j in range(n):
        tile = xin_ap[:, j * hdim:(j + 1) * hdim]
        p.emit("dve", lambda e, t=tile, j=j: e.tensor_scalar(
            xbf[:, j * hdim:(j + 1) * hdim], t, stats[:, 2 * n + j:2 * n + j + 1],
            stats[:, 3 * n + j:3 * n + j + 1], ALU.subtract, ALU.mult),
            reads=[key_in, stats_key], writes=[xbf_key])


def emit_transpose_chunk(cx, xbf, xbf_key, xlnT, xlnT_key, psbf, psbf_key, ntile=4, nht=16):
    p = cx.p
    tw = ntile * 128
    for ht in range(nht):
        for j in range(ntile):
            src = xbf[:, j * (nht * 128) + ht * 128: j * (nht * 128) + ht * 128 + 128]
            dst = psbf[:, j * 128:(j + 1) * 128]
            p.emit("pe", lambda e, s=src, d=dst: e.transpose(d, s, cx.ident_sb[:]),
                   reads=[xbf_key, "ident_sb"], writes=[psbf_key])
        p.emit("dve", lambda e, ht=ht: e.tensor_copy(xlnT[:, ht * tw:(ht + 1) * tw], psbf[:, 0:tw]),
               reads=[psbf_key], writes=[xlnT_key])



def phase1_qkv_attn(cx, d):
    nc, p, st = cx.nc, cx.p, cx.st
    with contextlib.ExitStack() as sst:
        xin = [sst.enter_context(nc.sbuf_tensor(f"p1_xin{i}", [128, 4 * H], F32)) for i in range(2)]
        xbf = sst.enter_context(nc.sbuf_tensor("p1_xbf", [128, 4 * H], BF16))
        xlnT = [sst.enter_context(nc.sbuf_tensor("p1_xlnT0", [128, 16 * 512], BF16))] * 2
        stats = sst.enter_context(nc.sbuf_tensor("p1_stats", [128, 16], F32))
        trash = sst.enter_context(nc.sbuf_tensor("p1_trash", [128, H], BF16))
        wsb = sst.enter_context(nc.sbuf_tensor("p1_wsb", [128, 16 * 768], BF16))
        qT = [sst.enter_context(nc.sbuf_tensor(f"p1_qT{i}", [128, 4096], BF16)) for i in range(2)]
        kT = [sst.enter_context(nc.sbuf_tensor(f"p1_kT{i}", [128, 4096], BF16)) for i in range(2)]
        vE = [sst.enter_context(nc.sbuf_tensor(f"p1_vE{i}", [128, 32 * 132], BF16)) for i in range(2)]
        vtmp = sst.enter_context(nc.sbuf_tensor("p1_vtmp", [128, 512], BF16))
        expT = sst.enter_context(nc.sbuf_tensor("p1_expT", [128, 32 * 512], BF16))
        ctmp = sst.enter_context(nc.sbuf_tensor("p1_ctmp", [128, 128], BF16))
        cTsb = sst.enter_context(nc.sbuf_tensor("p1_cTsb", [128, 512], BF16))
        rs1 = sst.enter_context(nc.sbuf_tensor("p1_rs1", [128, 1], F32))
        psf = [sst.enter_context(nc.psum_tensor(f"p1_psf{i}", [128, 512], F32)) for i in range(6)]
        psb = [sst.enter_context(nc.psum_tensor(f"p1_psb{i}", [128, 1024], BF16)) for i in range(2)]

        p.emit("pool", lambda e: e.dma_start(
            out=wsb[:].rearrange("p (ht dd) -> p ht dd", dd=768),
            in_=d["wqkv0T"].rearrange("(ht p) dd -> p ht dd", p=128)),
            writes=["p1_wsb"], dma=True)
        for i in range(2):
            p.emit("dve", lambda e, i=i: e.memset(
                vE[i][:].rearrange("p (kt c) -> p kt c", c=132)[:, :, 128:129], 1.0),
                writes=[f"p1_vE{i}_ones"])

        for m in range(B):
            for tc in range(8):
                bi = tc % 2
                xink, xlnk = f"p1_xin{bi}", "p1_xlnT0"
                src = d["x_full"][(m * S + tc * 512):(m * S + (tc + 1) * 512), :]
                p.emit("sp", lambda e, s=src, bi=bi: e.dma_start(
                    out=xin[bi][:].rearrange("p (j h) -> p j h", h=H),
                    in_=s.rearrange("(j p) h -> p j h", p=128)),
                    writes=[xink], dma=True)
                emit_ln_chunk(cx, xink, xin[bi][:], xbf, "p1_xbf", stats, "p1_stats", trash)
                emit_transpose_chunk(cx, xbf, "p1_xbf", xlnT[bi], xlnk, psb[tc % 2], f"p1_psb{tc % 2}")
                for dt in range(6):
                    bank = psf[dt % 2]
                    bkey = f"p1_psf{dt % 2}"
                    for ht in range(16):
                        p.emit("pe", lambda e, dt=dt, ht=ht, bank=bank, bi=bi: e.matmul(
                            bank[:], wsb[:, ht * 768 + dt * 128: ht * 768 + (dt + 1) * 128],
                            xlnT[bi][:, ht * 512:(ht + 1) * 512],
                            start=(ht == 0), stop=(ht == 15)),
                            reads=["p1_wsb", xlnk], writes=[bkey])
                    hp = dt // 3
                    kind = dt % 3
                    if kind == 0:
                        p.emit("dve", lambda e, hp=hp, bank=bank, tc=tc: e.tensor_scalar(
                            qT[hp][:, tc * 512:(tc + 1) * 512], bank[:], NORM, None, ALU.mult),
                            reads=[bkey], writes=[f"p1_qT{hp}"])
                    elif kind == 1:
                        p.emit("dve", lambda e, hp=hp, bank=bank, tc=tc: e.tensor_copy(
                            kT[hp][:, tc * 512:(tc + 1) * 512], bank[:]),
                            reads=[bkey], writes=[f"p1_kT{hp}"])
                    else:
                        p.emit("dve", lambda e, bank=bank: e.tensor_copy(vtmp[:], bank[:]),
                               reads=[bkey], writes=["p1_vtmp"])
                        for q4 in range(4):
                            p.emit("pe", lambda e, q4=q4: e.transpose(
                                psb[1][:, q4 * 128:(q4 + 1) * 128], vtmp[:, q4 * 128:(q4 + 1) * 128],
                                cx.ident_sb[:]),
                                reads=["p1_vtmp", "ident_sb"], writes=["p1_psb1"])
                        p.emit("dve", lambda e, hp=hp, tc=tc: e.tensor_copy(
                            vE[hp][:].rearrange("p (kt c) -> p kt c", c=132)[:, tc * 4:(tc + 1) * 4, 0:128],
                            psb[1][:].rearrange("p (q4 c) -> p q4 c", c=128)[:, 0:4, :]),
                            reads=["p1_psb1"], writes=[f"p1_vE{hp}"])
            for hp in range(2):
                row0 = 256 * m + 128 * hp
                for qc in range(8):
                    for kt in range(32):
                        bank = psf[2 + kt % 2]
                        bkey = f"p1_psf{2 + kt % 2}"
                        p.emit("pe", lambda e, hp=hp, kt=kt, qc=qc, bank=bank: e.matmul(
                            bank[:], kT[hp][:, kt * 128:(kt + 1) * 128],
                            qT[hp][:, qc * 512:(qc + 1) * 512], start=True, stop=True),
                            reads=[f"p1_kT{hp}", f"p1_qT{hp}"], writes=[bkey])
                        p.emit("act", lambda e, kt=kt, bank=bank: e.activation(
                            expT[:, kt * 512:(kt + 1) * 512], bank[:], AF.Exp, bias=cx.eshift),
                            reads=[bkey, "consts"], writes=[f"p1_expT{kt}"])
                    for qs in range(4):
                        bank = psf[4 + qs % 2]
                        bkey = f"p1_psf{4 + qs % 2}"
                        for kt in range(32):
                            p.emit("pe", lambda e, hp=hp, kt=kt, qs=qs, bank=bank: e.matmul(
                                bank[:, 0:129], expT[:, kt * 512 + qs * 128: kt * 512 + qs * 128 + 128],
                                vE[hp][:, kt * 132: kt * 132 + 129],
                                start=(kt == 0), stop=(kt == 31)),
                                reads=[f"p1_expT{kt}", f"p1_vE{hp}", f"p1_vE{hp}_ones"], writes=[bkey])
                        p.emit("act", lambda e, bank=bank: e.activation(rs1[:], bank[:, 128:129], AF.Ln),
                               reads=[bkey], writes=["p1_rs1"])
                        p.emit("act", lambda e: e.activation(rs1[:], rs1[:], AF.Exp, scale=-1.0),
                               reads=["p1_rs1"], writes=["p1_rs1"])
                        p.emit("dve", lambda e, bank=bank: e.tensor_scalar(
                            ctmp[:], bank[:, 0:128], rs1[:], None, ALU.mult),
                            reads=[bkey, "p1_rs1"], writes=["p1_ctmp"])
                        p.emit("pe", lambda e, qs=qs: e.transpose(
                            psb[0][:, qs * 128:(qs + 1) * 128], ctmp[:], cx.ident_sb[:]),
                            reads=["p1_ctmp", "ident_sb"], writes=["p1_psb0"])
                        p.emit("dve", lambda e, qs=qs: e.tensor_copy(
                            cTsb[:, qs * 128:(qs + 1) * 128], psb[0][:, qs * 128:(qs + 1) * 128]),
                            reads=["p1_psb0"], writes=["p1_cTsb"])
                    p.emit("act", lambda e, row0=row0, qc=qc: e.dma_start(
                        out=d["ctxT0"][row0:row0 + 128, qc * 512:(qc + 1) * 512], in_=cTsb[:]),
                        reads=["p1_cTsb"], writes=[f"ctxT0_{row0}_{qc}"], dma=True)


def phase1_proj(cx, d):
    nc, p = cx.nc, cx.p
    with contextlib.ExitStack() as sst:
        wsb = sst.enter_context(nc.sbuf_tensor("p1b_w", [128, 16 * 2048], BF16))
        bbc = sst.enter_context(nc.sbuf_tensor("p1b_bbc", [128, 2048], F32))
        cT = [sst.enter_context(nc.sbuf_tensor(f"p1b_cT{i}", [128, 16 * 512], BF16)) for i in range(2)]
        xo = [sst.enter_context(nc.sbuf_tensor(f"p1b_xo{i}", [128, 4 * 2048], F32)) for i in range(2)]
        x1 = [sst.enter_context(nc.sbuf_tensor("p1b_x10", [128, 4 * 2048], F32))] * 2
        psf = [sst.enter_context(nc.psum_tensor(f"p1b_ps{i}", [128, 512], F32)) for i in range(4)]

        for i in range(4):
            p.emit("pool", lambda e, i=i: e.dma_start(
                out=wsb[:, i * 4 * 2048:(i + 1) * 4 * 2048].rearrange("p (ht o) -> p ht o", o=2048),
                in_=d["wproj0T"][i * 512:(i + 1) * 512, :].rearrange("(ht p) o -> p ht o", p=128)),
                writes=["p1b_w"], dma=True)
        p.emit("sp", lambda e: e.dma_start(
            out=bbc[:], in_=d["bproj0"].partition_broadcast(128)),
            writes=["p1b_bbc"], dma=True)

        for tc in range(8):
            bi = tc % 2
            ck, xok, x1k = f"p1b_cT{bi}", f"p1b_xo{bi}", "p1b_x10"
            p.emit("sp", lambda e, tc=tc, bi=bi: e.dma_start(
                out=cT[bi][:].rearrange("p (ht t) -> p ht t", t=512),
                in_=d["ctxT0"][:, tc * 512:(tc + 1) * 512].rearrange("(ht p) t -> p ht t", p=128)),
                reads=[f"ctxT0_all"], writes=[ck], dma=True)
            p.emit("sp", lambda e, tc=tc, bi=bi: e.dma_start(
                out=xo[bi][:].rearrange("p (j h) -> p j h", h=2048),
                in_=d["x_own"][tc * 512:(tc + 1) * 512, :].rearrange("(j p) h -> p j h", p=128)),
                writes=[xok], dma=True)
            for j in range(4):
                p.emit("dve", lambda e, j=j, bi=bi: e.tensor_add(
                    xo[bi][:, j * 2048:(j + 1) * 2048], xo[bi][:, j * 2048:(j + 1) * 2048], bbc[:]),
                    reads=[xok, "p1b_bbc"], writes=[xok])
            for ts in range(4):
                for oc in range(4):
                    bank = psf[oc]
                    bkey = f"p1b_ps{oc}"
                    for kt in range(16):
                        p.emit("pe", lambda e, ts=ts, oc=oc, kt=kt, bank=bank, bi=bi: e.matmul(
                            bank[:], cT[bi][:, kt * 512 + ts * 128: kt * 512 + ts * 128 + 128],
                            wsb[:, kt * 2048 + oc * 512: kt * 2048 + (oc + 1) * 512],
                            start=(kt == 0), stop=(kt == 15)),
                            reads=[ck, "p1b_w"], writes=[bkey])
                    p.emit("dve", lambda e, ts=ts, oc=oc, bank=bank, bi=bi: e.tensor_add(
                        x1[bi][:, ts * 2048 + oc * 512: ts * 2048 + (oc + 1) * 512],
                        bank[:], xo[bi][:, ts * 2048 + oc * 512: ts * 2048 + (oc + 1) * 512]),
                        reads=[f"p1b_ps{oc}", xok], writes=[x1k])
            p.emit("act", lambda e, tc=tc, bi=bi: e.dma_start(
                out=d["x1"][tc * 512:(tc + 1) * 512, :].rearrange("(j p) h -> p j h", p=128),
                in_=x1[bi][:].rearrange("p (j h) -> p j h", h=2048)),
                reads=[x1k], writes=[f"x1_{tc}"], dma=True)



def phase2_ffn0(cx, d):
    lvl = int(os.environ.get("P2LVL", "4"))
    nc, p = cx.nc, cx.p
    with contextlib.ExitStack() as sst:
        w0sb = sst.enter_context(nc.sbuf_tensor("p2_w0", [128, 16 * 2048], BF16))
        a2sb = sst.enter_context(nc.sbuf_tensor("p2_a2", [128, 16 * 128], BF16))
        b2sb = sst.enter_context(nc.sbuf_tensor("p2_b2", [128, 2048], BF16))
        gwsb = sst.enter_context(nc.sbuf_tensor("p2_gw", [128, 16 * 8], BF16))
        xin = [sst.enter_context(nc.sbuf_tensor(f"p2_xin{i}", [128, 4 * 2048], F32)) for i in range(2)]
        xbf = sst.enter_context(nc.sbuf_tensor("p2_xbf", [128, 4 * 2048], BF16))
        xlnT = sst.enter_context(nc.sbuf_tensor("p2_xlnT", [128, 16 * 512], BF16))
        x2b = sst.enter_context(nc.sbuf_tensor("p2_x2", [128, 4 * 2048], F32))
        stats = sst.enter_context(nc.sbuf_tensor("p2_stats", [128, 16], F32))
        trash = sst.enter_context(nc.sbuf_tensor("p2_trash", [128, H], BF16))
        gl = sst.enter_context(nc.sbuf_tensor("p2_gl", [128, 8], F32))
        el = sst.enter_context(nc.sbuf_tensor("p2_el", [128, 8], F32))
        sel = sst.enter_context(nc.sbuf_tensor("p2_sel", [128, 8], F32))
        msk = sst.enter_context(nc.sbuf_tensor("p2_msk", [128, 8], F32))
        cw = sst.enter_context(nc.sbuf_tensor("p2_cw", [128, 8], F32))
        mm = sst.enter_context(nc.sbuf_tensor("p2_mm", [128, 4], F32))
        tmw = sst.enter_context(nc.sbuf_tensor("p2_tmw", [128, 128], BF16))
        tcwT = sst.enter_context(nc.sbuf_tensor("p2_tcwT", [128, 4 * 128], BF16))
        psf = [sst.enter_context(nc.psum_tensor(f"p2_psf{i}", [128, 512], F32)) for i in range(6)]
        psb = [sst.enter_context(nc.psum_tensor(f"p2_psb{i}", [128, 1024], BF16)) for i in range(2)]

        for i in range(4):
            p.emit("pool", lambda e, i=i: e.dma_start(
                out=w0sb[:, i * 4 * 2048:(i + 1) * 4 * 2048].rearrange("p (ht o) -> p ht o", o=2048),
                in_=d["w0T"][i * 512:(i + 1) * 512, :].rearrange("(ht p) o -> p ht o", p=128)),
                writes=["p2_w0"], dma=True)
        p.emit("pool", lambda e: e.dma_start(
            out=a2sb[:].rearrange("p (ht c) -> p ht c", c=128),
            in_=d["A2"].rearrange("(ht p) c -> p ht c", p=128)),
            writes=["p2_a2"], dma=True)
        p.emit("pool", lambda e: e.dma_start(out=b2sb[:], in_=d["B2"]),
               writes=["p2_b2"], dma=True)
        p.emit("pool", lambda e: e.dma_start(
            out=gwsb[:].rearrange("p (ht c) -> p ht c", c=8),
            in_=d["gatewT"].rearrange("(ht p) c -> p ht c", p=128)),
            writes=["p2_gw"], dma=True)

        for tc in range(int(os.environ.get("P2TC", "8"))):
            bi = tc % 2
            xink = f"p2_xin{bi}"
            p.emit("sp", lambda e, tc=tc, bi=bi: e.dma_start(
                out=xin[bi][:].rearrange("p (j h) -> p j h", h=H),
                in_=d["x1"][tc * 512:(tc + 1) * 512, :].rearrange("(j p) h -> p j h", p=128)),
                writes=[xink], dma=True)
            emit_ln_chunk(cx, xink, xin[bi][:], xbf, "p2_xbf", stats, "p2_stats", trash)
            emit_transpose_chunk(cx, xbf, "p2_xbf", xlnT, "p2_xlnT", psb[0], "p2_psb0")
            for ts in range(4):
              if lvl >= 3:
                bank, bkey = psf[4], "p2_psf4"
                for kt in range(16):
                    p.emit("pe", lambda e, ts=ts, kt=kt, bank=bank: e.matmul(
                        bank[:, 0:8], xlnT[:, kt * 512 + ts * 128: kt * 512 + ts * 128 + 128],
                        gwsb[:, kt * 8:(kt + 1) * 8], start=(kt == 0), stop=(kt == 15)),
                        reads=["p2_xlnT", "p2_gw"], writes=[bkey])
                p.emit("dve", lambda e, bank=bank: e.tensor_copy(gl[:], bank[:, 0:8]),
                       reads=[bkey], writes=["p2_gl"])
                p.emit("dve", lambda e: e.reduce_max(mm[:, 0:1], gl[:], axis=AXX, negate=True),
                       reads=["p2_gl"], writes=["p2_mm"])
                p.emit("act", lambda e: e.activation(el[:], gl[:], AF.Exp, bias=mm[:, 0:1]),
                       reads=["p2_gl", "p2_mm"], writes=["p2_el"])
                p.emit("dve", lambda e: e.reduce_max(mm[:, 0:1], el[:], axis=AXX),
                       reads=["p2_el"], writes=["p2_mm"])
                p.emit("dve", lambda e: e.tensor_scalar(msk[:], el[:], mm[:, 0:1], None, ALU.is_equal),
                       reads=["p2_el", "p2_mm"], writes=["p2_msk"])
                p.emit("dve", lambda e: e.scalar_tensor_tensor(sel[:], msk[:], -2.0, el[:], ALU.mult, ALU.add),
                       reads=["p2_msk", "p2_el"], writes=["p2_sel"])
                p.emit("dve", lambda e: e.reduce_max(mm[:, 1:2], sel[:], axis=AXX),
                       reads=["p2_sel"], writes=["p2_mm"])
                p.emit("dve", lambda e: e.tensor_add(mm[:, 2:3], mm[:, 0:1], mm[:, 1:2]),
                       reads=["p2_mm"], writes=["p2_mm"])
                p.emit("act", lambda e: e.activation(mm[:, 3:4], mm[:, 2:3], AF.Ln),
                       reads=["p2_mm"], writes=["p2_mm"])
                p.emit("act", lambda e: e.activation(mm[:, 3:4], mm[:, 3:4], AF.Exp, scale=-1.0),
                       reads=["p2_mm"], writes=["p2_mm"])
                p.emit("dve", lambda e: e.tensor_scalar(msk[:], el[:], mm[:, 1:2], None, ALU.is_ge),
                       reads=["p2_el", "p2_mm"], writes=["p2_msk"])
                p.emit("dve", lambda e: e.tensor_mul(sel[:], el[:], msk[:]),
                       reads=["p2_el", "p2_msk"], writes=["p2_sel"])
                p.emit("dve", lambda e: e.tensor_scalar(cw[:], sel[:], mm[:, 3:4], 16.0, ALU.mult, ALU.mult),
                       reads=["p2_sel", "p2_mm"], writes=["p2_cw"])
              if lvl >= 4:
                bank2, bkey2 = psf[5], "p2_psf5"
                for kt in range(16):
                    p.emit("pe", lambda e, ts=ts, kt=kt, bank=bank2: e.matmul(
                        bank[:, 0:128],
                        a2sb[:, kt * 128:(kt + 1) * 128],
                        xlnT[:, kt * 512 + ts * 128: kt * 512 + ts * 128 + 128],
                        start=(kt == 0), stop=(kt == 15)),
                        reads=["p2_a2", "p2_xlnT"], writes=[bkey2])
                p.emit("act", lambda e, bank=bank2: e.copy(tmw[:], bank[:, 0:128]),
                       reads=[bkey2], writes=["p2_tmw"])
                p.emit("pe", lambda e: e.transpose(psb[1][:, 0:128], tmw[:], cx.ident_sb[:]),
                       reads=["p2_tmw", "ident_sb"], writes=["p2_psb1"])
                p.emit("dve", lambda e: e.tensor_copy(tmw[:], psb[1][:, 0:128]),
                       reads=["p2_psb1"], writes=["p2_tmw"])
                for ex in range(8):
                    p.emit("dve", lambda e, ex=ex: e.tensor_scalar(
                        tmw[:, ex * 16:(ex + 1) * 16], tmw[:, ex * 16:(ex + 1) * 16],
                        cw[:, ex:ex + 1], None, ALU.mult),
                        reads=["p2_tmw", "p2_cw"], writes=["p2_tmw"])
                p.emit("pe", lambda e: e.transpose(psb[1][:, 128:256], tmw[:], cx.ident_sb[:]),
                       reads=["p2_tmw", "ident_sb"], writes=["p2_psb1"])
                p.emit("dve", lambda e, ts=ts: e.tensor_copy(tcwT[:, ts * 128:(ts + 1) * 128], psb[1][:, 128:256]),
                       reads=["p2_psb1"], writes=["p2_tcwT"])
              if lvl >= 2:
                for oc in range(4):
                    bank3 = psf[oc]
                    bkey3 = f"p2_psf{oc}"
                    for kt in range(16 if lvl >= 4 else 15):
                        p.emit("pe", lambda e, ts=ts, oc=oc, kt=kt, bank=bank3: e.matmul(
                            bank[:],
                            xlnT[:, kt * 512 + ts * 128: kt * 512 + ts * 128 + 128],
                            w0sb[:, kt * 2048 + oc * 512: kt * 2048 + (oc + 1) * 512],
                            start=(kt == 0), stop=False),
                            reads=["p2_xlnT", "p2_w0"], writes=[bkey3])
                    if lvl >= 4:
                        p.emit("pe", lambda e, ts=ts, oc=oc, bank=bank3: e.matmul(
                            bank[:], tcwT[:, ts * 128:(ts + 1) * 128],
                            b2sb[:, oc * 512:(oc + 1) * 512], start=False, stop=True),
                            reads=["p2_tcwT", "p2_b2"], writes=[bkey3])
                    else:
                        p.emit("pe", lambda e, ts=ts, oc=oc, kt=15, bank=bank3: e.matmul(
                            bank[:],
                            xlnT[:, kt * 512 + ts * 128: kt * 512 + ts * 128 + 128],
                            w0sb[:, kt * 2048 + oc * 512: kt * 2048 + (oc + 1) * 512],
                            start=False, stop=True),
                            reads=["p2_xlnT", "p2_w0"], writes=[bkey3])
                    ev = int(os.environ.get("P2EVAC", "1"))
                    if (lvl >= 3 or ev == 1):
                        p.emit("dve", lambda e, ts=ts, oc=oc, bank=bank3, bi=bi: e.tensor_add(
                            x2b[:, ts * 2048 + oc * 512: ts * 2048 + (oc + 1) * 512],
                            bank[:], xin[bi][:, ts * 2048 + oc * 512: ts * 2048 + (oc + 1) * 512]),
                            reads=[bkey3, xink], writes=["p2_x2"])
                    elif ev == 2:
                        p.emit("dve", lambda e, ts=ts, oc=oc, bank=bank3: e.tensor_copy(
                            x2b[:, ts * 2048 + oc * 512: ts * 2048 + (oc + 1) * 512],
                            bank[:]),
                            reads=[bkey3], writes=["p2_x2"])
                    else:
                        p.emit("dve", lambda e, ts=ts, oc=oc, bank=bank3, bi=bi: e.tensor_copy(
                            x2b[:, ts * 2048 + oc * 512: ts * 2048 + (oc + 1) * 512],
                            xin[bi][:, ts * 2048 + oc * 512: ts * 2048 + (oc + 1) * 512]),
                            reads=[bkey3, xink], writes=["p2_x2"])
              if lvl < 2:
                for j in range(4):
                    p.emit("dve", lambda e, j=j, bi=bi: e.tensor_copy(
                        x2b[:, j * 2048:(j + 1) * 2048], xin[bi][:, j * 2048:(j + 1) * 2048]),
                        reads=[xink], writes=["p2_x2"])
            p.emit("act", lambda e, tc=tc: e.dma_start(
                out=d["x2"][tc * 512:(tc + 1) * 512, :].rearrange("(j p) h -> p j h", p=128),
                in_=x2b[:].rearrange("p (j h) -> p j h", h=H)),
                reads=["p2_x2"], writes=[f"x2_{tc}"], dma=True)



def phase3_qkv(cx, d):
    nc, p = cx.nc, cx.p
    with contextlib.ExitStack() as sst:
        xlnF = sst.enter_context(nc.sbuf_tensor("p3_xlnF", [128, 16 * 4096], BF16))
        xin = [sst.enter_context(nc.sbuf_tensor("p3_xin0", [128, 4 * 2048], F32))] * 2
        xbf = sst.enter_context(nc.sbuf_tensor("p3_xbf", [128, 4 * 2048], BF16))
        stats = sst.enter_context(nc.sbuf_tensor("p3_stats", [128, 16], F32))
        trash = sst.enter_context(nc.sbuf_tensor("p3_trash", [128, H], BF16))
        wbuf = [sst.enter_context(nc.sbuf_tensor(f"p3_w{i}", [128, 16 * 128], BF16)) for i in range(2)]
        qbuf = [sst.enter_context(nc.sbuf_tensor(f"p3_q{i}", [128, 512], BF16)) for i in range(2)]
        psf = [sst.enter_context(nc.psum_tensor(f"p3_psf{i}", [128, 512], F32)) for i in range(2)]
        psb = [sst.enter_context(nc.psum_tensor(f"p3_psb{i}", [128, 1024], BF16)) for i in range(2)]

        for tc in range(8):
            bi = 0
            xink = "p3_xin0"
            p.emit("sp", lambda e, tc=tc, bi=bi: e.dma_start(
                out=xin[bi][:].rearrange("p (j h) -> p j h", h=H),
                in_=d["x2"][tc * 512:(tc + 1) * 512, :].rearrange("(j p) h -> p j h", p=128)),
                writes=[xink], dma=True)
            emit_ln_chunk(cx, xink, xin[bi][:], xbf, "p3_xbf", stats, "p3_stats", trash)
            for ht in range(16):
                pk = f"p3_psb{ht % 2}"
                for j in range(4):
                    src = xbf[:, j * 2048 + ht * 128: j * 2048 + ht * 128 + 128]
                    p.emit("pe", lambda e, s=src, j=j, ht=ht: e.transpose(
                        psb[ht % 2][:, j * 128:(j + 1) * 128], s, cx.ident_sb[:]),
                        reads=["p3_xbf", "ident_sb"], writes=[pk])
                p.emit("dve", lambda e, ht=ht, tc=tc: e.tensor_copy(
                    xlnF[:, ht * 4096 + tc * 512: ht * 4096 + tc * 512 + 512], psb[ht % 2][:, 0:512]),
                    reads=[pk], writes=["p3_xlnF"])
        for dt in range(48):
            wb = dt % 2
            wk = f"p3_w{wb}"
            p.emit("pool", lambda e, dt=dt, wb=wb: e.dma_start(
                out=wbuf[wb][:].rearrange("p (ht c) -> p ht c", c=128),
                in_=d["wqkv1T"][:, dt * 128:(dt + 1) * 128].rearrange("(ht p) c -> p ht c", p=128)),
                writes=[wk], dma=True)
            for tg in range(8):
                bank, bkey = psf[tg % 2], f"p3_psf{tg % 2}"
                for kt in range(16):
                    p.emit("pe", lambda e, dt=dt, tg=tg, kt=kt, bank=bank, wb=wb: e.matmul(
                        bank[:], wbuf[wb][:, kt * 128:(kt + 1) * 128],
                        xlnF[:, kt * 4096 + tg * 512: kt * 4096 + (tg) * 512 + 512],
                        start=(kt == 0), stop=(kt == 15)),
                        reads=[wk, "p3_xlnF"], writes=[bkey])
                qb = tg % 2
                qk = f"p3_q{qb}"
                if dt < 16:
                    p.emit("dve", lambda e, bank=bank, qb=qb: e.tensor_scalar(
                        qbuf[qb][:], bank[:], NORM, None, ALU.mult), reads=[bkey], writes=[qk])
                else:
                    p.emit("dve", lambda e, bank=bank, qb=qb: e.tensor_copy(
                        qbuf[qb][:], bank[:]), reads=[bkey], writes=[qk])
                p.emit("act", lambda e, dt=dt, tg=tg, qb=qb: e.dma_start(
                    out=d["qkvT1"][dt * 128:(dt + 1) * 128, tg * 512:(tg + 1) * 512], in_=qbuf[qb][:]),
                    reads=[qk], writes=[f"qkvT1_{dt}_{tg}"], dma=True)


_CAUSAL_MASK = None


def _causal_mask_np():
    global _CAUSAL_MASK
    if _CAUSAL_MASK is None:
        m = np.zeros((128, 4 * 512), np.float32)
        for dlt in range(4):
            kj = np.arange(128)[:, None]
            qj = np.arange(512)[None, :]
            m[:, dlt * 512:(dlt + 1) * 512] = np.where(128 * dlt + kj <= qj, 0.0, -1e30)
        _CAUSAL_MASK = m
    return _CAUSAL_MASK


def phase3_attn(cx, d):
    nc, p = cx.nc, cx.p
    with contextlib.ExitStack() as sst:
        qT = sst.enter_context(nc.sbuf_tensor("p3c_qT", [128, 4096], BF16))
        kT = sst.enter_context(nc.sbuf_tensor("p3c_kT", [128, 4096], BF16))
        vT = sst.enter_context(nc.sbuf_tensor("p3c_vT", [128, 4096], BF16))
        vE = sst.enter_context(nc.sbuf_tensor("p3c_vE", [128, 32 * 132], BF16))
        expT = sst.enter_context(nc.sbuf_tensor("p3c_expT", [128, 32 * 512], BF16))
        msksb = sst.enter_context(nc.sbuf_tensor("p3c_msk", [128, 4 * 512], F32))
        ctmp = sst.enter_context(nc.sbuf_tensor("p3c_ctmp", [128, 128], BF16))
        cTsb = sst.enter_context(nc.sbuf_tensor("p3c_cTsb", [128, 512], BF16))
        rs1 = sst.enter_context(nc.sbuf_tensor("p3c_rs1", [128, 1], F32))
        psf = [sst.enter_context(nc.psum_tensor(f"p3c_psf{i}", [128, 512], F32)) for i in range(4)]
        psb = [sst.enter_context(nc.psum_tensor(f"p3c_psb{i}", [128, 1024], BF16)) for i in range(3)]

        mask_t = nc.inline_tensor(_causal_mask_np(), "causal_mask")
        p.emit("sp", lambda e: e.dma_start(out=msksb[:], in_=mask_t.ap()),
               writes=["p3c_msk"], dma=True)
        p.emit("dve", lambda e: e.memset(
            vE[:].rearrange("p (kt c) -> p kt c", c=132)[:, :, 128:129], 1.0),
            writes=["p3c_vE_ones"])

        for hd in range(16):
            p.emit("sp", lambda e, hd=hd: e.dma_start(
                out=qT[:], in_=d["qkvT1"][hd * 128:(hd + 1) * 128, :]),
                writes=["p3c_qT"], dma=True)
            p.emit("sp", lambda e, hd=hd: e.dma_start(
                out=kT[:], in_=d["qkvT1"][2048 + hd * 128: 2048 + (hd + 1) * 128, :]),
                writes=["p3c_kT"], dma=True)
            p.emit("sp", lambda e, hd=hd: e.dma_start(
                out=vT[:], in_=d["qkvT1"][4096 + hd * 128: 4096 + (hd + 1) * 128, :]),
                writes=["p3c_vT"], dma=True)
            for g4 in range(8):
                pk = f"p3c_psb{g4 % 2}"
                for q4 in range(4):
                    p.emit("pe", lambda e, g4=g4, q4=q4: e.transpose(
                        psb[g4 % 2][:, q4 * 128:(q4 + 1) * 128],
                        vT[:, g4 * 512 + q4 * 128: g4 * 512 + q4 * 128 + 128], cx.ident_sb[:]),
                        reads=["p3c_vT", "ident_sb"], writes=[pk])
                p.emit("dve", lambda e, g4=g4: e.tensor_copy(
                    vE[:].rearrange("p (kt c) -> p kt c", c=132)[:, g4 * 4:(g4 + 1) * 4, 0:128],
                    psb[g4 % 2][:].rearrange("p (q4 c) -> p q4 c", c=128)[:, 0:4, :]),
                    reads=[pk], writes=["p3c_vE"])
            row0 = hd * 128
            for qc in range(8):
                nkt = 4 * qc + 4
                for kt in range(nkt):
                    bank, bkey = psf[2 + kt % 2], f"p3c_psf{2 + kt % 2}"
                    p.emit("pe", lambda e, kt=kt, qc=qc, bank=bank: e.matmul(
                        bank[:], kT[:, kt * 128:(kt + 1) * 128],
                        qT[:, qc * 512:(qc + 1) * 512], start=True, stop=True),
                        reads=["p3c_kT", "p3c_qT"], writes=[bkey])
                    if kt >= 4 * qc:
                        dlt = kt - 4 * qc
                        p.emit("dve", lambda e, bank=bank, dlt=dlt: e.tensor_add(
                            bank[:], bank[:], msksb[:, dlt * 512:(dlt + 1) * 512]),
                            reads=[bkey, "p3c_msk"], writes=[bkey])
                    p.emit("act", lambda e, kt=kt, bank=bank: e.activation(
                        expT[:, kt * 512:(kt + 1) * 512], bank[:], AF.Exp, bias=cx.eshift),
                        reads=[bkey, "consts"], writes=[f"p3c_expT{kt}"])
                for qs in range(4):
                    bank, bkey = psf[qs % 2], f"p3c_psf{qs % 2}"
                    nk2 = 4 * qc + qs + 1
                    for kt in range(nk2):
                        p.emit("pe", lambda e, kt=kt, qs=qs, bank=bank, nk2=nk2: e.matmul(
                            bank[:, 0:129], expT[:, kt * 512 + qs * 128: kt * 512 + qs * 128 + 128],
                            vE[:, kt * 132: kt * 132 + 129],
                            start=(kt == 0), stop=(kt == nk2 - 1)),
                            reads=[f"p3c_expT{kt}", "p3c_vE", "p3c_vE_ones"], writes=[bkey])
                    p.emit("act", lambda e, bank=bank: e.activation(rs1[:], bank[:, 128:129], AF.Ln),
                           reads=[bkey], writes=["p3c_rs1"])
                    p.emit("act", lambda e: e.activation(rs1[:], rs1[:], AF.Exp, scale=-1.0),
                           reads=["p3c_rs1"], writes=["p3c_rs1"])
                    p.emit("dve", lambda e, bank=bank: e.tensor_scalar(
                        ctmp[:], bank[:, 0:128], rs1[:], None, ALU.mult),
                        reads=[bkey, "p3c_rs1"], writes=["p3c_ctmp"])
                    p.emit("pe", lambda e, qs=qs: e.transpose(
                        psb[2][:, qs * 128:(qs + 1) * 128], ctmp[:], cx.ident_sb[:]),
                        reads=["p3c_ctmp", "ident_sb"], writes=["p3c_psb2"])
                    p.emit("dve", lambda e, qs=qs: e.tensor_copy(
                        cTsb[:, qs * 128:(qs + 1) * 128], psb[2][:, qs * 128:(qs + 1) * 128]),
                        reads=["p3c_psb2"], writes=["p3c_cTsb"])
                p.emit("act", lambda e, row0=row0, qc=qc: e.dma_start(
                    out=d["ctxT1"][row0:row0 + 128, qc * 512:(qc + 1) * 512], in_=cTsb[:]),
                    reads=["p3c_cTsb"], writes=[f"ctxT1_{row0}_{qc}"], dma=True)


def phase3_proj(cx, d):
    nc, p = cx.nc, cx.p
    with contextlib.ExitStack() as sst:
        wsb = sst.enter_context(nc.sbuf_tensor("p3d_w", [128, 16 * 2048], BF16))
        cT = [sst.enter_context(nc.sbuf_tensor(f"p3d_cT{i}", [128, 16 * 512], BF16)) for i in range(2)]
        xo = [sst.enter_context(nc.sbuf_tensor(f"p3d_xo{i}", [128, 4 * 2048], F32)) for i in range(2)]
        x1 = [sst.enter_context(nc.sbuf_tensor("p3d_x10", [128, 4 * 2048], F32))] * 2
        psf = [sst.enter_context(nc.psum_tensor(f"p3d_ps{i}", [128, 512], F32)) for i in range(4)]

        for i in range(4):
            p.emit("pool", lambda e, i=i: e.dma_start(
                out=wsb[:, i * 4 * 2048:(i + 1) * 4 * 2048].rearrange("p (ht o) -> p ht o", o=2048),
                in_=d["wo1T"][i * 512:(i + 1) * 512, :].rearrange("(ht p) o -> p ht o", p=128)),
                writes=["p3d_w"], dma=True)
        for tc in range(8):
            bi = tc % 2
            ck, xok, x1k = f"p3d_cT{bi}", f"p3d_xo{bi}", "p3d_x10"
            p.emit("sp", lambda e, tc=tc, bi=bi: e.dma_start(
                out=cT[bi][:].rearrange("p (ht t) -> p ht t", t=512),
                in_=d["ctxT1"][:, tc * 512:(tc + 1) * 512].rearrange("(ht p) t -> p ht t", p=128)),
                writes=[ck], dma=True)
            p.emit("sp", lambda e, tc=tc, bi=bi: e.dma_start(
                out=xo[bi][:].rearrange("p (j h) -> p j h", h=2048),
                in_=d["x2"][tc * 512:(tc + 1) * 512, :].rearrange("(j p) h -> p j h", p=128)),
                writes=[xok], dma=True)
            for ts in range(4):
                for oc in range(4):
                    bank = psf[oc]
                    bkey = f"p3d_ps{oc}"
                    for kt in range(16):
                        p.emit("pe", lambda e, ts=ts, oc=oc, kt=kt, bank=bank, bi=bi: e.matmul(
                            bank[:], cT[bi][:, kt * 512 + ts * 128: kt * 512 + ts * 128 + 128],
                            wsb[:, kt * 2048 + oc * 512: kt * 2048 + (oc + 1) * 512],
                            start=(kt == 0), stop=(kt == 15)),
                            reads=[ck, "p3d_w"], writes=[bkey])
                    p.emit("dve", lambda e, ts=ts, oc=oc, bank=bank, bi=bi: e.tensor_add(
                        x1[bi][:, ts * 2048 + oc * 512: ts * 2048 + (oc + 1) * 512],
                        bank[:], xo[bi][:, ts * 2048 + oc * 512: ts * 2048 + (oc + 1) * 512]),
                        reads=[f"p3d_ps{oc}", xok], writes=[x1k])
            p.emit("act", lambda e, tc=tc, bi=bi: e.dma_start(
                out=d["x3"][tc * 512:(tc + 1) * 512, :].rearrange("(j p) h -> p j h", p=128),
                in_=x1[bi][:].rearrange("p (j h) -> p j h", h=2048)),
                reads=[x1k], writes=[f"x3_{tc}"], dma=True)




def emit_nvfp4_decode_pool(cx, pk_ap, pk_key, sc_ap, sc_key, out_ap, out_key,
                           it_ap, it_key, tf, tf_key):
    p = cx.p
    sc_b = sc_ap.rearrange("p (b one) -> p b one", one=1).broadcast_to([128, 64, 16])
    cf, st_, rr, vv = (tf[:, i * 1024:(i + 1) * 1024] for i in range(4))
    for which in range(2):
        if which == 0:
            p.emit("dve", lambda e, pk_ap=pk_ap, it_ap=it_ap: e.tensor_scalar(
                it_ap, pk_ap, 15, None, ALU.bitwise_and), reads=[pk_key], writes=[it_key])
        else:
            p.emit("dve", lambda e, pk_ap=pk_ap, it_ap=it_ap: e.tensor_scalar(
                it_ap, pk_ap, 4, 15, ALU.logical_shift_right, ALU.bitwise_and),
                reads=[pk_key], writes=[it_key])
        E = "pool"
        p.emit(E, lambda e, a=cf, b=it_ap: e.tensor_copy(a[:], b), reads=[it_key], writes=[tf_key])
        p.emit(E, lambda e, a=rr, b=cf: e.tensor_scalar(a[:], b[:], -7.0, 0.0, ALU.add, ALU.max),
               reads=[tf_key], writes=[tf_key])
        p.emit(E, lambda e, a=st_, b=rr: e.tensor_scalar(a[:], b[:], 1.0, None, ALU.min),
               reads=[tf_key], writes=[tf_key])
        p.emit(E, lambda e, a=rr, b=st_: e.tensor_scalar(a[:], b[:], -8.0, None, ALU.mult),
               reads=[tf_key], writes=[tf_key])
        p.emit(E, lambda e, a=cf, b=rr: e.tensor_add(a[:], a[:], b[:]),
               reads=[tf_key], writes=[tf_key])
        p.emit(E, lambda e, a=rr, b=cf: e.tensor_scalar(a[:], b[:], -4.0, 0.0, ALU.add, ALU.max),
               reads=[tf_key], writes=[tf_key])
        p.emit(E, lambda e, a=vv, b=cf: e.tensor_scalar(a[:], b[:], -6.0, 0.0, ALU.add, ALU.max),
               reads=[tf_key], writes=[tf_key])
        p.emit(E, lambda e, a=rr: e.tensor_scalar(a[:], a[:], 0.5, None, ALU.mult),
               reads=[tf_key], writes=[tf_key])
        p.emit(E, lambda e, a=rr, b=vv: e.tensor_add(a[:], a[:], b[:]),
               reads=[tf_key], writes=[tf_key])
        p.emit(E, lambda e, a=cf: e.tensor_scalar(a[:], a[:], 0.5, None, ALU.mult),
               reads=[tf_key], writes=[tf_key])
        p.emit(E, lambda e, a=rr, b=cf: e.tensor_add(a[:], a[:], b[:]),
               reads=[tf_key], writes=[tf_key])
        p.emit(E, lambda e, a=st_: e.tensor_scalar(a[:], a[:], -2.0, 1.0, ALU.mult, ALU.add),
               reads=[tf_key], writes=[tf_key])
        p.emit(E, lambda e, a=vv, b=rr, c=st_: e.tensor_mul(a[:], b[:], c[:]),
               reads=[tf_key], writes=[tf_key])
        p.emit(E, lambda e, w=which, a=vv, scb=sc_b, o=out_ap: e.tensor_tensor(
            o[:, w:2048:2].rearrange("p (b k) -> p b k", k=16),
            a[:].rearrange("p (b k) -> p b k", k=16), scb, ALU.mult),
            reads=[tf_key, sc_key], writes=[out_key])


def emit_nvfp4_decode(cx, eng, pk_ap, pk_key, sc_ap, sc_key, out_ap, out_key,
                      it_ap, it_key, tf, tf_key):
    p = cx.p
    sc_b = sc_ap.rearrange("p (b one) -> p b one", one=1).broadcast_to([128, 64, 16])
    cf, st_, rr, vv = (tf[:, i * 1024:(i + 1) * 1024] for i in range(4))
    for which in range(2):
        if which == 0:
            p.emit(eng, lambda e, pk_ap=pk_ap, it_ap=it_ap: e.tensor_scalar(
                it_ap, pk_ap, 15, None, ALU.bitwise_and), reads=[pk_key], writes=[it_key])
        else:
            p.emit(eng, lambda e, pk_ap=pk_ap, it_ap=it_ap: e.tensor_scalar(
                it_ap, pk_ap, 4, 15, ALU.logical_shift_right, ALU.bitwise_and),
                reads=[pk_key], writes=[it_key])
        p.emit(eng, lambda e, a=cf, b=it_ap: e.tensor_copy(a[:], b), reads=[it_key], writes=[tf_key])
        p.emit(eng, lambda e, a=st_, b=cf: e.tensor_scalar(a[:], b[:], 7.5, None, ALU.is_ge),
               reads=[tf_key], writes=[tf_key])
        p.emit(eng, lambda e, a=cf, b=st_: e.scalar_tensor_tensor(a[:], b[:], -8.0, a[:], ALU.mult, ALU.add),
               reads=[tf_key], writes=[tf_key])
        p.emit(eng, lambda e, a=rr, b=cf: e.tensor_scalar(a[:], b[:], -4.0, 0.0, ALU.add, ALU.max),
               reads=[tf_key], writes=[tf_key])
        p.emit(eng, lambda e, a=vv, b=cf: e.tensor_scalar(a[:], b[:], -6.0, 0.0, ALU.add, ALU.max),
               reads=[tf_key], writes=[tf_key])
        p.emit(eng, lambda e, a=rr, b=vv: e.scalar_tensor_tensor(a[:], a[:], 0.5, b[:], ALU.mult, ALU.add),
               reads=[tf_key], writes=[tf_key])
        p.emit(eng, lambda e, a=rr, b=cf: e.scalar_tensor_tensor(a[:], b[:], 0.5, a[:], ALU.mult, ALU.add),
               reads=[tf_key], writes=[tf_key])
        p.emit(eng, lambda e, a=st_: e.tensor_scalar(a[:], a[:], -2.0, 1.0, ALU.mult, ALU.add),
               reads=[tf_key], writes=[tf_key])
        p.emit(eng, lambda e, a=vv, b=rr, c=st_: e.tensor_mul(a[:], b[:], c[:]),
               reads=[tf_key], writes=[tf_key])
        p.emit(eng, lambda e, w=which, a=vv, scb=sc_b, o=out_ap: e.tensor_tensor(
            o[:, w:2048:2].rearrange("p (b k) -> p b k", k=16),
            a[:].rearrange("p (b k) -> p b k", k=16), scb, ALU.mult),
            reads=[tf_key, sc_key], writes=[out_key])


def phase4_gateup(cx, d):
    nc, p = cx.nc, cx.p
    with contextlib.ExitStack() as sst:
        xlnF = sst.enter_context(nc.sbuf_tensor("p4_xlnF", [128, 16 * 4096], BF16))
        psbln = [sst.enter_context(nc.psum_tensor(f"p4_psbln{i}", [128, 1024], BF16)) for i in range(2)]
        with contextlib.ExitStack() as lnst:
            xin = [lnst.enter_context(nc.sbuf_tensor("p4_xin0", [128, 4 * 2048], F32))] * 2
            xbf = lnst.enter_context(nc.sbuf_tensor("p4_xbf", [128, 4 * 2048], BF16))
            stats = lnst.enter_context(nc.sbuf_tensor("p4_stats", [128, 16], F32))
            trash = lnst.enter_context(nc.sbuf_tensor("p4_trash", [128, H], BF16))
            for tc in range(8):
                xink = "p4_xin0"
                p.emit("sp", lambda e, tc=tc: e.dma_start(
                    out=xin[0][:].rearrange("p (j h) -> p j h", h=H),
                    in_=d["x3"][tc * 512:(tc + 1) * 512, :].rearrange("(j p) h -> p j h", p=128)),
                    writes=[xink], dma=True)
                emit_ln_chunk(cx, xink, xin[0][:], xbf, "p4_xbf", stats, "p4_stats", trash)
                for ht in range(16):
                    pk2 = f"p4_psbln{ht % 2}"
                    for j in range(4):
                        src2 = xbf[:, j * 2048 + ht * 128: j * 2048 + ht * 128 + 128]
                        p.emit("pe", lambda e, s=src2, j=j, ht=ht: e.transpose(
                            psbln[ht % 2][:, j * 128:(j + 1) * 128], s, cx.ident_sb[:]),
                            reads=["p4_xbf", "ident_sb"], writes=[pk2])
                    p.emit("dve", lambda e, ht=ht, tc=tc: e.tensor_copy(
                        xlnF[:, ht * 4096 + tc * 512: ht * 4096 + tc * 512 + 512],
                        psbln[ht % 2][:, 0:512]),
                        reads=[pk2], writes=["p4_xlnF"])
        p.barrier()
        pk = [sst.enter_context(nc.sbuf_tensor(f"p4_pk{i}", [128, 1024], I32)) for i in range(4)]
        sc = [sst.enter_context(nc.sbuf_tensor(f"p4_sc{i}", [128, 64], F32)) for i in range(4)]
        it = sst.enter_context(nc.sbuf_tensor("p4_it", [128, 1024], I32))
        tf = sst.enter_context(nc.sbuf_tensor("p4_tf", [128, 4 * 1024], BF16))
        wde = [sst.enter_context(nc.sbuf_tensor(f"p4_wde{i}", [128, 2048], BF16)) for i in range(4)]
        wT = [sst.enter_context(nc.sbuf_tensor(f"p4_wT{i}", [128, 16 * 128], BF16)) for i in range(4)]
        ge = [sst.enter_context(nc.sbuf_tensor(f"p4_ge{i}", [128, 512], F32)) for i in range(2)]
        ue = [sst.enter_context(nc.sbuf_tensor(f"p4_ue{i}", [128, 512], F32)) for i in range(2)]
        g2 = [sst.enter_context(nc.sbuf_tensor(f"p4_g2{i}", [128, 512], F32)) for i in range(2)]
        g3 = [sst.enter_context(nc.sbuf_tensor(f"p4_g3{i}", [128, 512], F32)) for i in range(2)]
        itb = [sst.enter_context(nc.sbuf_tensor(f"p4_itb{i}", [128, 512], BF16)) for i in range(2)]
        psf = [sst.enter_context(nc.psum_tensor(f"p4_psf{i}", [128, 512], F32)) for i in range(4)]
        psb = [sst.enter_context(nc.psum_tensor(f"p4_psb{i}", [128, 1024], BF16)) for i in range(2)]

        def dequant(eng, pki, sci, wdei, nt_key):
            emit_nvfp4_decode(cx, eng, pk[pki][:], f"p4_pk{pki}", sc[sci][:], f"p4_sc{sci}",
                              wde[wdei][:], f"p4_wde{wdei}", it[:], "p4_it", tf, "p4_tf")

        NTT = int(os.environ.get("P4NT", "128"))

        def prep_decode(nt):
            par = nt % 2
            for gi, (pnm, snm) in enumerate((("pkg", "sg"), ("pku", "su"))):
                bu = 2 * par + gi
                p.emit("sp", lambda e, nt=nt, pnm=pnm, bu=bu: e.dma_start(
                    out=pk[bu][:], in_=d[pnm][nt * 128:(nt + 1) * 128, :]),
                    writes=[f"p4_pk{bu}"], dma=True)
                p.emit("sp", lambda e, nt=nt, snm=snm, bu=bu: e.dma_start(
                    out=sc[bu][:], in_=d[snm][nt * 128:(nt + 1) * 128, :]),
                    writes=[f"p4_sc{bu}"], dma=True)
            for gi in range(2):
                bu = 2 * par + gi
                dequant("dve", bu, bu, bu, nt)

        def prep_transpose(nt):
            par = nt % 2
            for gi in range(2):
                bu = 2 * par + gi
                for kt in range(16):
                    pk2 = f"p4_psb{gi}"
                    p.emit("pe", lambda e, kt=kt, bu=bu, gi=gi: e.transpose(
                        psb[gi][:, (kt % 8) * 128:(kt % 8 + 1) * 128],
                        wde[bu][:, kt * 128:(kt + 1) * 128], cx.ident_sb[:]),
                        reads=[f"p4_wde{bu}", "ident_sb"], writes=[pk2])
                    p.emit("act", lambda e, kt=kt, bu=bu, gi=gi: e.copy(
                        wT[bu][:, kt * 128:(kt + 1) * 128], psb[gi][:, (kt % 8) * 128:(kt % 8 + 1) * 128]),
                        reads=[pk2], writes=[f"p4_wT{bu}"])

        prep_decode(0)
        prep_transpose(0)
        if NTT > 1:
            prep_decode(1)
        for nt in range(NTT):
            par = nt % 2
            for tg in range(8):
                for gi in range(2):
                    bu = 2 * par + gi
                    bank, bkey = psf[gi], f"p4_psf{gi}"
                    for kt in range(16):
                        p.emit("pe", lambda e, kt=kt, tg=tg, bu=bu, bank=bank: e.matmul(
                            bank[:], wT[bu][:, kt * 128:(kt + 1) * 128],
                            xlnF[:, kt * 4096 + tg * 512: kt * 4096 + tg * 512 + 512],
                            start=(kt == 0), stop=(kt == 15)),
                            reads=[f"p4_wT{bu}", "p4_xlnF"], writes=[bkey])
                ib = tg % 2
                gek, uek, g2k, g3k = f"p4_ge{ib}", f"p4_ue{ib}", f"p4_g2{ib}", f"p4_g3{ib}"
                gex, uex, g2x, g3x = ge[ib], ue[ib], g2[ib], g3[ib]
                p.emit("act", lambda e, a=gex: e.copy(a[:], psf[0][:]), reads=["p4_psf0"], writes=[gek])
                p.emit("act", lambda e, a=uex: e.copy(a[:], psf[1][:]), reads=["p4_psf1"], writes=[uek])
                p.emit("act", lambda e, a=g2x, b=gex: e.activation(a[:], b[:], AF.Square),
                       reads=[gek], writes=[g2k])
                p.emit("act", lambda e, a=g2x: e.activation(
                    a[:], a[:], AF.Copy, scale=0.7978845608028654 * 0.044715,
                    bias=0.7978845608028654),
                    reads=[g2k, "consts"], writes=[g2k])
                p.emit("dve", lambda e, a=g3x, b=gex, c=g2x: e.tensor_mul(a[:], b[:], c[:]),
                       reads=[gek, g2k], writes=[g3k])
                p.emit("act", lambda e, a=g3x: e.activation(a[:], a[:], AF.Tanh),
                       reads=[g3k], writes=[g3k])
                p.emit("act", lambda e, a=g3x: e.activation(a[:], a[:], AF.Copy, scale=0.5, bias=0.5),
                       reads=[g3k, "consts"], writes=[g3k])
                p.emit("dve", lambda e, a=g2x, b=gex, c=g3x: e.tensor_mul(a[:], b[:], c[:]),
                       reads=[gek, g3k], writes=[g2k])
                p.emit("dve", lambda e, ib=ib, a=g2x, b=uex: e.tensor_mul(itb[ib][:], a[:], b[:]),
                       reads=[g2k, uek], writes=[f"p4_itb{ib}"])
                p.emit("act", lambda e, nt=nt, tg=tg, ib=ib: e.dma_start(
                    out=d["interT"][nt * 128:(nt + 1) * 128, tg * 512:(tg + 1) * 512], in_=itb[ib][:]),
                    reads=[f"p4_itb{ib}"], writes=[f"interT_{nt}_{tg}"], dma=True)
            if nt + 1 < NTT:
                prep_transpose(nt + 1)
            if nt + 2 < NTT:
                prep_decode(nt + 2)


def phase4_wd(cx, d):
    nc, p = cx.nc, cx.p
    with contextlib.ExitStack() as sst:
        pk = [sst.enter_context(nc.sbuf_tensor(f"p4c_pk{i}", [128, 1024], I32)) for i in range(2)]
        sc = [sst.enter_context(nc.sbuf_tensor(f"p4c_sc{i}", [128, 64], F32)) for i in range(2)]
        it = sst.enter_context(nc.sbuf_tensor("p4c_it", [128, 1024], I32))
        tf = sst.enter_context(nc.sbuf_tensor("p4c_tf", [128, 4 * 1024], BF16))
        it2 = sst.enter_context(nc.sbuf_tensor("p4c_it2", [128, 1024], I32))
        tf2 = sst.enter_context(nc.sbuf_tensor("p4c_tf2", [128, 4 * 1024], BF16))
        wde = [sst.enter_context(nc.sbuf_tensor(f"p4c_wde{i}", [128, 2048], BF16)) for i in range(2)]
        wT = [sst.enter_context(nc.sbuf_tensor(f"p4c_wT{i}", [128, 16 * 128], BF16)) for i in range(2)]
        psb = [sst.enter_context(nc.psum_tensor(f"p4c_psb{i}", [128, 1024], BF16)) for i in range(2)]

        for hb in range(16):
            for wc in range(8):
                bi = wc % 2
                p.emit("sp", lambda e, hb=hb, wc=wc, bi=bi: e.dma_start(
                    out=pk[bi][:], in_=d["pkd"][hb * 128:(hb + 1) * 128, wc * 1024:(wc + 1) * 1024]),
                    writes=[f"p4c_pk{bi}"], dma=True)
                p.emit("sp", lambda e, hb=hb, wc=wc, bi=bi: e.dma_start(
                    out=sc[bi][:], in_=d["sd"][hb * 128:(hb + 1) * 128, wc * 64:(wc + 1) * 64]),
                    writes=[f"p4c_sc{bi}"], dma=True)
                itx, itk = (it, "p4c_it") if bi == 0 else (it2, "p4c_it2")
                tfx, tfk = (tf, "p4c_tf") if bi == 0 else (tf2, "p4c_tf2")
                emit_nvfp4_decode(cx, "dve", pk[bi][:], f"p4c_pk{bi}",
                                  sc[bi][:], f"p4c_sc{bi}", wde[bi][:], f"p4c_wde{bi}",
                                  itx[:], itk, tfx, tfk)
                for kt in range(16):
                    pk2 = f"p4c_psb{bi}"
                    p.emit("pe", lambda e, kt=kt, bi=bi: e.transpose(
                        psb[bi][:, (kt % 8) * 128:(kt % 8 + 1) * 128],
                        wde[bi][:, kt * 128:(kt + 1) * 128], cx.ident_sb[:]),
                        reads=[f"p4c_wde{bi}", "ident_sb"], writes=[pk2])
                    p.emit("act", lambda e, kt=kt, bi=bi: e.copy(
                        wT[bi][:, kt * 128:(kt + 1) * 128], psb[bi][:, (kt % 8) * 128:(kt % 8 + 1) * 128]),
                        reads=[pk2], writes=[f"p4c_wT{bi}"])
                p.emit("act", lambda e, hb=hb, wc=wc, bi=bi: e.dma_start(
                    out=d["wdT"][wc * 2048:(wc + 1) * 2048, hb * 128:(hb + 1) * 128].rearrange(
                        "(i p) c -> p i c", p=128),
                    in_=wT[bi][:].rearrange("p (i c) -> p i c", c=128)),
                    reads=[f"p4c_wT{bi}"], writes=[f"wdT_{hb}_{wc}"], dma=True)


def phase4_down(cx, d):
    nc, p = cx.nc, cx.p
    with contextlib.ExitStack() as sst:
        itF = sst.enter_context(nc.sbuf_tensor("p4d_itF", [128, 128 * 512], BF16))
        wdt = [sst.enter_context(nc.sbuf_tensor(f"p4d_wdt{i}", [128, 512], BF16)) for i in range(6)]
        xo = sst.enter_context(nc.sbuf_tensor("p4d_xo", [128, 4 * 2048], F32))
        x4 = sst.enter_context(nc.sbuf_tensor("p4d_x4", [128, 4 * 2048], F32))
        psf = [sst.enter_context(nc.psum_tensor(f"p4d_ps{i}", [128, 512], F32)) for i in range(4)]

        for tb in range(8):
            for sb8 in range(8):
                p.emit("sp", lambda e, tb=tb, sb8=sb8: e.dma_start(
                    out=itF[:, sb8 * 16 * 512:(sb8 + 1) * 16 * 512].rearrange("p (i t) -> p i t", t=512),
                    in_=d["interT"][sb8 * 2048:(sb8 + 1) * 2048, tb * 512:(tb + 1) * 512].rearrange(
                        "(i p) t -> p i t", p=128)),
                    writes=["p4d_itF"], dma=True)
            p.emit("sp", lambda e, tb=tb: e.dma_start(
                out=xo[:].rearrange("p (j h) -> p j h", h=H),
                in_=d["x3"][tb * 512:(tb + 1) * 512, :].rearrange("(j p) h -> p j h", p=128)),
                writes=["p4d_xo"], dma=True)
            for hc in range(4):
                for nt in range(128):
                    wb = nt % 6
                    wk = f"p4d_wdt{wb}"
                    q = "pool" if nt % 2 else "sp"
                    p.emit(q, lambda e, nt=nt, hc=hc, wb=wb: e.dma_start(
                        out=wdt[wb][:], in_=d["wdT"][nt * 128:(nt + 1) * 128, hc * 512:(hc + 1) * 512]),
                        writes=[wk], dma=True)
                    for ts in range(4):
                        bank, bkey = psf[ts], f"p4d_ps{ts}"
                        p.emit("pe", lambda e, nt=nt, ts=ts, wb=wb, bank=bank: e.matmul(
                            bank[:], itF[:, nt * 512 + ts * 128: nt * 512 + ts * 128 + 128],
                            wdt[wb][:], start=(nt == 0), stop=(nt == 127)),
                            reads=[wk, "p4d_itF"], writes=[bkey])
                for ts in range(4):
                    p.emit("dve", lambda e, ts=ts, hc=hc: e.tensor_add(
                        x4[:, ts * 2048 + hc * 512: ts * 2048 + hc * 512 + 512],
                        psf[ts][:], xo[:, ts * 2048 + hc * 512: ts * 2048 + hc * 512 + 512]),
                        reads=[f"p4d_ps{ts}", "p4d_xo"], writes=["p4d_x4"])
            p.emit("act", lambda e, tb=tb: e.dma_start(
                out=d["x4"][tb * 512:(tb + 1) * 512, :].rearrange("(j p) h -> p j h", p=128),
                in_=x4[:].rearrange("p (j h) -> p j h", h=H)),
                reads=["p4d_x4"], writes=[f"x4_{tb}"], dma=True)



def _prep_inputs(inp):
    unused = (inp["L0_attn_key"], inp["L0_attn_value"])
    del unused
    f = np.ascontiguousarray
    x = np.asarray(inp["L0_attn_query"], dtype=np.float32)
    Wqkv = np.asarray(inp["L0_attn_Wqkv"], dtype=np.float32)
    per_core = []
    shared = {
        "wproj0T": f(np.asarray(inp["L0_attn_Wproj"], np.float32).T),
        "bproj0": f(np.asarray(inp["L0_attn_bproj"], np.float32)[None, :]),
        "w0T": f(np.asarray(inp["L0_ffn_weight"], np.float32).T),
        "gatewT": f(np.asarray(inp["L0_ffn_gate_w"], np.float32).T),
        "A2": f(np.asarray(inp["L0_ffn_A_w"], np.float32).reshape(128, 2048).T),
        "B2": f(np.asarray(inp["L0_ffn_B_w"], np.float32).transpose(0, 2, 1).reshape(128, 2048)),
        "wqkv1T": f(np.asarray(inp["L1_attn_w_qkv"], np.float32).T),
        "wo1T": f(np.asarray(inp["L1_attn_w_o"], np.float32).T),
        "pkg": f(np.asarray(inp["L1_ffn_gate_W_packed"], np.int32)),
        "sg": f(np.asarray(inp["L1_ffn_gate_scales"], np.float32)),
        "pku": f(np.asarray(inp["L1_ffn_up_W_packed"], np.int32)),
        "su": f(np.asarray(inp["L1_ffn_up_scales"], np.float32)),
        "pkd": f(np.asarray(inp["L1_ffn_down_W_packed"], np.int32)),
        "sd": f(np.asarray(inp["L1_ffn_down_scales"], np.float32)),
    }
    xfull = f(x.reshape(B * S, H))
    for c in range(8):
        sub = np.concatenate([Wqkv[384 * c:384 * c + 384], Wqkv[384 * (c + 8):384 * (c + 8) + 384]], axis=0)
        m = dict(shared)
        m["x_full"] = xfull
        m["x_own"] = f(x[c])
        m["wqkv0T"] = f(sub.T)
        per_core.append(m)
    return per_core



IN_SPECS = {
    "x_full": ([B * S, H], F32), "x_own": ([S, H], F32),
    "wqkv0T": ([H, 768], F32), "wproj0T": ([H, H], F32), "bproj0": ([1, H], F32),
    "w0T": ([H, H], F32), "gatewT": ([H, 8], F32), "A2": ([H, 128], F32), "B2": ([128, H], F32),
    "wqkv1T": ([H, 3 * H], F32), "wo1T": ([H, H], F32),
    "pkg": ([MLP, H // 2], I32), "sg": ([MLP, 64], F32),
    "pku": ([MLP, H // 2], I32), "su": ([MLP, 64], F32),
    "pkd": ([H, MLP // 2], I32), "sd": ([H, 512], F32),
}
PHASE_IO = {
    "p1a": (["x_full", "wqkv0T"], [], ["ctxT0"]),
    "p1b": (["x_own", "wproj0T", "bproj0"], ["ctxT0"], ["x1"]),
    "p2": (["w0T", "gatewT", "A2", "B2"], ["x1"], ["x2"]),
    "p3a": (["wqkv1T"], ["x2"], ["qkvT1"]),
    "p3c": ([], ["qkvT1"], ["ctxT1"]),
    "p3d": (["wo1T"], ["ctxT1", "x2"], ["x3"]),
    "p4a": (["pkg", "sg", "pku", "su"], ["x3"], ["interT"]),
    "p4c": (["pkd", "sd"], [], ["wdT"]),
    "p4d": ([], ["interT", "wdT", "x3"], ["x4"]),
}
PHASE_FN = {}
SCRATCH_SPECS = {
    "ctxT0": ([H, S], BF16), "x1": ([S, H], F32), "x2": ([S, H], F32),
    "x3": ([S, H], F32), "x4": ([S, H], F32),
    "ctxT1": ([H, S], BF16), "xlnT3": ([H, S], BF16), "qkvT1": ([3 * H, S], BF16),
    "interT": ([MLP, S], BF16), "wdT": ([MLP, H], BF16),
}


def build_program(phase_names, out_name):
    nc = bass.Bass("TRN2", target_bir_lowering=False)
    d = {}
    needed_in, needed_scr = set(), set()
    for pn in phase_names:
        ins, rs, ws = PHASE_IO[pn]
        needed_in |= set(ins)
        needed_scr |= set(rs) | set(ws)
    produced = set()
    for pn in phase_names:
        produced |= set(PHASE_IO[pn][2])
    for n in sorted(needed_in):
        shp, dt = IN_SPECS[n]
        d[n] = nc.dram_tensor(n, shp, dt, kind="ExternalInput").ap()
    for n in sorted(needed_scr):
        shp, dt = SCRATCH_SPECS.get(n, ([S, H], F32))
        if n == out_name:
            kind = "ExternalOutput"
        elif n in produced:
            kind = "Internal"
        else:
            kind = "ExternalInput"
            needed_in.add(n)
        d[n] = nc.dram_tensor(n, shp, dt, kind=kind).ap()

    st = contextlib.ExitStack()
    with st:
        prog = Prog(nc)
        cx = Ctx(nc, prog, st)
        prog.barrier()
        for i, pn in enumerate(phase_names):
            PHASE_FN[pn](cx, d)
            if i < len(phase_names) - 1:
                prog.barrier()
        prog.check()
        prog.realize(st)
    return nc, sorted(needed_in)


def run_phases(per_core, phase_names, out_name):
    nc, in_names = build_program(phase_names, out_name)
    in_maps = [{k: m[k] for k in in_names} for m in per_core]
    res = run_bass_kernel_spmd(nc, in_maps, core_ids=list(range(8)))
    return [res.results[c][out_name] for c in range(8)]


ALL_PHASES = ["p1a", "p1b", "p2", "p3a", "p3c", "p3d", "p4a", "p4c", "p4d"]


def kernel(**inp):
    per_core = _prep_inputs(inp)
    out = run_phases(per_core, ALL_PHASES, "x4")
    return np.stack(out, axis=0)


PHASE_FN.update({"p1a": phase1_qkv_attn, "p1b": phase1_proj, "p2": phase2_ffn0,
                 "p3a": phase3_qkv, "p3c": phase3_attn, "p3d": phase3_proj,
                 "p4a": phase4_gateup, "p4c": phase4_wd, "p4d": phase4_down})
